# version 01
# baseline (speedup 1.0000x reference)
import numpy as np
from contextlib import ExitStack, contextmanager
import concourse.bass as bass
import concourse.mybir as mybir
from concourse.bass_utils import run_bass_kernel_spmd

F32 = mybir.dt.float32
BF16 = mybir.dt.bfloat16
AF = mybir.ActivationFunctionType
ALU = mybir.AluOpType
AX = mybir.AxisListType

NCORES = 8
S = 2048
TTOT = 2 * S
NT = TTOT // 128
D = 1024
KC = D // 128



class Buf:
    __slots__ = ("h", "writer", "readers", "name")

    def __init__(self, h, name=""):
        self.h = h
        self.name = name or getattr(h, "name", "")
        self.writer = None
        self.readers = []

    def __getitem__(self, key):
        return self.h[key]


class Stream:
    def __init__(self, name, eng, sem):
        self.name = name
        self.eng = eng
        self.sem = sem
        self.seq = 0
        self.insts = []
        self.incs = []
        self.tick = 0
        self.waited = {}

    def ensure_inc(self, seq):
        for s, t in self.incs:
            if s >= seq:
                return t
        inst = self.insts[seq]
        if inst is None:
            raise RuntimeError(f"stream {self.name} op {seq} has no instruction")
        self.tick += 1
        inst.then_inc(self.sem, 1)
        self.incs.append((seq, self.tick))
        return self.tick


class Ctx:
    def __init__(self, nc, stack):
        self.nc = nc
        self.stack = stack
        self.streams = {}
        self.uid = 0
        for name, eng in [("pe", nc.tensor), ("act", nc.scalar), ("dve", nc.vector),
                          ("pool", nc.gpsimd), ("sp", nc.sync)]:
            sem = stack.enter_context(nc.semaphore("sem_" + name))
            self.streams[name] = Stream(name, eng, sem)

    def _nm(self, name):
        self.uid += 1
        return f"{name}_{self.uid}"

    def sbuf(self, name, shape, dtype):
        return Buf(self.stack.enter_context(self.nc.sbuf_tensor(self._nm(name), shape, dtype)), name)

    def psum(self, name, shape, dtype):
        return Buf(self.stack.enter_context(self.nc.psum_tensor(self._nm(name), shape, dtype)), name)

    @contextmanager
    def scope(self):
        outer = self.stack
        with ExitStack() as inner:
            self.stack = inner
            try:
                yield
            finally:
                self.stack = outer
                self.barrier()

    def barrier(self):
        ticks = []
        for s in self.streams.values():
            if s.seq > 0:
                ticks.append((s, s.ensure_inc(s.seq - 1)))
        for s2 in self.streams.values():
            for s, t in ticks:
                if s is not s2 and s2.waited.get(s.name, -1) < t:
                    s2.eng.wait_ge(s.sem, t)
                    s2.waited[s.name] = t

    def _wait_for(self, st, producer_stream, producer_seq, late=False):
        if producer_stream is st:
            gap = 2 if late else 1
            if producer_seq >= st.seq - gap:
                t = producer_stream.ensure_inc(producer_seq)
                st.eng.wait_ge(st.sem, t)
            return
        t = producer_stream.ensure_inc(producer_seq)
        if st.waited.get(producer_stream.name, -1) < t:
            st.eng.wait_ge(producer_stream.sem, t)
            st.waited[producer_stream.name] = t

    def _pre(self, st, r, w):
        for b in r:
            if b.writer is not None:
                self._wait_for(st, *b.writer)
        for b in w:
            if b.writer is not None:
                self._wait_for(st, *b.writer)
            for (s2, q2) in b.readers:
                self._wait_for(st, s2, q2) if s2 is not st else None

    def _post(self, st, seq, r, w, late):
        st.seq += 1
        for b in r:
            b.readers.append((st, seq))
        for b in w:
            b.writer = (st, seq, late)
            b.readers = []

    def op(self, stream, emit, r=(), w=(), late=False):
        st = self.streams[stream]
        self._pre(st, r, w)
        inst = emit(st.eng)
        if isinstance(inst, (list, tuple)):
            inst = inst[-1]
        seq = st.seq
        st.insts.append(inst)
        self._post(st, seq, r, w, late)
        return seq

    def dma(self, qname, out_ap, in_ap, r=(), w=(), **kw):
        st = self.streams[qname]
        self._pre(st, r, w)
        inst = st.eng.dma_start(out=out_ap, in_=in_ap, **kw)
        seq = st.seq
        st.insts.append(inst)
        st.tick += 16
        inst.then_inc(st.sem, 16)
        st.incs.append((seq, st.tick))
        self._post(st, seq, r, w, False)
        return seq

    def finish(self):
        self.barrier()



def emit_constants(ctx, c):
    c["ident"] = ctx.sbuf("ident", [128, 128], BF16)
    c["ones"] = ctx.sbuf("ones_row", [1, 128], BF16)
    c["epsc"] = ctx.sbuf("epsc", [128, 4], F32)
    with ctx.scope():
        identf = ctx.sbuf("identf", [128, 128], F32)
        ctx.op("pool", lambda e: e.iota(identf[:, :], [[1, 128]], channel_multiplier=-1,
                                        allow_small_or_imprecise_dtypes=True), w=[identf])
        ctx.op("dve", lambda e: e.tensor_scalar(c["ident"][:, :], identf[:, :], 0.0, None,
                                                op0=ALU.is_equal), r=[identf], w=[c["ident"]])
    ctx.op("dve", lambda e: e.memset(c["ones"][:, :], 1.0), w=[c["ones"]])
    for i, v in enumerate([1e-5, 1e-6, 1e-8, 0.0]):
        ctx.op("dve", lambda e, i=i, v=v: e.memset(c["epsc"][:, i:i + 1], v), w=[c["epsc"]])


def emit_tri_mask(ctx, c, dst):
    with ctx.scope():
        tif = ctx.sbuf("trif", [128, 128], F32)
        ctx.op("pool", lambda e: e.iota(tif[:, :], [[1, 128]], channel_multiplier=-1,
                                        allow_small_or_imprecise_dtypes=True), w=[tif])
        ctx.op("dve", lambda e: e.tensor_scalar(dst[:, :], tif[:, :], 0.0, None,
                                                op0=ALU.is_ge), r=[tif], w=[dst])


def emit_band_mask(ctx, c, dst):
    with ctx.scope():
        tif = ctx.sbuf("bandf", [128, 128], F32)
        tif2 = ctx.sbuf("bandf2", [128, 128], F32)
        ctx.op("pool", lambda e: e.iota(tif[:, :], [[1, 128]], channel_multiplier=-1,
                                        allow_small_or_imprecise_dtypes=True), w=[tif])
        ctx.op("dve", lambda e: e.tensor_scalar(tif2[:, :], tif[:, :], 0.0, None,
                                                op0=ALU.is_ge), r=[tif], w=[tif2])
        ctx.op("dve", lambda e: e.tensor_scalar(tif[:, :], tif[:, :], 6.0, None,
                                                op0=ALU.is_le), r=[tif], w=[tif])
        ctx.op("dve", lambda e: e.tensor_mul(dst[:, :], tif[:, :], tif2[:, :]),
               r=[tif, tif2], w=[dst])


def emit_bias_cols(ctx, c, rows_bf, cols_f32, n, ptb):
    def tr(e):
        last = None
        for cc in range(n):
            last = e.transpose(ptb[:, cc:cc + 1], rows_bf[0:1, cc * 128:(cc + 1) * 128],
                               c["ident"][:, :])
        return last
    ctx.op("pe", tr, r=[rows_bf, c["ident"]], w=[ptb])
    ctx.op("act", lambda e: e.copy(cols_f32[:, 0:n], ptb[:, 0:n]), r=[ptb], w=[cols_f32])


def emit_load_x(ctx, x_hbm, xb):
    src = x_hbm.rearrange("(t p) f -> p t f", p=128)
    for i in range(4):
        t0 = i * (NT // 4)
        ctx.dma("sp", xb[:, t0 * D:(t0 + NT // 4) * D], src[:, t0:t0 + NT // 4, :], w=[xb])


def emit_store_x(ctx, xb, xo_hbm):
    dst = xo_hbm.rearrange("(t p) f -> p t f", p=128)
    for i in range(4):
        t0 = i * (NT // 4)
        ctx.dma("sp", dst[:, t0:t0 + NT // 4, :], xb[:, t0 * D:(t0 + NT // 4) * D], r=[xb])


class LNScratch:
    def __init__(self, ctx, tag):
        self.sums = ctx.sbuf(f"lns_{tag}", [128, 1], F32)
        self.mu = ctx.sbuf(f"lnm_{tag}", [128, 1], F32)
        self.ssq = ctx.sbuf(f"lnq_{tag}", [128, 1], F32)
        self.sd = ctx.sbuf(f"lnd_{tag}", [128, 1], F32)
        self.rstd = ctx.sbuf(f"lnr_{tag}", [128, 1], F32)
        self.tctr = ctx.sbuf(f"lnt_{tag}", [128, D], F32)
        self.z = ctx.sbuf(f"lnz_{tag}", [128, D], BF16)


def emit_ln_tile(ctx, c, ln, src_buf, src_col0, zT, zT_tcol, GT, ptb, junk_ps,
                 gamma_b=None, beta_b=None, z_out=None, zt_inplace=None):
    xa = src_buf[:, src_col0:src_col0 + D]
    ctx.op("dve", lambda e: e.tensor_reduce(ln.sums[:, :], xa, axis=AX.X, op=ALU.add),
           r=[src_buf], w=[ln.sums], late=True)
    ctx.op("dve", lambda e: e.tensor_scalar(ln.mu[:, :], ln.sums[:, :], 1.0 / D, None, op0=ALU.mult),
           r=[ln.sums], w=[ln.mu])
    ctx.op("dve", lambda e: e.tensor_scalar(ln.tctr[:, :], xa, ln.mu[:, :], None, op0=ALU.subtract),
           r=[src_buf, ln.mu], w=[ln.tctr])
    ctx.op("act", lambda e: e.activation(junk_ps[:, 0:D], ln.tctr[:, :], AF.Square,
                                         accum_out=ln.ssq[:, :]),
           r=[ln.tctr], w=[junk_ps, ln.ssq], late=True)
    ctx.op("act", lambda e: e.activation(ln.sd[:, :], ln.ssq[:, :], AF.Sqrt,
                                         bias=c["epsc"][:, 0:1], scale=1.0 / D),
           r=[ln.ssq, c["epsc"]], w=[ln.sd])
    ctx.op("dve", lambda e: e.reciprocal(ln.rstd[:, :], ln.sd[:, :]), r=[ln.sd], w=[ln.rstd],
           late=True)
    if gamma_b is None:
        ctx.op("dve", lambda e: e.tensor_scalar(ln.z[:, :], ln.tctr[:, :], ln.rstd[:, :], None,
                                                op0=ALU.mult), r=[ln.tctr, ln.rstd], w=[ln.z])
    else:
        ctx.op("dve", lambda e: e.tensor_scalar(ln.tctr[:, :], ln.tctr[:, :], ln.rstd[:, :], None,
                                                op0=ALU.mult), r=[ln.tctr, ln.rstd], w=[ln.tctr])
        ctx.op("dve", lambda e: e.tensor_mul(ln.tctr[:, :], ln.tctr[:, :], gamma_b[:, :]),
               r=[ln.tctr, gamma_b], w=[ln.tctr])
        ctx.op("dve", lambda e: e.tensor_add(ln.z[:, :], ln.tctr[:, :], beta_b[:, :]),
               r=[ln.tctr, beta_b], w=[ln.z])
    if z_out is not None:
        buf, col = z_out
        ctx.op("dve", lambda e: e.tensor_copy(buf[:, col:col + D], ln.z[:, :]),
               r=[ln.z], w=[buf])

    def tr(e):
        last = None
        for cc in range(KC):
            last = e.transpose(ptb[:, cc * 128:(cc + 1) * 128],
                               ln.z[:, cc * 128:(cc + 1) * 128], c["ident"][:, :])
        return last
    ctx.op("pe", tr, r=[ln.z, c["ident"]], w=[ptb])

    if zt_inplace is not None:
        buf, col = zt_inplace
        ctx.op("act", lambda e: e.copy(buf[:, col:col + D], ptb[:, 0:D]), r=[ptb], w=[buf])
    else:
        def ev(e):
            out_view = zT[:, :].rearrange("p (c g) -> p c g", g=GT)[:, :, zT_tcol:zT_tcol + 128]
            return e.copy(out_view, ptb[:, :].rearrange("p (c w) -> p c w", w=128))
        ctx.op("act", ev, r=[ptb], w=[zT])


def load_w_kc(ctx, dst, hbm_ap, k0, krows, cols0, ncols, dst_col0=0):
    src = hbm_ap[k0:k0 + krows, cols0:cols0 + ncols].rearrange("(kc p) m -> p kc m", p=128)
    ctx.dma("pool", dst[:, dst_col0:dst_col0 + (krows // 128) * ncols], src, w=[dst])



def emit_l0_ffn(ctx, c, W, xb):
    G = 4
    GT = TTOT // G
    NTG = GT // 128
    with ctx.scope():
        zT = ctx.sbuf("f0_zT", [128, KC * GT], BF16)
        coef = ctx.sbuf("f0_coef", [128, NTG * 8], F32)
        sgt = ctx.sbuf("f0_sg", [128, NTG], F32)
        w1b = [ctx.sbuf("f0_w1a", [128, KC * 1024], BF16),
               ctx.sbuf("f0_w1b", [128, KC * 1024], BF16)]
        w2b = ctx.sbuf("f0_w2", [128, KC * 1024], BF16)
        hT = ctx.sbuf("f0_hT", [128, KC * 512], BF16)
        rwb = ctx.sbuf("f0_rw", [128, KC * 9], BF16)
        rs = ctx.sbuf("f0_rs", [128, 8], F32)
        probs = ctx.sbuf("f0_probs", [128, 8], F32)
        pr2 = ctx.sbuf("f0_pr2", [128, 8], F32)
        msk = ctx.sbuf("f0_msk", [128, 8], F32)
        p_small = ctx.psum("f0_psmall", [128, 16], F32)
        p_h = [ctx.psum("f0_ph0", [128, 512], F32), ctx.psum("f0_ph1", [128, 512], F32)]
        p_y = [ctx.psum("f0_py0", [128, 1024], F32), ctx.psum("f0_py1", [128, 1024], F32)]
        ptb = ctx.psum("f0_ptb", [128, 1024], BF16)
        ln = LNScratch(ctx, "f0")

        ctx.op("dve", lambda e: e.memset(rs[:, 4:5], 0.0), w=[rs])
        load_w_kc(ctx, rwb, W["L0_ffn_rwsg"], 0, D, 0, 9)

        for g in range(G):
            for j in range(NTG):
                jg = g * NTG + j
                emit_ln_tile(ctx, c, ln, xb, jg * D, zT, j * 128, GT, ptb, p_y[0])
            for j in range(NTG):
                def logits_mm(e, j=j):
                    last = None
                    for k in range(KC):
                        last = e.matmul(p_small[:, 0:9],
                                        zT[:, k * GT + j * 128:k * GT + j * 128 + 128],
                                        rwb[:, k * 9:(k + 1) * 9],
                                        start=(k == 0), stop=(k == KC - 1))
                    return last
                ctx.op("pe", logits_mm, r=[zT, rwb], w=[p_small])
                ctx.op("act", lambda e, j=j: e.activation(sgt[:, j:j + 1], p_small[:, 8:9],
                                                          AF.Sigmoid), r=[p_small], w=[sgt])
                ctx.op("dve", lambda e: e.tensor_reduce(rs[:, 0:1], p_small[:, 0:8], axis=AX.X,
                                                        op=ALU.max, negate=True),
                       r=[p_small], w=[rs], late=True)
                ctx.op("act", lambda e: e.activation(probs[:, :], p_small[:, 0:8], AF.Exp,
                                                     bias=rs[:, 0:1], accum_out=rs[:, 1:2]),
                       r=[p_small, rs], w=[probs, rs], late=True)
                ctx.op("dve", lambda e: e.reciprocal(rs[:, 2:3], rs[:, 1:2]), r=[rs], w=[rs],
                       late=True)
                ctx.op("dve", lambda e: e.tensor_scalar(probs[:, :], probs[:, :], rs[:, 2:3], None,
                                                        op0=ALU.mult), r=[probs, rs], w=[probs])
                ctx.op("dve", lambda e: e.tensor_reduce(rs[:, 3:4], probs[:, :], axis=AX.X,
                                                        op=ALU.max, negate=True),
                       r=[probs], w=[rs], late=True)
                ctx.op("dve", lambda e: e.tensor_scalar(msk[:, :], probs[:, :], rs[:, 3:4],
                                                        rs[:, 4:5], op0=ALU.add, op1=ALU.is_ge),
                       r=[probs, rs], w=[msk])
                ctx.op("dve", lambda e: e.scalar_tensor_tensor(pr2[:, :], msk[:, :], -1e9,
                                                               probs[:, :], op0=ALU.mult,
                                                               op1=ALU.add),
                       r=[msk, probs], w=[pr2])
                ctx.op("dve", lambda e: e.tensor_reduce(rs[:, 5:6], pr2[:, :], axis=AX.X,
                                                        op=ALU.max, negate=True),
                       r=[pr2], w=[rs], late=True)
                ctx.op("dve", lambda e: e.tensor_scalar(msk[:, :], probs[:, :], rs[:, 5:6],
                                                        rs[:, 4:5], op0=ALU.add, op1=ALU.is_ge),
                       r=[probs, rs], w=[msk])
                ctx.op("dve", lambda e, j=j: e.tensor_mul(coef[:, j * 8:(j + 1) * 8],
                                                          probs[:, :], msk[:, :]),
                       r=[probs, msk], w=[coef])
            for e_i in range(9):
                is_sh = (e_i == 8)
                if not is_sh:
                    wa = w1b[e_i % 2]
                    if e_i == 0 and g == 0:
                        load_w_kc(ctx, wa, W["L0_ffn_w1"][0], 0, D, 0, 1024)
                    elif e_i == 0:
                        load_w_kc(ctx, wa, W["L0_ffn_w1"][0], 0, D, 0, 1024)
                    load_w_kc(ctx, w2b, W["L0_ffn_w2"][e_i], 0, D, 0, 1024)
                else:
                    load_w_kc(ctx, w1b[0], W["L0_ffn_shared_gate_up"], 0, D, 0, 1024)
                    load_w_kc(ctx, w1b[1], W["L0_ffn_shared_gate_up"], 0, D, 1024, 1024)
                    load_w_kc(ctx, w2b, W["L0_ffn_shared_down"], 0, D, 0, 1024)
                for t in range(GT // 512):
                    for m in range(KC):
                        ph = p_h[m % 2]
                        if not is_sh:
                            def hmm(e, m=m, t=t, wa=wa, ph=ph):
                                last = None
                                for k in range(KC):
                                    last = e.matmul(
                                        ph[:, :],
                                        wa[:, k * 1024 + m * 128:k * 1024 + (m + 1) * 128],
                                        zT[:, k * GT + t * 512:k * GT + (t + 1) * 512],
                                        start=(k == 0), stop=(k == KC - 1))
                                return last
                            ctx.op("pe", hmm, r=[wa, zT], w=[ph])
                            ctx.op("act", lambda e, m=m, ph=ph: e.activation(
                                hT[:, m * 512:(m + 1) * 512], ph[:, :], AF.Silu),
                                r=[ph], w=[hT])
                        else:
                            ph2 = p_h[(m + 1) % 2]
                            def gumm(e, m=m, t=t, ph=ph, ph2=ph2):
                                last = None
                                for k in range(KC):
                                    last = e.matmul(
                                        ph[:, :],
                                        w1b[0][:, k * 1024 + m * 128:k * 1024 + (m + 1) * 128],
                                        zT[:, k * GT + t * 512:k * GT + (t + 1) * 512],
                                        start=(k == 0), stop=(k == KC - 1))
                                for k in range(KC):
                                    last = e.matmul(
                                        ph2[:, :],
                                        w1b[1][:, k * 1024 + m * 128:k * 1024 + (m + 1) * 128],
                                        zT[:, k * GT + t * 512:k * GT + (t + 1) * 512],
                                        start=(k == 0), stop=(k == KC - 1))
                                return last
                            ctx.op("pe", gumm, r=[w1b[0], w1b[1], zT], w=[ph, ph2])
                            ctx.op("act", lambda e, ph=ph: e.activation(
                                ln.z[:, 0:512], ph[:, :], AF.Silu), r=[ph], w=[ln.z])
                            ctx.op("dve", lambda e, m=m, ph2=ph2: e.tensor_mul(
                                hT[:, m * 512:(m + 1) * 512], ln.z[:, 0:512], ph2[:, :]),
                                r=[ln.z, ph2], w=[hT])
                    for sub in range(4):
                        py = p_y[sub % 2]
                        sc = sub * 128
                        def ymm(e, sc=sc, py=py):
                            last = None
                            for k in range(KC):
                                for half in range(2):
                                    last = e.matmul(
                                        py[:, half * 512:(half + 1) * 512],
                                        hT[:, k * 512 + sc:k * 512 + sc + 128],
                                        w2b[:, k * 1024 + half * 512:k * 1024 + (half + 1) * 512],
                                        start=(k == 0), stop=(k == KC - 1))
                            return last
                        ctx.op("pe", ymm, r=[hT, w2b], w=[py])
                        j_in_g = t * 4 + sub
                        jg = g * NTG + j_in_g
                        cc_buf = sgt if is_sh else coef
                        cc = (sgt[:, j_in_g:j_in_g + 1] if is_sh
                              else coef[:, j_in_g * 8 + e_i:j_in_g * 8 + e_i + 1])
                        ctx.op("dve", lambda e, py=py, jg=jg, cc=cc: e.scalar_tensor_tensor(
                            xb[:, jg * D:(jg + 1) * D], py[:, :], cc,
                            xb[:, jg * D:(jg + 1) * D], op0=ALU.mult, op1=ALU.add),
                            r=[py, cc_buf, xb], w=[xb])
                if not is_sh and e_i + 1 < 8:
                    load_w_kc(ctx, w1b[(e_i + 1) % 2], W["L0_ffn_w1"][e_i + 1], 0, D, 0, 1024)



def emit_l1_ffn(ctx, c, W, xb):
    G = 4
    GT = TTOT // G
    NTG = GT // 128
    NQ = 4
    with ctx.scope():
        zT = ctx.sbuf("f1_zT", [128, KC * GT], BF16)
        coef = ctx.sbuf("f1_coef", [128, NTG * 8], F32)
        w1b = ctx.sbuf("f1_w1", [128, KC * 1024], BF16)
        w2b = ctx.sbuf("f1_w2", [128, KC * 1024], BF16)
        hT = ctx.sbuf("f1_hT", [128, KC * 512], BF16)
        rwb = ctx.sbuf("f1_rw", [128, KC * 8], BF16)
        rs = ctx.sbuf("f1_rs", [128, 8], F32)
        msk = ctx.sbuf("f1_msk", [128, 8], F32)
        p_small = ctx.psum("f1_psmall", [128, 16], F32)
        p_h = [ctx.psum("f1_ph0", [128, 512], F32), ctx.psum("f1_ph1", [128, 512], F32)]
        p_y = [ctx.psum("f1_py0", [128, 1024], F32), ctx.psum("f1_py1", [128, 1024], F32)]
        ptb = ctx.psum("f1_ptb", [128, 1024], BF16)
        ln = LNScratch(ctx, "f1")

        ctx.op("dve", lambda e: e.memset(rs[:, 4:5], 0.0), w=[rs])
        load_w_kc(ctx, rwb, W["L1_ffn_Wr"], 0, D, 0, 8)

        for g in range(G):
            for j in range(NTG):
                jg = g * NTG + j
                emit_ln_tile(ctx, c, ln, xb, jg * D, zT, j * 128, GT, ptb, p_y[0])
            for j in range(NTG):
                def logits_mm(e, j=j):
                    last = None
                    for k in range(KC):
                        last = e.matmul(p_small[:, 0:8],
                                        zT[:, k * GT + j * 128:k * GT + j * 128 + 128],
                                        rwb[:, k * 8:(k + 1) * 8],
                                        start=(k == 0), stop=(k == KC - 1))
                    return last
                ctx.op("pe", logits_mm, r=[zT, rwb], w=[p_small])
                ctx.op("dve", lambda e: e.tensor_reduce(rs[:, 0:1], p_small[:, 0:8], axis=AX.X,
                                                        op=ALU.max, negate=True),
                       r=[p_small], w=[rs], late=True)
                ctx.op("dve", lambda e: e.tensor_scalar(msk[:, :], p_small[:, 0:8], rs[:, 0:1],
                                                        rs[:, 4:5], op0=ALU.add, op1=ALU.is_ge),
                       r=[p_small, rs], w=[msk])
                ctx.op("dve", lambda e: e.scalar_tensor_tensor(ln.tctr[:, 0:8], msk[:, :], -1e9,
                                                               p_small[:, 0:8], op0=ALU.mult,
                                                               op1=ALU.add),
                       r=[msk, p_small], w=[ln.tctr])
                ctx.op("dve", lambda e: e.tensor_reduce(rs[:, 1:2], ln.tctr[:, 0:8], axis=AX.X,
                                                        op=ALU.max, negate=True),
                       r=[ln.tctr], w=[rs], late=True)
                ctx.op("dve", lambda e: e.tensor_sub(rs[:, 2:3], rs[:, 1:2], rs[:, 0:1]),
                       r=[rs], w=[rs])
                ctx.op("act", lambda e: e.activation(rs[:, 3:4], rs[:, 2:3], AF.Sigmoid),
                       r=[rs], w=[rs])
                ctx.op("dve", lambda e, j=j: e.tensor_scalar(coef[:, j * 8:(j + 1) * 8],
                                                             msk[:, :], rs[:, 3:4], None,
                                                             op0=ALU.mult),
                       r=[msk, rs], w=[coef])
            for e_i in range(8):
                for q in range(NQ):
                    load_w_kc(ctx, w1b, W["L1_ffn_W1"][e_i], 0, D, q * 1024, 1024)
                    load_w_kc(ctx, w2b, W["L1_ffn_W2"][e_i], q * 1024, 1024, 0, 1024)
                    for t in range(GT // 512):
                        for m in range(KC):
                            ph = p_h[m % 2]
                            def hmm(e, m=m, t=t, ph=ph):
                                last = None
                                for k in range(KC):
                                    last = e.matmul(
                                        ph[:, :],
                                        w1b[:, k * 1024 + m * 128:k * 1024 + (m + 1) * 128],
                                        zT[:, k * GT + t * 512:k * GT + (t + 1) * 512],
                                        start=(k == 0), stop=(k == KC - 1))
                                return last
                            ctx.op("pe", hmm, r=[w1b, zT], w=[ph])
                            ctx.op("act", lambda e, m=m, ph=ph: e.activation(
                                hT[:, m * 512:(m + 1) * 512], ph[:, :], AF.Gelu),
                                r=[ph], w=[hT])
                        for sub in range(4):
                            py = p_y[sub % 2]
                            sc = sub * 128
                            def ymm(e, sc=sc, py=py):
                                last = None
                                for k in range(KC):
                                    for half in range(2):
                                        last = e.matmul(
                                            py[:, half * 512:(half + 1) * 512],
                                            hT[:, k * 512 + sc:k * 512 + sc + 128],
                                            w2b[:, k * 1024 + half * 512:k * 1024 + (half + 1) * 512],
                                            start=(k == 0), stop=(k == KC - 1))
                                return last
                            ctx.op("pe", ymm, r=[hT, w2b], w=[py])
                            j_in_g = t * 4 + sub
                            jg = g * NTG + j_in_g
                            cc = coef[:, j_in_g * 8 + e_i:j_in_g * 8 + e_i + 1]
                            ctx.op("dve", lambda e, py=py, jg=jg, cc=cc: e.scalar_tensor_tensor(
                                xb[:, jg * D:(jg + 1) * D], py[:, :], cc,
                                xb[:, jg * D:(jg + 1) * D], op0=ALU.mult, op1=ALU.add),
                                r=[py, coef, xb], w=[xb])

def emit_l0_attn_b(ctx, c, W, xb):
    NC = S // 128
    with ctx.scope():
        bqc = ctx.sbuf("ab_bqc", [128, 8], F32)
        bkc = ctx.sbuf("ab_bkc", [128, 8], F32)
        tri = ctx.sbuf("ab_tri", [128, 128], BF16)
        onesf = ctx.sbuf("ab_onesf", [1, 128], F32)
        bxr = ctx.sbuf("ab_brow", [1, 2 * D], BF16)
        ptb = ctx.psum("ab_ptb", [128, 1024], BF16)
        emit_tri_mask(ctx, c, tri)
        ctx.op("dve", lambda e: e.memset(onesf[:, :], 1.0), w=[onesf])
        for i, nm in enumerate(["bv", "bo"]):
            ctx.dma("pool", bxr[0:1, i * D:(i + 1) * D],
                    W["L0_attn_b_" + nm].unsqueeze(0), w=[bxr])
        with ctx.scope():
            bqkr = ctx.sbuf("ab_bqkr", [1, 2 * D], BF16)
            for i, nm in enumerate(["bq", "bk"]):
                ctx.dma("pool", bqkr[0:1, i * D:(i + 1) * D],
                        W["L0_attn_b_" + nm].unsqueeze(0), w=[bqkr])
            for cols, off in [(bqc, 0), (bkc, D)]:
                def tr(e, off=off):
                    last = None
                    for cc in range(8):
                        last = e.transpose(ptb[:, 2 * cc:2 * cc + 1],
                                           bqkr[0:1, off + cc * 128:off + (cc + 1) * 128],
                                           c["ident"][0:1, 0:1])
                    return last
                ctx.op("pe", tr, r=[bqkr, c["ident"]], w=[ptb])
                ctx.op("act", lambda e, cols=cols: e.copy(
                    cols[:, 0:8], ptb[:, :].rearrange("p (a b) -> p a b", b=2)[:, 0:8, 0]),
                       r=[ptb], w=[cols])
        zT = ctx.sbuf("ab_zT", [128, KC * S], BF16)
        wq_sl = ctx.sbuf("ab_wq", [128, KC * 128], BF16)
        wk_sl = ctx.sbuf("ab_wk", [128, KC * 128], BF16)
        wv_sl = ctx.sbuf("ab_wv", [128, KC * 128], BF16)
        wo_sl = ctx.sbuf("ab_wo", [128, 1024], BF16)
        qphi = ctx.sbuf("ab_qphi", [128, S], BF16)
        kphi = ctx.sbuf("ab_kphi", [128, S], BF16)
        ktokA = ctx.sbuf("ab_ktA", [128, NC * 64], BF16)
        ktokB = ctx.sbuf("ab_ktB", [128, NC * 64], BF16)
        vauA = ctx.sbuf("ab_vaA", [128, NC * 65], BF16)
        vauB = ctx.sbuf("ab_vaB", [128, NC * 65], BF16)
        otp = ctx.sbuf("ab_otp", [128, S], BF16)
        stsb = ctx.sbuf("ab_stsb", [128, 65], BF16)
        den = ctx.sbuf("ab_den", [1, 256], F32)
        rden = ctx.sbuf("ab_rden", [1, 256], F32)
        rbA = ctx.sbuf("ab_rbA", [64, 128], F32)
        rbB = ctx.sbuf("ab_rbB", [64, 128], F32)
        elu1 = ctx.sbuf("ab_e1", [128, 512], F32)
        elu2 = ctx.sbuf("ab_e2", [128, 512], BF16)
        ln = LNScratch(ctx, "ab")

        for b in range(2):
            with ctx.scope():
                junk = ctx.psum("ab_junk", [128, 1024], F32)
                for j in range(NC):
                    jg = b * NC + j
                    emit_ln_tile(ctx, c, ln, xb, jg * D, zT, j * 128, S, ptb, junk)
            for pr in range(8):
                hA = 2 * pr
                load_w_kc(ctx, wq_sl, W["L0_attn_b_WqT"], 0, D, hA * 64, 128)
                load_w_kc(ctx, wk_sl, W["L0_attn_b_WkT"], 0, D, hA * 64, 128)
                load_w_kc(ctx, wv_sl, W["L0_attn_b_WvT"], 0, D, hA * 64, 128)
                load_w_kc(ctx, wo_sl, W["L0_attn_b_WoT"], pr * 128, 128, 0, 1024)
                with ctx.scope():
                    p0 = ctx.psum("ab_p0", [128, 512], F32)
                    p1 = ctx.psum("ab_p1", [128, 512], F32)
                    for dst, wsl, bc in [(qphi, wq_sl, bqc), (kphi, wk_sl, bkc)]:
                        for t in range(4):
                            pp = [p0, p1][t % 2]
                            def qmm(e, t=t, pp=pp, wsl=wsl):
                                last = None
                                for k in range(KC):
                                    last = e.matmul(
                                        pp[:, :], wsl[:, k * 128:(k + 1) * 128],
                                        zT[:, k * S + t * 512:k * S + (t + 1) * 512],
                                        start=(k == 0), stop=(k == KC - 1))
                                return last
                            ctx.op("pe", qmm, r=[wsl, zT], w=[pp])
                            bap = bc[:, pr:pr + 1]
                            ctx.op("dve", lambda e, pp=pp, bap=bap, bc=bc: e.tensor_scalar(
                                elu1[:, :], pp[:, :], bap, 0.0, op0=ALU.add, op1=ALU.min),
                                r=[pp, bc], w=[elu1])
                            ctx.op("act", lambda e: e.activation(elu2[:, :], elu1[:, :], AF.Exp),
                                   r=[elu1], w=[elu2])
                            ctx.op("dve", lambda e, pp=pp, bap=bap, bc=bc: e.tensor_scalar(
                                elu1[:, :], pp[:, :], bap, 0.0, op0=ALU.add, op1=ALU.max),
                                r=[pp, bc], w=[elu1])
                            ctx.op("dve", lambda e, t=t, dst=dst: e.tensor_add(
                                dst[:, t * 512:(t + 1) * 512], elu1[:, :], elu2[:, :]),
                                r=[elu1, elu2], w=[dst])
                    for src_lo, ktok in [(0, ktokA), (64, ktokB)]:
                        def ktr(e, src_lo=src_lo):
                            last = None
                            for i in range(NC):
                                last = e.transpose(ptb[:, i * 64:(i + 1) * 64],
                                                   kphi[src_lo:src_lo + 64, i * 128:(i + 1) * 128],
                                                   c["ident"][src_lo:src_lo + 64,
                                                              src_lo:src_lo + 64])
                            return last
                        ctx.op("pe", ktr, r=[kphi, c["ident"]], w=[ptb])
                        ctx.op("act", lambda e, ktok=ktok: e.copy(ktok[:, :], ptb[:, 0:NC * 64]),
                               r=[ptb], w=[ktok])
                    ctx.op("dve", lambda e: e.memset(vauA[:, :], 1.0), w=[vauA])
                    ctx.op("dve", lambda e: e.memset(vauB[:, :], 1.0), w=[vauB])
                    for t in range(NC):
                        pp = [p0, p1][t % 2]
                        def vmm(e, t=t, pp=pp):
                            for k in range(KC):
                                e.matmul(pp[:, 0:128],
                                         zT[:, k * S + t * 128:k * S + (t + 1) * 128],
                                         wv_sl[:, k * 128:(k + 1) * 128],
                                         start=(k == 0), stop=False)
                            return e.matmul(pp[:, 0:128], c["ones"][0:1, :],
                                            bxr[0:1, hA * 64:hA * 64 + 128],
                                            start=False, stop=True)
                        ctx.op("pe", vmm, r=[zT, wv_sl, c["ones"], bxr], w=[pp])
                        ctx.op("act", lambda e, t=t, pp=pp: e.copy(vauA[:, t * 65:t * 65 + 64],
                                                                   pp[:, 0:64]), r=[pp], w=[vauA])
                        ctx.op("act", lambda e, t=t, pp=pp: e.copy(vauB[:, t * 65:t * 65 + 64],
                                                                   pp[:, 64:128]), r=[pp], w=[vauB])
                with ctx.scope():
                    smA = ctx.psum("ab_smA", [128, 128], F32)
                    smB = ctx.psum("ab_smB", [128, 128], F32)
                    nmA = ctx.psum("ab_nmA", [65, 128], F32)
                    nmB = ctx.psum("ab_nmB", [65, 128], F32)
                    pbA = ctx.psum("ab_pbA", [64, 128], F32)
                    pbB = ctx.psum("ab_pbB", [64, 128], F32)
                    st2 = ctx.psum("ab_st2", [128, 65], F32)
                    smAs = ctx.sbuf("ab_smAs", [128, 128], BF16)
                    smBs = ctx.sbuf("ab_smBs", [128, 128], BF16)
                    ctx.op("dve", lambda e: e.memset(stsb[:, :], 0.0), w=[stsb])
                    for i in range(NC):
                        for lo, sm, sms, nm, pb, rb, ktok, vau, dc in [
                                (0, smA, smAs, nmA, pbA, rbA, ktokA, vauA, 0),
                                (64, smB, smBs, nmB, pbB, rbB, ktokB, vauB, 128)]:
                            ctx.op("pe", lambda e, lo=lo, sm=sm, i=i: e.matmul(
                                sm[:, :], kphi[lo:lo + 64, i * 128:(i + 1) * 128],
                                qphi[lo:lo + 64, i * 128:(i + 1) * 128],
                                start=True, stop=True, tile_position=(lo, 0)),
                                r=[kphi, qphi], w=[sm])
                            ctx.op("dve", lambda e, sm=sm, sms=sms: e.tensor_mul(
                                sms[:, :], sm[:, :], tri[:, :]),
                                r=[sm, tri], w=[sms])
                            def numm(e, lo=lo, nm=nm, sms=sms, vau=vau, i=i):
                                e.matmul(nm[:, :], stsb[lo:lo + 64, :],
                                         qphi[lo:lo + 64, i * 128:(i + 1) * 128],
                                         start=True, stop=False)
                                return e.matmul(nm[:, :], vau[:, i * 65:(i + 1) * 65],
                                                sms[:, :], start=False, stop=True)
                            ctx.op("pe", numm, r=[stsb, qphi, vau, sms], w=[nm])
                            ctx.op("pe", lambda e, lo=lo, ktok=ktok, vau=vau, i=i: e.matmul(
                                st2[lo:lo + 64, :], ktok[:, i * 64:(i + 1) * 64],
                                vau[:, i * 65:(i + 1) * 65],
                                start=(i == 0), stop=(i == NC - 1)),
                                r=[ktok, vau], w=[st2])
                            ctx.op("act", lambda e, lo=lo: e.copy(
                                stsb[lo:lo + 64, :], st2[lo:lo + 64, :]), r=[st2], w=[stsb])
                            ctx.op("dve", lambda e, nm=nm, dc=dc: e.tensor_scalar(
                                den[0:1, dc:dc + 128], nm[64:65, :], 1e-6, None, op0=ALU.add),
                                r=[nm], w=[den])
                            ctx.op("dve", lambda e, dc=dc: e.reciprocal(
                                rden[0:1, dc:dc + 128], den[0:1, dc:dc + 128]),
                                r=[den], w=[rden], late=True)
                            ctx.op("pe", lambda e, pb=pb, dc=dc: e.matmul(
                                pb[:, :], onesf[0:1, 0:64], rden[0:1, dc:dc + 128],
                                start=True, stop=True), r=[onesf, rden], w=[pb])
                            ctx.op("act", lambda e, rb=rb, pb=pb: e.copy(rb[:, :], pb[:, :]),
                                   r=[pb], w=[rb])
                            ctx.op("dve", lambda e, lo=lo, nm=nm, rb=rb, i=i: e.tensor_mul(
                                otp[lo:lo + 64, i * 128:(i + 1) * 128], nm[0:64, :], rb[:, :]),
                                r=[nm, rb], w=[otp])
                with ctx.scope():
                    py = [ctx.psum("ab_py0", [128, 512], F32), ctx.psum("ab_py1", [128, 512], F32)]
                    for t in range(NC):
                        for half in range(2):
                            pp = py[half]
                            def ymm(e, t=t, half=half, pp=pp, pr=pr):
                                last = e.matmul(pp[:, :], otp[:, t * 128:(t + 1) * 128],
                                                wo_sl[:, half * 512:(half + 1) * 512],
                                                start=True, stop=(pr != 0))
                                if pr == 0:
                                    last = e.matmul(pp[:, :], c["ones"][0:1, :],
                                                    bxr[0:1, D + half * 512:
                                                        D + (half + 1) * 512],
                                                    start=False, stop=True)
                                return last
                            ctx.op("pe", ymm, r=[otp, wo_sl, c["ones"], bxr], w=[pp])
                            jg = b * NC + t
                            ctx.op("dve", lambda e, pp=pp, jg=jg, half=half: e.tensor_add(
                                xb[:, jg * D + half * 512:jg * D + (half + 1) * 512],
                                xb[:, jg * D + half * 512:jg * D + (half + 1) * 512],
                                pp[:, :]), r=[pp, xb], w=[xb])


def emit_l0_attn_a(ctx, c, W, xb):
    NC = S // 128
    QT = 122
    NTILE = (S + QT - 1) // QT
    PAD = (NTILE - 1) * QT + 128
    with ctx.scope():
        zT = ctx.sbuf("aa_zT", [128, KC * S], BF16)
        band = ctx.sbuf("aa_band", [128, 128], BF16)
        bor = ctx.sbuf("aa_bor", [1, D], BF16)
        projB = ctx.sbuf("aa_proj", [128, 64], BF16)
        wsl = ctx.sbuf("aa_wsl", [128, KC * 128], BF16)
        wosl = ctx.sbuf("aa_wo", [128, 1024], BF16)
        qf = ctx.sbuf("aa_qf", [128, S], BF16)
        kfp = ctx.sbuf("aa_kfp", [128, PAD], BF16)
        vfp = ctx.sbuf("aa_vfp", [128, PAD], BF16)
        qp = ctx.sbuf("aa_qp", [64, S], BF16)
        kpp = ctx.sbuf("aa_kpp", [64, PAD], BF16)
        otb = ctx.sbuf("aa_ot", [128, S], BF16)
        smf = ctx.sbuf("aa_smf", [128, 128], F32)
        smz = ctx.sbuf("aa_smz", [128, 128], BF16)
        szt = ctx.sbuf("aa_szt", [128, 128], BF16)
        vwt = ctx.sbuf("aa_vwt", [128, 128], BF16)
        rs = ctx.sbuf("aa_rs", [128, 2], F32)
        elu1 = ctx.sbuf("aa_e1", [64, 512], F32)
        elu2 = ctx.sbuf("aa_e2", [64, 512], BF16)
        ln = LNScratch(ctx, "aa")
        ptb = ctx.psum("aa_ptb", [128, 1024], BF16)

        emit_band_mask(ctx, c, band)
        ctx.dma("pool", bor[0:1, :], W["L0_attn_a_b_out"].unsqueeze(0), w=[bor])
        ctx.dma("pool", projB[:, :], W["L0_attn_a_proj"], w=[projB])

        for b in range(2):
            with ctx.scope():
                junk = ctx.psum("aa_junk", [128, 1024], F32)
                for j in range(NC):
                    jg = b * NC + j
                    emit_ln_tile(ctx, c, ln, xb, jg * D, zT, j * 128, S, ptb, junk)
            for h in range(8):
                with ctx.scope():
                    p0 = ctx.psum("aa_p0", [128, 512], F32)
                    p1 = ctx.psum("aa_p1", [128, 512], F32)
                    pf = ctx.psum("aa_pf", [64, 512], F32)
                    for dst, coff, pad in [(qf, h * 128, 0), (kfp, D + h * 128, 3),
                                           (vfp, 2 * D + h * 128, 3)]:
                        if pad:
                            ctx.op("dve", lambda e, dst=dst: e.memset(dst[:, :], 0.0), w=[dst])
                        load_w_kc(ctx, wsl, W["L0_attn_a_w_qkv"], 0, D, coff, 128)
                        for t in range(4):
                            pp = [p0, p1][t % 2]
                            def qmm(e, t=t, pp=pp):
                                last = None
                                for k in range(KC):
                                    last = e.matmul(
                                        pp[:, :], wsl[:, k * 128:(k + 1) * 128],
                                        zT[:, k * S + t * 512:k * S + (t + 1) * 512],
                                        start=(k == 0), stop=(k == KC - 1))
                                return last
                            ctx.op("pe", qmm, r=[wsl, zT], w=[pp])
                            ctx.op("act", lambda e, t=t, pp=pp, dst=dst, pad=pad: e.copy(
                                dst[:, pad + t * 512:pad + (t + 1) * 512], pp[:, :]),
                                r=[pp], w=[dst])
                    for src, dst, pad, n in [(qf, qp, 0, 4), (kfp, kpp, 3, 4)]:
                        if pad:
                            ctx.op("dve", lambda e, dst=dst: e.memset(dst[:, :], 0.0), w=[dst])
                        for t in range(n):
                            def fmm(e, t=t, src=src, pad=pad):
                                return e.matmul(pf[:, :], projB[:, :],
                                                src[:, pad + t * 512:pad + (t + 1) * 512],
                                                start=True, stop=True)
                            ctx.op("pe", fmm, r=[projB, src], w=[pf])
                            ctx.op("dve", lambda e: e.tensor_scalar(
                                elu1[:, :], pf[:, :], 0.0, None, op0=ALU.min),
                                r=[pf], w=[elu1])
                            ctx.op("act", lambda e: e.activation(elu2[:, :], elu1[:, :], AF.Exp),
                                   r=[elu1], w=[elu2])
                            ctx.op("dve", lambda e: e.tensor_scalar(
                                elu1[:, :], pf[:, :], 0.0, None, op0=ALU.max),
                                r=[pf], w=[elu1])
                            ctx.op("dve", lambda e, t=t, dst=dst, pad=pad: e.tensor_add(
                                dst[:, pad + t * 512:pad + (t + 1) * 512], elu1[:, :],
                                elu2[:, :]), r=[elu1, elu2], w=[dst])
                with ctx.scope():
                    psm = ctx.psum("aa_psm", [128, 128], F32)
                    pot = ctx.psum("aa_pot", [128, 128], F32)
                    for j in range(NTILE):
                        q0 = j * QT
                        qn = min(QT, S - q0)
                        ctx.op("pe", lambda e, q0=q0, qn=qn: e.matmul(
                            psm[0:qn, :], qp[:, q0:q0 + qn], kpp[:, q0:q0 + 128],
                            start=True, stop=True), r=[qp, kpp], w=[psm])
                        ctx.op("dve", lambda e, qn=qn: e.tensor_mul(
                            smf[0:qn, :], psm[0:qn, :], band[0:qn, :]),
                            r=[psm, band], w=[smf])
                        ctx.op("dve", lambda e, qn=qn: e.tensor_reduce(
                            rs[0:qn, 0:1], smf[0:qn, :], axis=AX.X, op=ALU.add),
                            r=[smf], w=[rs], late=True)
                        ctx.op("dve", lambda e, qn=qn: e.tensor_scalar(
                            rs[0:qn, 1:2], rs[0:qn, 0:1], 1e-8, None, op0=ALU.add),
                            r=[rs], w=[rs])
                        ctx.op("dve", lambda e, qn=qn: e.reciprocal(
                            rs[0:qn, 0:1], rs[0:qn, 1:2]), r=[rs], w=[rs], late=True)
                        ctx.op("dve", lambda e, qn=qn: e.tensor_scalar(
                            smz[0:qn, :], smf[0:qn, :], rs[0:qn, 0:1], None, op0=ALU.mult),
                            r=[smf, rs], w=[smz])
                        def trz(e, qn=qn):
                            return e.transpose(ptb[:, 0:qn], smz[0:qn, :],
                                               c["ident"][0:qn, 0:qn])
                        ctx.op("pe", trz, r=[smz, c["ident"]], w=[ptb])
                        ctx.op("act", lambda e, qn=qn: e.copy(szt[:, 0:qn], ptb[:, 0:qn]),
                               r=[ptb], w=[szt])
                        def trv(e, q0=q0):
                            return e.transpose(ptb[:, 128:256], vfp[:, q0:q0 + 128],
                                               c["ident"][:, :])
                        ctx.op("pe", trv, r=[vfp, c["ident"]], w=[ptb])
                        ctx.op("act", lambda e: e.copy(vwt[:, :], ptb[:, 128:256]),
                               r=[ptb], w=[vwt])
                        ctx.op("pe", lambda e, qn=qn: e.matmul(
                            pot[:, 0:qn], vwt[:, :], szt[:, 0:qn], start=True, stop=True),
                            r=[vwt, szt], w=[pot])
                        ctx.op("act", lambda e, q0=q0, qn=qn: e.copy(
                            otb[:, q0:q0 + qn], pot[:, 0:qn]), r=[pot], w=[otb])
                load_w_kc(ctx, wosl, W["L0_attn_a_w_out"], h * 128, 128, 0, 1024)
                with ctx.scope():
                    py = [ctx.psum("aa_py0", [128, 512], F32), ctx.psum("aa_py1", [128, 512], F32)]
                    for t in range(NC):
                        for half in range(2):
                            pp = py[half]
                            def ymm(e, t=t, half=half, pp=pp, h=h):
                                last = e.matmul(pp[:, :], otb[:, t * 128:(t + 1) * 128],
                                                wosl[:, half * 512:(half + 1) * 512],
                                                start=True, stop=(h != 0))
                                if h == 0:
                                    last = e.matmul(pp[:, :], c["ones"][0:1, :],
                                                    bor[0:1, half * 512:(half + 1) * 512],
                                                    start=False, stop=True)
                                return last
                            ctx.op("pe", ymm, r=[otb, wosl, c["ones"], bor], w=[pp])
                            jg = b * NC + t
                            ctx.op("dve", lambda e, pp=pp, jg=jg, half=half: e.tensor_add(
                                xb[:, jg * D + half * 512:jg * D + (half + 1) * 512],
                                xb[:, jg * D + half * 512:jg * D + (half + 1) * 512],
                                pp[:, :]), r=[pp, xb], w=[xb])



def emit_rope_tables(ctx, c, ct, st, mode):
    TWO_PI = 6.283185307179586
    with ctx.scope():
        pos = ctx.sbuf("rp_pos", [128, 1024], F32)
        th = ctx.sbuf("rp_th", [128, 64], F32)
        m = ctx.sbuf("rp_m", [128, 1024], F32)
        fr = ctx.sbuf("rp_fr", [128, 1024], F32)
        i32 = ctx.sbuf("rp_i32", [128, 1024], mybir.dt.int32)
        ctx.op("pool", lambda e: e.iota(pos[:, :].rearrange("p (j d) -> p j d", d=64),
                                        [[128, 16], [0, 64]], channel_multiplier=1,
                                        allow_small_or_imprecise_dtypes=True), w=[pos])
        if mode == "neox":
            ctx.op("pool", lambda e: e.iota(th[:, :], [[0, 2], [1, 32]], channel_multiplier=0,
                                            allow_small_or_imprecise_dtypes=True), w=[th])
        else:
            ctx.op("pool", lambda e: e.iota(th[:, :], [[1, 32], [0, 2]], channel_multiplier=0,
                                            allow_small_or_imprecise_dtypes=True), w=[th])
        ctx.op("act", lambda e: e.activation(th[:, :], th[:, :], AF.Exp,
                                             scale=-0.2878231366242557), r=[th], w=[th])
        ctx.op("dve", lambda e: e.tensor_mul(m[:, :].rearrange("p (j d) -> p j d", d=64),
                                             pos[:, :].rearrange("p (j d) -> p j d", d=64),
                                             th[:, :].unsqueeze(1).broadcast_to((128, 16, 64))),
               r=[pos, th], w=[m])
        ctx.op("dve", lambda e: e.tensor_scalar(m[:, :], m[:, :], 1.0 / TWO_PI, None,
                                                op0=ALU.mult), r=[m], w=[m])
        ctx.op("dve", lambda e: e.tensor_copy(i32[:, :], m[:, :]), r=[m], w=[i32])
        ctx.op("dve", lambda e: e.tensor_copy(pos[:, :], i32[:, :]), r=[i32], w=[pos])
        ctx.op("dve", lambda e: e.tensor_sub(fr[:, :], m[:, :], pos[:, :]), r=[m, pos], w=[fr])
        ctx.op("act", lambda e: e.activation(fr[:, :], fr[:, :], AF.Sin, scale=TWO_PI),
               r=[fr], w=[fr])
        if mode == "neox":
            v = fr[:, :].rearrange("p (j h d) -> p j h d", h=2, d=32)
            sv = st[:, :].rearrange("p (j h d) -> p j h d", h=2, d=32)
            ctx.op("dve", lambda e: e.tensor_scalar(sv[:, :, 0, :], v[:, :, 0, :], -1.0, None,
                                                    op0=ALU.mult), r=[fr], w=[st])
            ctx.op("dve", lambda e: e.tensor_copy(sv[:, :, 1, :], v[:, :, 1, :]),
                   r=[fr], w=[st])
        else:
            v2 = fr[:, :].rearrange("p (j d h) -> p j d h", h=2, d=32)
            sv = st[:, :].rearrange("p (j d h) -> p j d h", h=2, d=32)
            ctx.op("dve", lambda e: e.tensor_scalar(sv[:, :, :, 0], v2[:, :, :, 0], -1.0, None,
                                                    op0=ALU.mult), r=[fr], w=[st])
            ctx.op("dve", lambda e: e.tensor_copy(sv[:, :, :, 1], v2[:, :, :, 1]),
                   r=[fr], w=[st])
        ctx.op("dve", lambda e: e.tensor_scalar(m[:, :], m[:, :], 0.25, None, op0=ALU.add),
               r=[m], w=[m])
        ctx.op("dve", lambda e: e.tensor_copy(i32[:, :], m[:, :]), r=[m], w=[i32])
        ctx.op("dve", lambda e: e.tensor_copy(pos[:, :], i32[:, :]), r=[i32], w=[pos])
        ctx.op("dve", lambda e: e.tensor_sub(fr[:, :], m[:, :], pos[:, :]), r=[m, pos], w=[fr])
        ctx.op("act", lambda e: e.activation(ct[:, :], fr[:, :], AF.Sin, scale=TWO_PI),
               r=[fr], w=[ct])



def emit_l1_attn_b(ctx, c, W, xb):
    NC = S // 128
    H, KV, HD = 16, 4, 64
    with ctx.scope():
        ct = ctx.sbuf("sb_ct", [128, NC * HD], BF16)
        st = ctx.sbuf("sb_st", [128, NC * HD], BF16)
        zT = ctx.sbuf("sb_zT", [128, KC * S], BF16)
        masks = ctx.sbuf("sb_msk", [128, 8 * 512], BF16)
        emit_rope_tables(ctx, c, ct, st, "neox")
        with ctx.scope():
            mi = ctx.sbuf("sb_mi", [128, 512], F32)
            mj = ctx.sbuf("sb_mj", [128, 512], F32)
            mk = ctx.sbuf("sb_mk", [128, 512], F32)
            ctx.op("pool", lambda e: e.iota(
                mi[:, :], [[1, 512]], channel_multiplier=-1,
                allow_small_or_imprecise_dtypes=True), w=[mi])
            for mix, delta in enumerate(range(-384, 513, 128)):
                ctx.op("dve", lambda e, delta=delta: e.tensor_scalar(
                    mj[:, :], mi[:, :], float(-delta), None, op0=ALU.is_ge),
                    r=[mi], w=[mj])
                ctx.op("dve", lambda e, delta=delta: e.tensor_scalar(
                    mk[:, :], mi[:, :], float(511 - delta), None,
                    op0=ALU.is_le), r=[mi], w=[mk])
                ctx.op("dve", lambda e, mix=mix: e.tensor_mul(
                    masks[:, mix * 512:(mix + 1) * 512], mj[:, :], mk[:, :]),
                    r=[mj, mk], w=[masks])
        for b in range(2):
            with ctx.scope():
                junk = ctx.psum("sb_junk", [128, 1024], F32)
                ptb0 = ctx.psum("sb_ptb0", [128, 1024], BF16)
                ln = LNScratch(ctx, "sb")
                for j in range(NC):
                    jg = b * NC + j
                    emit_ln_tile(ctx, c, ln, xb, jg * D, zT, j * 128, S, ptb0, junk)
            for g in range(KV):
                with ctx.scope():
                    qf = ctx.sbuf("sb_qf", [64, 4 * S], BF16)
                    kf = ctx.sbuf("sb_kf", [64, S], BF16)
                    vau = ctx.sbuf("sb_vau", [128, NC * 65], BF16)
                    otb = ctx.sbuf("sb_ot", [128, S], BF16)
                    with ctx.scope():
                        wq_sl = ctx.sbuf("sb_wq", [128, KC * 256], BF16)
                        wkv_sl = ctx.sbuf("sb_wkv", [128, KC * 128], BF16)
                        qn = ctx.sbuf("sb_qn", [128, 256], F32)
                        t1 = ctx.sbuf("sb_t1", [128, 256], F32)
                        t2 = ctx.sbuf("sb_t2", [128, 256], BF16)
                        qro = ctx.sbuf("sb_qro", [128, 256], BF16)
                        nsq = ctx.sbuf("sb_nsq", [128, 12], F32)
                        pq = ctx.psum("sb_pq", [128, 384], F32)
                        ptb = ctx.psum("sb_ptb", [128, 1024], BF16)
                        load_w_kc(ctx, wq_sl, W["L1_attn_b_WqT"], 0, D, g * 256, 256)
                        load_w_kc(ctx, wkv_sl, W["L1_attn_b_WkT"], 0, D, g * 64, 64)
                        load_w_kc(ctx, wkv_sl, W["L1_attn_b_WvT"], 0, D, g * 64, 64,
                                  dst_col0=KC * 64)
                        ctx.op("dve", lambda e: e.memset(vau[:, :], 1.0), w=[vau])
                        for j in range(NC):
                            def pmm(e, j=j):
                                for k in range(KC):
                                    e.matmul(pq[:, 0:256],
                                             zT[:, k * S + j * 128:k * S + (j + 1) * 128],
                                             wq_sl[:, k * 256:(k + 1) * 256],
                                             start=(k == 0), stop=(k == KC - 1))
                                for k in range(KC):
                                    e.matmul(pq[:, 256:320],
                                             zT[:, k * S + j * 128:k * S + (j + 1) * 128],
                                             wkv_sl[:, k * 64:(k + 1) * 64],
                                             start=(k == 0), stop=(k == KC - 1))
                                last = None
                                for k in range(KC):
                                    last = e.matmul(
                                        pq[:, 320:384],
                                        zT[:, k * S + j * 128:k * S + (j + 1) * 128],
                                        wkv_sl[:, KC * 64 + k * 64:KC * 64 + (k + 1) * 64],
                                        start=(k == 0), stop=(k == KC - 1))
                                return last
                            ctx.op("pe", pmm, r=[zT, wq_sl, wkv_sl], w=[pq])
                            ctx.op("act", lambda e, j=j: e.copy(vau[:, j * 65:j * 65 + 64],
                                                                pq[:, 320:384]),
                                   r=[pq], w=[vau])
                            ctx.op("act", lambda e: e.activation(
                                qn[:, 0:256], pq[:, 0:256], AF.Square), r=[pq], w=[qn])
                            ctx.op("act", lambda e: e.activation(
                                t1[:, 0:64], pq[:, 256:320], AF.Square), r=[pq], w=[t1])
                            ctx.op("dve", lambda e: e.tensor_reduce(
                                nsq[:, 0:4], qn[:, 0:256].rearrange("p (h d) -> p h d", d=64),
                                axis=AX.X, op=ALU.add), r=[qn], w=[nsq], late=True)
                            ctx.op("dve", lambda e: e.tensor_reduce(
                                nsq[:, 4:5], t1[:, 0:64].rearrange("p (h d) -> p h d", d=64),
                                axis=AX.X, op=ALU.add), r=[t1], w=[nsq], late=True)
                            ctx.op("act", lambda e: e.activation(nsq[:, 6:11], nsq[:, 0:5],
                                                                 AF.Sqrt), r=[nsq], w=[nsq])
                            ctx.op("dve", lambda e: e.tensor_scalar(
                                nsq[:, 0:5], nsq[:, 6:11], 1e-6, None, op0=ALU.add),
                                r=[nsq], w=[nsq])
                            ctx.op("dve", lambda e: e.reciprocal(nsq[:, 6:11], nsq[:, 0:5]),
                                   r=[nsq], w=[nsq], late=True)
                            ctx.op("dve", lambda e: e.tensor_mul(
                                qn[:, 0:256].rearrange("p (h d) -> p h d", d=64),
                                pq[:, 0:256].rearrange("p (h d) -> p h d", d=64),
                                nsq[:, 6:10].unsqueeze(2).broadcast_to((128, 4, 64))),
                                r=[pq, nsq], w=[qn])
                            ctx.op("dve", lambda e, j=j: e.tensor_mul(
                                t1[:, 0:256].rearrange("p (h d) -> p h d", d=64),
                                qn[:, 0:256].rearrange("p (h d) -> p h d", d=64),
                                ct[:, j * 64:(j + 1) * 64].unsqueeze(1).broadcast_to((128, 4, 64))),
                                r=[qn, ct], w=[t1])
                            ctx.op("dve", lambda e, j=j: e.tensor_mul(
                                t2[:, 0:256].rearrange("p (h c d) -> p h c d", c=2, d=32),
                                qn[:, 0:256].rearrange("p (h c d) -> p h c d", c=2, d=32)
                                [:, :, ::-1, :],
                                st[:, j * 64:(j + 1) * 64].rearrange("p (c d) -> p c d", d=32)
                                .unsqueeze(1).broadcast_to((128, 4, 2, 32))),
                                r=[qn, st], w=[t2])
                            ctx.op("dve", lambda e: e.tensor_add(qro[:, 0:256], t1[:, 0:256],
                                                                 t2[:, 0:256]),
                                   r=[t1, t2], w=[qro])
                            def qtr(e, j=j):
                                last = None
                                for i in range(4):
                                    last = e.transpose(ptb[:, i * 128:(i + 1) * 128][0:64, :],
                                                       qro[:, i * 64:(i + 1) * 64],
                                                       c["ident"][:, :])
                                return last
                            ctx.op("pe", qtr, r=[qro, c["ident"]], w=[ptb])
                            ctx.op("act", lambda e, j=j: e.copy(
                                qf[:, :].rearrange("p (h t) -> p h t", t=S)[:, :, j * 128:(j + 1) * 128],
                                ptb[0:64, 0:512].rearrange("p (h t) -> p h t", t=128)),
                                r=[ptb], w=[qf])
                            ctx.op("dve", lambda e: e.tensor_scalar(
                                qn[:, 0:64], pq[:, 256:320], nsq[:, 10:11], None, op0=ALU.mult),
                                r=[pq, nsq], w=[qn])
                            ctx.op("dve", lambda e, j=j: e.tensor_mul(
                                t1[:, 0:64], qn[:, 0:64], ct[:, j * 64:(j + 1) * 64]),
                                r=[qn, ct], w=[t1])
                            ctx.op("dve", lambda e, j=j: e.tensor_mul(
                                t2[:, 0:64].rearrange("p (c d) -> p c d", d=32),
                                qn[:, 0:64].rearrange("p (c d) -> p c d", d=32)[:, ::-1, :],
                                st[:, j * 64:(j + 1) * 64].rearrange("p (c d) -> p c d", d=32)),
                                r=[qn, st], w=[t2])
                            ctx.op("dve", lambda e: e.tensor_add(qro[:, 0:64], t1[:, 0:64],
                                                                 t2[:, 0:64]),
                                   r=[t1, t2], w=[qro])
                            ctx.op("pe", lambda e, j=j: e.transpose(
                                ptb[:, 512:640][0:64, :], qro[:, 0:64], c["ident"][:, :]),
                                r=[qro, c["ident"]], w=[ptb])
                            ctx.op("act", lambda e, j=j: e.copy(
                                kf[:, j * 128:(j + 1) * 128], ptb[0:64, 512:640]),
                                r=[ptb], w=[kf])
                    with ctx.scope():
                        atb = ctx.sbuf("sb_at", [128, 512], BF16)
                        exps = ctx.sbuf("sb_exp", [128, 512], BF16)
                        rden = ctx.sbuf("sb_rden", [1, 512], F32)
                        rb = ctx.sbuf("sb_rb", [64, 512], F32)
                        onesf = ctx.sbuf("sb_onesf", [1, 64], F32)
                        wo_sl = ctx.sbuf("sb_wo", [128, 1024], BF16)
                        pst = [ctx.psum("sb_pst0", [128, 512], F32),
                               ctx.psum("sb_pst1", [128, 512], F32)]
                        ppv = [ctx.psum("sb_ppv0", [65, 512], F32),
                               ctx.psum("sb_ppv1", [65, 512], F32)]
                        ppb = ctx.psum("sb_ppb", [64, 512], F32)
                        py = ctx.psum("sb_py", [128, 512], F32)
                        ctx.op("dve", lambda e: e.memset(onesf[:, :], 1.0), w=[onesf])
                        for hi in range(4):
                            lo = (hi % 2) * 64
                            for qt in range(4):
                                ppvb = ppv[qt % 2]
                                kts = list(range(max(0, 4 * qt - 4), 4 * qt + 4))
                                for ki, kt in enumerate(kts):
                                    delta = qt * 512 - kt * 128
                                    mix = (delta + 384) // 128
                                    pss = pst[ki % 2]
                                    ctx.op("pe", lambda e, kt=kt, hi=hi, qt=qt, pss=pss: e.matmul(
                                        pss[:, :], kf[:, kt * 128:(kt + 1) * 128],
                                        qf[:, hi * S + qt * 512:hi * S + (qt + 1) * 512],
                                        start=True, stop=True), r=[kf, qf], w=[pss])
                                    ctx.op("act", lambda e, pss=pss: e.activation(
                                        exps[:, :], pss[:, :], AF.Exp, scale=0.125),
                                        r=[pss], w=[exps])
                                    ctx.op("dve", lambda e, mix=mix: e.tensor_mul(
                                        atb[:, :], exps[:, :],
                                        masks[:, mix * 512:(mix + 1) * 512]),
                                        r=[exps, masks], w=[atb])
                                    ctx.op("pe", lambda e, kt=kt, ki=ki, ppvb=ppvb: e.matmul(
                                        ppvb[:, :], vau[:, kt * 65:(kt + 1) * 65], atb[:, :],
                                        start=(ki == 0), stop=(ki == len(kts) - 1)),
                                        r=[vau, atb], w=[ppvb])
                                ctx.op("dve", lambda e, ppvb=ppvb: e.reciprocal(
                                    rden[0:1, :], ppvb[64:65, :]), r=[ppvb], w=[rden], late=True)
                                ctx.op("pe", lambda e: e.matmul(
                                    ppb[:, :], onesf[0:1, :], rden[0:1, :],
                                    start=True, stop=True), r=[onesf, rden], w=[ppb])
                                ctx.op("act", lambda e: e.copy(rb[:, :], ppb[:, :]),
                                       r=[ppb], w=[rb])
                                ctx.op("dve", lambda e, lo=lo, qt=qt, ppvb=ppvb:
                                       e.tensor_mul(otb[lo:lo + 64, qt * 512:(qt + 1) * 512],
                                                    ppvb[0:64, :], rb[:, :]),
                                       r=[ppvb, rb], w=[otb])
                            if hi % 2 == 1:
                                pglob = g * 2 + hi // 2
                                load_w_kc(ctx, wo_sl, W["L1_attn_b_WprojT"], pglob * 128, 128,
                                          0, 1024)
                                for t in range(NC):
                                    for half in range(2):
                                        ctx.op("pe", lambda e, t=t, half=half: e.matmul(
                                            py[:, :], otb[:, t * 128:(t + 1) * 128],
                                            wo_sl[:, half * 512:(half + 1) * 512],
                                            start=True, stop=True), r=[otb, wo_sl], w=[py])
                                        jg = b * NC + t
                                        ctx.op("dve", lambda e, jg=jg, half=half: e.tensor_add(
                                            xb[:, jg * D + half * 512:jg * D + (half + 1) * 512],
                                            xb[:, jg * D + half * 512:jg * D + (half + 1) * 512],
                                            py[:, :]), r=[py, xb], w=[xb])




def emit_l1_attn_a(ctx, c, W, xb):
    NC = S // 128
    with ctx.scope():
        g1c = ctx.sbuf("ta_g1c", [128, 8], F32)
        b1c = ctx.sbuf("ta_b1c", [128, 8], BF16)
        stats = ctx.sbuf("ta_stats", [128, NT * 4], F32)
        with ctx.scope():
            ptb = ctx.psum("ta_ptb", [128, 1024], BF16)
            pbc = ctx.psum("ta_pbc", [128, 1024], F32)
            grow = ctx.sbuf("ta_grow", [1, 4 * D], BF16)
            for i, nm in enumerate(["g1", "beta1", "g2", "beta2"]):
                ctx.dma("pool", grow[0:1, i * D:(i + 1) * D],
                        W["L1_attn_a_" + nm].unsqueeze(0), w=[grow])
            for cols, off in [(g1c, 0), (b1c, D)]:
                def tr(e, off=off):
                    last = None
                    for cc in range(8):
                        last = e.transpose(ptb[:, 2 * cc:2 * cc + 1],
                                           grow[0:1, off + cc * 128:off + (cc + 1) * 128],
                                           c["ident"][0:1, 0:1])
                    return last
                ctx.op("pe", tr, r=[grow, c["ident"]], w=[ptb])
                ctx.op("act", lambda e, cols=cols: e.copy(
                    cols[:, 0:8], ptb[:, :].rearrange("p (a b2) -> p a b2", b2=2)[:, 0:8, 0]),
                    r=[ptb], w=[cols])
        for b in range(2):
            with ctx.scope():
                zt2 = ctx.sbuf("ta_zt2", [128, NC * D], BF16)
                with ctx.scope():
                    ln = LNScratch(ctx, "ta")
                    t1s = ctx.sbuf("ta_t1s", [128, D], F32)
                    junk = ctx.psum("ta_junk", [128, 1024], F32)
                    for j in range(NC):
                        jg = b * NC + j
                        xa = xb[:, jg * D:(jg + 1) * D]
                        sc = stats[:, jg * 4:jg * 4 + 4]
                        ctx.op("dve", lambda e, xa=xa: e.tensor_reduce(
                            ln.sums[:, :], xa, axis=AX.X, op=ALU.add),
                            r=[xb], w=[ln.sums], late=True)
                        ctx.op("dve", lambda e, sc=sc: e.tensor_scalar(
                            sc[:, 0:1], ln.sums[:, :], 1.0 / D, None, op0=ALU.mult),
                            r=[ln.sums], w=[stats])
                        ctx.op("dve", lambda e, xa=xa, sc=sc: e.tensor_scalar(
                            ln.tctr[:, :], xa, sc[:, 0:1], None, op0=ALU.subtract),
                            r=[xb, stats], w=[ln.tctr])
                        ctx.op("act", lambda e: e.activation(junk[:, 0:D], ln.tctr[:, :],
                                                             AF.Square, accum_out=ln.ssq[:, :]),
                               r=[ln.tctr], w=[junk, ln.ssq], late=True)
                        ctx.op("act", lambda e: e.activation(ln.sd[:, :], ln.ssq[:, :], AF.Sqrt,
                                                             bias=c["epsc"][:, 0:1],
                                                             scale=1.0 / D),
                               r=[ln.ssq, c["epsc"]], w=[ln.sd])
                        ctx.op("dve", lambda e, sc=sc: e.reciprocal(sc[:, 1:2], ln.sd[:, :]),
                               r=[ln.sd], w=[stats], late=True)
                        ctx.op("dve", lambda e, sc=sc: e.tensor_scalar(
                            t1s[:, :], ln.tctr[:, :], sc[:, 1:2], None, op0=ALU.mult),
                            r=[ln.tctr, stats], w=[t1s])
                        ctx.op("dve", lambda e: e.tensor_reduce(
                            ln.sums[:, :], t1s[:, :], axis=AX.X, op=ALU.add),
                            r=[t1s], w=[ln.sums], late=True)
                        ctx.op("dve", lambda e, sc=sc: e.tensor_scalar(
                            sc[:, 2:3], ln.sums[:, :], 1.0 / D, None, op0=ALU.mult),
                            r=[ln.sums], w=[stats])
                        ctx.op("dve", lambda e, sc=sc: e.tensor_scalar(
                            ln.tctr[:, :], t1s[:, :], sc[:, 2:3], None, op0=ALU.subtract),
                            r=[t1s, stats], w=[ln.tctr])
                        ctx.op("act", lambda e: e.activation(junk[:, 0:D], ln.tctr[:, :],
                                                             AF.Square, accum_out=ln.ssq[:, :]),
                               r=[ln.tctr], w=[junk, ln.ssq], late=True)
                        ctx.op("act", lambda e: e.activation(ln.sd[:, :], ln.ssq[:, :], AF.Sqrt,
                                                             bias=c["epsc"][:, 0:1],
                                                             scale=1.0 / D),
                               r=[ln.ssq, c["epsc"]], w=[ln.sd])
                        ctx.op("dve", lambda e, sc=sc: e.reciprocal(sc[:, 3:4], ln.sd[:, :]),
                               r=[ln.sd], w=[stats], late=True)
                with ctx.scope():
                  ct = ctx.sbuf("ta_ct", [128, NC * 64], BF16)
                  st = ctx.sbuf("ta_st", [128, NC * 64], BF16)
                  bor = ctx.sbuf("ta_bor", [1, D], BF16)
                  emit_rope_tables(ctx, c, ct, st, "ilv")
                  ctx.dma("pool", bor[0:1, :], W["L1_attn_a_bo"].unsqueeze(0), w=[bor])
                  for pr in range(8):
                    with ctx.scope():
                        wsl = [ctx.sbuf(f"ta_w{i}", [128, KC * 128], BF16) for i in range(3)]
                        wtmp = ctx.sbuf("ta_wtmp", [128, KC * 128], BF16)
                        befr = ctx.sbuf("ta_bef", [1, 3 * 128], BF16)
                        bqr = ctx.sbuf("ta_bqr", [1, 3 * 128], BF16)
                        for i, nm in enumerate(["bq", "bk", "bv"]):
                            ctx.dma("pool", bqr[0:1, i * 128:(i + 1) * 128],
                                    W["L1_attn_a_" + nm][pr * 128:(pr + 1) * 128].unsqueeze(0),
                                    w=[bqr])
                        wo_sl = ctx.sbuf("ta_wo", [128, 1024], BF16)
                        qf = ctx.sbuf("ta_qf", [128, S], BF16)
                        kf = ctx.sbuf("ta_kf", [128, S], BF16)
                        vau = ctx.sbuf("ta_vau", [128, NC * 130], BF16)
                        otp = ctx.sbuf("ta_otp", [128, S], BF16)
                        ztl = ctx.sbuf("ta_ztl", [128, D], BF16)
                        zsc = ctx.sbuf("ta_zsc", [128, D], F32)
                        tt1 = ctx.sbuf("ta_tt1", [128, 128], BF16)
                        tt2 = ctx.sbuf("ta_tt2", [128, 128], BF16)
                        qro = ctx.sbuf("ta_qro", [128, 128], BF16)
                        nmx = ctx.sbuf("ta_nmx", [128, 4], F32)
                        nmxb = ctx.sbuf("ta_nmxb", [128, 4], BF16)
                        nsq = ctx.sbuf("ta_nsq", [128, 4], F32)
                        mrow = ctx.sbuf("ta_mrow", [4, 128], BF16)
                        mcol = ctx.sbuf("ta_mcol", [4, 1], BF16)
                        crow = ctx.sbuf("ta_crow", [1, 8], F32)
                        negc = ctx.sbuf("ta_negc", [128, 2], F32)
                        exps = ctx.sbuf("ta_exps", [128, 512], BF16)
                        rden = ctx.sbuf("ta_rden", [1, 512], BF16)
                        rb = ctx.sbuf("ta_rb", [64, 512], BF16)
                        crowb = ctx.sbuf("ta_crowb", [1, 2], BF16)
                        ctx.op("dve", lambda e: e.memset(vau[:, :], 1.0), w=[vau])
                        ctx.op("dve", lambda e: e.memset(nmx[:, :], 0.0), w=[nmx])
                        with ctx.scope():
                            pq = ctx.psum("ta_pq", [128, 384], F32)
                            ptb = ctx.psum("ta_ptb1", [128, 1024], BF16)
                            for wi, nm in enumerate(["Wq", "Wk", "Wv"]):
                                load_w_kc(ctx, wtmp, W["L1_attn_a_" + nm], 0, D, pr * 128, 128)
                                def bmm2(e, wi=wi):
                                    last = None
                                    for k in range(KC):
                                        last = e.matmul(pq[0:1, 0:128],
                                                        b1c[:, k:k + 1],
                                                        wtmp[:, k * 128:(k + 1) * 128],
                                                        start=(k == 0), stop=(k == KC - 1))
                                    return last
                                ctx.op("pe", bmm2, r=[b1c, wtmp], w=[pq])
                                ctx.op("dve", lambda e, wi=wi: e.tensor_add(
                                    befr[0:1, wi * 128:(wi + 1) * 128], pq[0:1, 0:128],
                                    bqr[0:1, wi * 128:(wi + 1) * 128]),
                                    r=[pq, bqr], w=[befr])
                                for k in range(KC):
                                    ctx.op("dve", lambda e, wi=wi, k=k: e.tensor_scalar(
                                        wsl[wi][:, k * 128:(k + 1) * 128],
                                        wtmp[:, k * 128:(k + 1) * 128], g1c[:, k:k + 1],
                                        None, op0=ALU.mult), r=[wtmp, g1c], w=[wsl[wi]])
                            load_w_kc(ctx, wo_sl, W["L1_attn_a_Wo"], pr * 128, 128, 0, 1024)
                            for j in range(NC):
                                jg = b * NC + j
                                sc = stats[:, jg * 4:jg * 4 + 4]
                                ctx.op("dve", lambda e, jg=jg, sc=sc: e.tensor_scalar(
                                    zsc[:, :], xb[:, jg * D:(jg + 1) * D], sc[:, 0:1],
                                    sc[:, 1:2], op0=ALU.subtract, op1=ALU.mult),
                                    r=[xb, stats], w=[zsc])
                                ctx.op("dve", lambda e, sc=sc: e.tensor_scalar(
                                    ztl[:, :], zsc[:, :], sc[:, 2:3], sc[:, 3:4],
                                    op0=ALU.subtract, op1=ALU.mult),
                                    r=[zsc, stats], w=[ztl])
                                def ztr(e):
                                    last = None
                                    for cc in range(KC):
                                        last = e.transpose(ptb[:, cc * 128:(cc + 1) * 128],
                                                           ztl[:, cc * 128:(cc + 1) * 128],
                                                           c["ident"][:, :])
                                    return last
                                ctx.op("pe", ztr, r=[ztl, c["ident"]], w=[ptb])
                                ctx.op("act", lambda e: e.copy(ztl[:, :], ptb[:, 0:1024]),
                                       r=[ptb], w=[ztl])
                                def pmm(e, j=j):
                                    last = None
                                    for wi in range(3):
                                        for k in range(KC):
                                            e.matmul(pq[:, wi * 128:(wi + 1) * 128],
                                                     ztl[:, k * 128:(k + 1) * 128],
                                                     wsl[wi][:, k * 128:(k + 1) * 128],
                                                     start=(k == 0), stop=False)
                                        last = e.matmul(pq[:, wi * 128:(wi + 1) * 128],
                                                        c["ones"][0:1, :],
                                                        befr[0:1, wi * 128:(wi + 1) * 128],
                                                        start=False, stop=True)
                                    return last
                                ctx.op("pe", pmm, r=[ztl, wsl[0], wsl[1], wsl[2], c["ones"],
                                                     befr], w=[pq])
                                ctx.op("act", lambda e, j=j: e.copy(
                                    vau[:, :].rearrange("p (t h dd) -> p t h dd", h=2, dd=65)
                                    [:, j, :, 0:64],
                                    pq[:, 256:384].rearrange("p (h dd) -> p h dd", dd=64)),
                                    r=[pq], w=[vau])
                                ctx.op("act", lambda e: e.activation(zsc[:, 0:256],
                                                                     pq[:, 0:256], AF.Square),
                                       r=[pq], w=[zsc])
                                ctx.op("dve", lambda e: e.tensor_reduce(
                                    nsq[:, 0:4],
                                    zsc[:, 0:256].rearrange("p (h dd) -> p h dd", dd=64),
                                    axis=AX.X, op=ALU.add), r=[zsc], w=[nsq], late=True)
                                ctx.op("dve", lambda e: e.tensor_max(nmx[:, :], nmx[:, :],
                                                                     nsq[:, :]),
                                       r=[nsq, nmx], w=[nmx])
                                for src0, dstf in [(0, qf), (128, kf)]:
                                    ctx.op("dve", lambda e, src0=src0, j=j: e.tensor_mul(
                                        tt1[:, :].rearrange("p (h dd) -> p h dd", dd=64),
                                        pq[:, src0:src0 + 128].rearrange(
                                            "p (h dd) -> p h dd", dd=64),
                                        ct[:, j * 64:(j + 1) * 64].unsqueeze(1)
                                        .broadcast_to((128, 2, 64))), r=[pq, ct], w=[tt1])
                                    ctx.op("dve", lambda e, src0=src0, j=j: e.tensor_mul(
                                        tt2[:, :].rearrange("p (h dd c2) -> p h dd c2",
                                                            dd=32, c2=2),
                                        pq[:, src0:src0 + 128].rearrange(
                                            "p (h dd c2) -> p h dd c2", dd=32, c2=2)
                                        [:, :, :, ::-1],
                                        st[:, j * 64:(j + 1) * 64]
                                        .rearrange("p (dd c2) -> p dd c2", c2=2)
                                        .unsqueeze(1).broadcast_to((128, 2, 32, 2))),
                                        r=[pq, st], w=[tt2])
                                    ctx.op("dve", lambda e: e.tensor_add(qro[:, :], tt1[:, :],
                                                                         tt2[:, :]),
                                           r=[tt1, tt2], w=[qro])
                                    ctx.op("pe", lambda e: e.transpose(
                                        ptb[:, 0:128], qro[:, :], c["ident"][:, :]),
                                        r=[qro, c["ident"]], w=[ptb])
                                    ctx.op("act", lambda e, j=j, dstf=dstf: e.copy(
                                        dstf[:, j * 128:(j + 1) * 128], ptb[:, 0:128]),
                                        r=[ptb], w=[dstf])
                            ctx.op("dve", lambda e: e.tensor_copy(nmxb[:, :], nmx[:, :]),
                                   r=[nmx], w=[nmxb])
                            ctx.op("pe", lambda e: e.transpose(ptb[0:4, 0:128], nmxb[:, :],
                                                               c["ident"][:, :]),
                                   r=[nmxb, c["ident"]], w=[ptb])
                            ctx.op("act", lambda e: e.copy(mrow[0:4, 0:128], ptb[0:4, 0:128]),
                                   r=[ptb], w=[mrow])
                            ctx.op("dve", lambda e: e.tensor_reduce(
                                nsq[0:4, 0:1], mrow[0:4, 0:128], axis=AX.X, op=ALU.max),
                                r=[mrow], w=[nsq], late=True)
                            ctx.op("dve", lambda e: e.tensor_copy(mcol[0:4, 0:1],
                                                                  nsq[0:4, 0:1]),
                                   r=[nsq], w=[mcol])
                            ctx.op("pe", lambda e: e.transpose(ptb[0:1, 0:4], mcol[0:4, 0:1],
                                                               c["ident"][0:4, 0:4]),
                                   r=[mcol, c["ident"]], w=[ptb])
                            ctx.op("act", lambda e: e.copy(crow[0:1, 0:4], ptb[0:1, 0:4]),
                                   r=[ptb], w=[crow])
                            ctx.op("dve", lambda e: e.tensor_mul(crow[0:1, 4:6],
                                                                 crow[0:1, 0:2],
                                                                 crow[0:1, 2:4]),
                                   r=[crow], w=[crow])
                            ctx.op("act", lambda e: e.activation(crow[0:1, 6:8],
                                                                 crow[0:1, 4:6], AF.Sqrt),
                                   r=[crow], w=[crow], late=True)
                            ctx.op("dve", lambda e: e.tensor_scalar(
                                crow[0:1, 0:2], crow[0:1, 6:8], -0.125, None, op0=ALU.mult),
                                r=[crow], w=[crow])
                            ctx.op("dve", lambda e: e.tensor_copy(crowb[0:1, :],
                                                                  crow[0:1, 0:2]),
                                   r=[crow], w=[crowb])
                            def bcc(e):
                                e.matmul(pq[:, 0:1], c["ones"][0:1, :], crowb[0:1, 0:1],
                                         start=True, stop=True)
                                return e.matmul(pq[:, 1:2], c["ones"][0:1, :],
                                                crowb[0:1, 1:2], start=True, stop=True)
                            ctx.op("pe", bcc, r=[c["ones"], crowb], w=[pq])
                            ctx.op("act", lambda e: e.copy(negc[:, 0:2], pq[:, 0:2]),
                                   r=[pq], w=[negc])
                        with ctx.scope():
                            pst = [ctx.psum("ta_pst0", [128, 512], F32),
                                   ctx.psum("ta_pst1", [128, 512], F32)]
                            ppv = [ctx.psum("ta_ppv0", [65, 512], F32),
                                   ctx.psum("ta_ppv1", [65, 512], F32)]
                            ppb = ctx.psum("ta_ppb", [64, 512], F32)
                            for hi in range(2):
                                lo = hi * 64
                                for qt in range(4):
                                    ppvb = ppv[qt % 2]
                                    for kt in range(NC):
                                        pss = pst[kt % 2]
                                        ctx.op("pe", lambda e, kt=kt, qt=qt, pss=pss,
                                               lo=lo: e.matmul(
                                            pss[:, :], kf[lo:lo + 64, kt * 128:(kt + 1) * 128],
                                            qf[lo:lo + 64, qt * 512:(qt + 1) * 512],
                                            start=True, stop=True, tile_position=(lo, 0)),
                                            r=[kf, qf], w=[pss])
                                        ctx.op("act", lambda e, pss=pss, hi=hi: e.activation(
                                            exps[:, :], pss[:, :], AF.Exp, scale=0.125,
                                            bias=negc[:, hi:hi + 1]),
                                            r=[pss, negc], w=[exps])
                                        ctx.op("pe", lambda e, kt=kt, hi=hi, ppvb=ppvb:
                                               e.matmul(
                                            ppvb[:, :],
                                            vau[:, kt * 130 + hi * 65:kt * 130 + hi * 65 + 65],
                                            exps[:, :],
                                            start=(kt == 0), stop=(kt == NC - 1)),
                                            r=[vau, exps], w=[ppvb])
                                    ctx.op("dve", lambda e, ppvb=ppvb: e.reciprocal(
                                        zsc[0:1, 0:512], ppvb[64:65, :]), r=[ppvb], w=[zsc],
                                        late=True)
                                    ctx.op("dve", lambda e: e.tensor_copy(
                                        rden[0:1, :], zsc[0:1, 0:512]), r=[zsc], w=[rden])
                                    ctx.op("pe", lambda e: e.matmul(
                                        ppb[:, :], c["ones"][0:1, 0:64], rden[0:1, :],
                                        start=True, stop=True), r=[c["ones"], rden], w=[ppb])
                                    ctx.op("act", lambda e: e.copy(rb[:, :], ppb[:, :]),
                                           r=[ppb], w=[rb])
                                    ctx.op("dve", lambda e, lo=lo, qt=qt, ppvb=ppvb:
                                           e.tensor_mul(otp[lo:lo + 64,
                                                            qt * 512:(qt + 1) * 512],
                                                        ppvb[0:64, :], rb[:, :]),
                                           r=[ppvb, rb], w=[otp])
                        with ctx.scope():
                            py = [ctx.psum("ta_py0", [128, 512], F32),
                                  ctx.psum("ta_py1", [128, 512], F32)]
                            for t in range(NC):
                                jg = b * NC + t
                                sc = stats[:, jg * 4:jg * 4 + 4]
                                for half in range(2):
                                    pp = py[half]
                                    def ymm(e, t=t, half=half, pp=pp, pr=pr):
                                        last = e.matmul(pp[:, :], otp[:, t * 128:(t + 1) * 128],
                                                        wo_sl[:, half * 512:(half + 1) * 512],
                                                        start=True, stop=(pr != 0))
                                        if pr == 0:
                                            last = e.matmul(pp[:, :], c["ones"][0:1, :],
                                                            bor[0:1, half * 512:
                                                                (half + 1) * 512],
                                                            start=False, stop=True)
                                        return last
                                    ctx.op("pe", ymm, r=[otp, wo_sl, c["ones"], bor], w=[pp])
                                    ca = zt2[:, t * D + half * 512:t * D + (half + 1) * 512]
                                    if pr == 0:
                                        ctx.op("dve", lambda e, jg=jg, half=half, sc=sc:
                                               e.tensor_scalar(
                                            zsc[:, half * 512:(half + 1) * 512],
                                            xb[:, jg * D + half * 512:
                                               jg * D + (half + 1) * 512],
                                            sc[:, 0:1], None, op0=ALU.subtract),
                                            r=[xb, stats], w=[zsc])
                                        ctx.op("dve", lambda e, ca=ca, pp=pp, sc=sc, half=half:
                                               e.scalar_tensor_tensor(
                                            ca, zsc[:, half * 512:(half + 1) * 512],
                                            sc[:, 1:2], pp[:, :],
                                            op0=ALU.mult, op1=ALU.add),
                                            r=[zsc, stats, pp], w=[zt2])
                                    else:
                                        ctx.op("dve", lambda e, ca=ca, pp=pp: e.tensor_add(
                                            ca, ca, pp[:, :]), r=[pp, zt2], w=[zt2])
                with ctx.scope():
                    ln = LNScratch(ctx, "ta2")
                    g2b = ctx.sbuf("ta_g2b", [128, D], BF16)
                    b2b = ctx.sbuf("ta_b2b", [128, D], BF16)
                    grow2 = ctx.sbuf("ta_grow2", [1, 2 * D], BF16)
                    junk = ctx.psum("ta_junk2", [128, 1024], F32)
                    ptb = ctx.psum("ta_ptb2", [128, 1024], BF16)
                    for i, nm in enumerate(["g2", "beta2"]):
                        ctx.dma("pool", grow2[0:1, i * D:(i + 1) * D],
                                W["L1_attn_a_" + nm].unsqueeze(0), w=[grow2])
                    for dst, off in [(g2b, 0), (b2b, D)]:
                        def bc(e, off=off, dst=dst):
                            last = None
                            for hh in range(2):
                                last = e.matmul(junk[:, hh * 512:(hh + 1) * 512],
                                                c["ones"][0:1, :],
                                                grow2[0:1, off + hh * 512:off + (hh + 1) * 512],
                                                start=True, stop=True)
                            return last
                        ctx.op("pe", bc, r=[grow2, c["ones"]], w=[junk])
                        ctx.op("act", lambda e, dst=dst: e.copy(dst[:, :], junk[:, 0:1024]),
                               r=[junk], w=[dst])
                    for j in range(NC):
                        jg = b * NC + j
                        ctx.op("dve", lambda e, j=j, jg=jg: e.tensor_add(
                            xb[:, jg * D:(jg + 1) * D], xb[:, jg * D:(jg + 1) * D],
                            zt2[:, j * D:(j + 1) * D]), r=[zt2, xb], w=[xb])
                        emit_ln_tile(ctx, c, ln, zt2, j * D, None, None, None, ptb, junk,
                                     gamma_b=g2b, beta_b=b2b, zt_inplace=(zt2, j * D))
                with ctx.scope():
                    w1b = ctx.sbuf("ta_fw1", [128, KC * 1024], BF16)
                    w2b = ctx.sbuf("ta_fw2", [128, KC * 1024], BF16)
                    hT = ctx.sbuf("ta_fh", [128, KC * 512], BF16)
                    b1r = ctx.sbuf("ta_fb1r", [1, 1024], BF16)
                    b1cq = ctx.sbuf("ta_fb1c", [128, 8], F32)
                    b2r = ctx.sbuf("ta_fb2r", [1, 1024], BF16)
                    p_h = [ctx.psum("ta_fph0", [128, 512], F32),
                           ctx.psum("ta_fph1", [128, 512], F32)]
                    p_y = [ctx.psum("ta_fpy0", [128, 1024], F32),
                           ctx.psum("ta_fpy1", [128, 1024], F32)]
                    ptb = ctx.psum("ta_ptb3", [128, 128], BF16)
                    ctx.dma("pool", b2r[0:1, :], W["L1_attn_a_b2"].unsqueeze(0), w=[b2r])
                    for q in range(4):
                        load_w_kc(ctx, w1b, W["L1_attn_a_W1"], 0, D, q * 1024, 1024)
                        load_w_kc(ctx, w2b, W["L1_attn_a_W2"], q * 1024, 1024, 0, 1024)
                        ctx.dma("pool", b1r[0:1, :],
                                W["L1_attn_a_b1"][q * 1024:(q + 1) * 1024].unsqueeze(0),
                                w=[b1r])
                        def trb(e):
                            last = None
                            for cc in range(8):
                                last = e.transpose(ptb[:, 2 * cc:2 * cc + 1],
                                                   b1r[0:1, cc * 128:(cc + 1) * 128],
                                                   c["ident"][0:1, 0:1])
                            return last
                        ctx.op("pe", trb, r=[b1r, c["ident"]], w=[ptb])
                        ctx.op("act", lambda e: e.copy(
                            b1cq[:, 0:8],
                            ptb[:, 0:16].rearrange("p (a c2) -> p a c2", c2=2)[:, 0:8, 0]),
                            r=[ptb], w=[b1cq])
                        for t in range(4):
                            for m in range(KC):
                                ph = p_h[m % 2]
                                def hmm(e, m=m, t=t, ph=ph):
                                    last = None
                                    for k in range(KC):
                                        last = e.matmul(
                                            ph[:, :],
                                            w1b[:, k * 1024 + m * 128:k * 1024 + (m + 1) * 128],
                                            zt2[:, :].rearrange(
                                                "p (tt kc w) -> p tt kc w", kc=KC, w=128)
                                            [:, 4 * t:4 * t + 4, k, :],
                                            start=(k == 0), stop=(k == KC - 1))
                                    return last
                                ctx.op("pe", hmm, r=[w1b, zt2], w=[ph])
                                ctx.op("act", lambda e, m=m, ph=ph: e.activation(
                                    hT[:, m * 512:(m + 1) * 512], ph[:, :], AF.Relu,
                                    bias=b1cq[:, m:m + 1]), r=[ph, b1cq], w=[hT])
                            for sub in range(4):
                                pyy = p_y[sub % 2]
                                sc2 = sub * 128
                                def ymm2(e, sc2=sc2, pyy=pyy, q=q):
                                    last = None
                                    for k in range(KC):
                                        for half in range(2):
                                            last = e.matmul(
                                                pyy[:, half * 512:(half + 1) * 512],
                                                hT[:, k * 512 + sc2:k * 512 + sc2 + 128],
                                                w2b[:, k * 1024 + half * 512:
                                                    k * 1024 + (half + 1) * 512],
                                                start=(k == 0),
                                                stop=(k == KC - 1 and q != 0))
                                    if q == 0:
                                        for half in range(2):
                                            last = e.matmul(
                                                pyy[:, half * 512:(half + 1) * 512],
                                                c["ones"][0:1, :],
                                                b2r[0:1, half * 512:(half + 1) * 512],
                                                start=False, stop=True)
                                    return last
                                ctx.op("pe", ymm2, r=[hT, w2b, c["ones"], b2r], w=[pyy])
                                jg = b * NC + t * 4 + sub
                                ctx.op("dve", lambda e, pyy=pyy, jg=jg: e.tensor_add(
                                    xb[:, jg * D:(jg + 1) * D], xb[:, jg * D:(jg + 1) * D],
                                    pyy[:, :]), r=[pyy, xb], w=[xb])



WSPECS = {
    "l0ffn": ["L0_ffn_rwsg", "L0_ffn_w1", "L0_ffn_w2", "L0_ffn_shared_gate_up",
              "L0_ffn_shared_down"],
    "l1ffn": ["L1_ffn_Wr", "L1_ffn_W1", "L1_ffn_W2"],
    "l0ab": ["L0_attn_b_WqT", "L0_attn_b_WkT", "L0_attn_b_WvT", "L0_attn_b_WoT",
             "L0_attn_b_bq", "L0_attn_b_bk", "L0_attn_b_bv", "L0_attn_b_bo"],
    "l0aa": ["L0_attn_a_w_qkv", "L0_attn_a_proj", "L0_attn_a_w_out", "L0_attn_a_b_out"],
    "l1ab": ["L1_attn_b_WqT", "L1_attn_b_WkT", "L1_attn_b_WvT", "L1_attn_b_WprojT"],
    "l1aa": ["L1_attn_a_Wq", "L1_attn_a_Wk", "L1_attn_a_Wv", "L1_attn_a_Wo",
             "L1_attn_a_W1", "L1_attn_a_W2", "L1_attn_a_bq", "L1_attn_a_bk",
             "L1_attn_a_bv", "L1_attn_a_bo", "L1_attn_a_b1", "L1_attn_a_b2",
             "L1_attn_a_g1", "L1_attn_a_beta1", "L1_attn_a_g2", "L1_attn_a_beta2"],
}

EMITTERS = {
    "l0ffn": emit_l0_ffn,
    "l1ffn": emit_l1_ffn,
    "l0ab": emit_l0_attn_b,
    "l0aa": emit_l0_attn_a,
    "l1ab": emit_l1_attn_b,
    "l1aa": emit_l1_attn_a,
}

WSHAPES = {
    "L0_attn_a_w_qkv": (D, 3072), "L0_attn_a_proj": (128, 64),
    "L0_attn_a_w_out": (D, D), "L0_attn_a_b_out": (D,),
    "L0_ffn_rwsg": (D, 9), "L0_ffn_w1": (8, D, D), "L0_ffn_w2": (8, D, D),
    "L0_ffn_shared_gate_up": (D, 2048), "L0_ffn_shared_down": (D, D),
    "L0_ffn_shared_gate_w": (D, 1),
    "L0_attn_b_WqT": (D, D), "L0_attn_b_WkT": (D, D), "L0_attn_b_WvT": (D, D),
    "L0_attn_b_WoT": (D, D), "L0_attn_b_bq": (D,), "L0_attn_b_bk": (D,),
    "L0_attn_b_bv": (D,), "L0_attn_b_bo": (D,),
    "L1_attn_a_Wq": (D, D), "L1_attn_a_Wk": (D, D), "L1_attn_a_Wv": (D, D),
    "L1_attn_a_Wo": (D, D), "L1_attn_a_W1": (D, 4096), "L1_attn_a_W2": (4096, D),
    "L1_attn_a_bq": (D,), "L1_attn_a_bk": (D,), "L1_attn_a_bv": (D,),
    "L1_attn_a_bo": (D,), "L1_attn_a_b1": (4096,), "L1_attn_a_b2": (D,),
    "L1_attn_a_g1": (D,), "L1_attn_a_beta1": (D,), "L1_attn_a_g2": (D,),
    "L1_attn_a_beta2": (D,),
    "L1_ffn_Wr": (D, 8), "L1_ffn_W1": (8, D, 4096), "L1_ffn_W2": (8, 4096, D),
    "L1_attn_b_WqT": (D, D), "L1_attn_b_WkT": (D, 256), "L1_attn_b_WvT": (D, 256),
    "L1_attn_b_WprojT": (D, D),
}


def host_weights(inputs):
    w = {}
    for k, v in inputs.items():
        if k == "L0_attn_a_x":
            continue
        w[k] = np.ascontiguousarray(np.asarray(v, dtype=np.float32))
    w["L0_ffn_rwsg"] = np.ascontiguousarray(
        np.concatenate([w["L0_ffn_router_w"], w["L0_ffn_shared_gate_w"]], axis=1))
    w["L0_attn_b_WqT"] = np.ascontiguousarray(w.pop("L0_attn_b_Wq").T)
    w["L0_attn_b_WkT"] = np.ascontiguousarray(w.pop("L0_attn_b_Wk").T)
    w["L0_attn_b_WvT"] = np.ascontiguousarray(w.pop("L0_attn_b_Wv").T)
    w["L0_attn_b_WoT"] = np.ascontiguousarray(w.pop("L0_attn_b_Wo").T)
    w["L1_attn_b_WqT"] = np.ascontiguousarray(w.pop("L1_attn_b_Wq").T)
    kvT = np.ascontiguousarray(w.pop("L1_attn_b_Wkv").T)
    w["L1_attn_b_WkT"] = np.ascontiguousarray(kvT[:, :256])
    w["L1_attn_b_WvT"] = np.ascontiguousarray(kvT[:, 256:])
    w["L1_attn_b_WprojT"] = np.ascontiguousarray(w.pop("L1_attn_b_Wproj").T)
    return w


def build_program(sublayers):
    nc = bass.Bass("TRN2", target_bir_lowering=False)
    x_in = nc.dram_tensor("x", [TTOT, D], F32, kind="ExternalInput").ap()
    x_out = nc.dram_tensor("xo", [TTOT, D], F32, kind="ExternalOutput").ap()
    wnames = sorted({n for s_ in sublayers for n in WSPECS[s_]})
    W = {n: nc.dram_tensor(n, list(WSHAPES[n]), F32, kind="ExternalInput").ap() for n in wnames}
    with ExitStack() as stack:
        ctx = Ctx(nc, stack)
        c = {}
        xb = ctx.sbuf("x_res", [128, NT * D], F32)
        emit_constants(ctx, c)
        emit_load_x(ctx, x_in, xb)
        for s_ in sublayers:
            EMITTERS[s_](ctx, c, W, xb)
        emit_store_x(ctx, xb, x_out)
        ctx.finish()
    return nc, wnames


ALL_SUBLAYERS = ["l0aa", "l0ffn", "l0ab", "l0ffn", "l1aa", "l1ffn", "l1ab", "l1ffn"]


def run_chain(inputs, chains):
    w = host_weights(inputs)
    x = np.ascontiguousarray(np.asarray(inputs["L0_attn_a_x"], np.float32).reshape(16 * S, D))
    shards = [np.ascontiguousarray(x[i * TTOT:(i + 1) * TTOT]) for i in range(NCORES)]
    progs = {}
    for ch in chains:
        key = tuple(ch)
        if key not in progs:
            progs[key] = build_program(ch)
    for ch in chains:
        nc, wnames = progs[tuple(ch)]
        in_maps = [dict({"x": shards[i]}, **{n: w[n] for n in wnames}) for i in range(NCORES)]
        res = run_bass_kernel_spmd(nc, in_maps, core_ids=list(range(NCORES)))
        shards = [np.asarray(res.results[i]["xo"]) for i in range(NCORES)]
    return np.concatenate(shards, axis=0).reshape(16, S, D)


def kernel(**inputs):
    return run_chain(inputs, [ALL_SUBLAYERS])


# version 02
# speedup vs baseline: 1.1253x; 1.1253x over previous
import numpy as np
from contextlib import ExitStack, contextmanager
import concourse.bass as bass
import concourse.mybir as mybir
from concourse.bass_utils import run_bass_kernel_spmd

F32 = mybir.dt.float32
BF16 = mybir.dt.bfloat16
AF = mybir.ActivationFunctionType
ALU = mybir.AluOpType
AX = mybir.AxisListType

NCORES = 8
S = 2048
TTOT = 2 * S
NT = TTOT // 128
D = 1024
KC = D // 128



class Buf:
    __slots__ = ("h", "writer", "readers", "name")

    def __init__(self, h, name=""):
        self.h = h
        self.name = name or getattr(h, "name", "")
        self.writer = None
        self.readers = []

    def __getitem__(self, key):
        return self.h[key]


class Stream:
    def __init__(self, name, eng, sem):
        self.name = name
        self.eng = eng
        self.sem = sem
        self.seq = 0
        self.insts = []
        self.incs = []
        self.tick = 0
        self.waited = {}

    def ensure_inc(self, seq):
        for s, t in self.incs:
            if s >= seq:
                return t
        inst = self.insts[seq]
        if inst is None:
            raise RuntimeError(f"stream {self.name} op {seq} has no instruction")
        self.tick += 1
        inst.then_inc(self.sem, 1)
        self.incs.append((seq, self.tick))
        return self.tick


class Ctx:
    def __init__(self, nc, stack):
        self.nc = nc
        self.stack = stack
        self.streams = {}
        self.uid = 0
        for name, eng in [("pe", nc.tensor), ("act", nc.scalar), ("dve", nc.vector),
                          ("pool", nc.gpsimd), ("sp", nc.sync)]:
            sem = stack.enter_context(nc.semaphore("sem_" + name))
            self.streams[name] = Stream(name, eng, sem)

    def _nm(self, name):
        self.uid += 1
        return f"{name}_{self.uid}"

    def sbuf(self, name, shape, dtype):
        return Buf(self.stack.enter_context(self.nc.sbuf_tensor(self._nm(name), shape, dtype)), name)

    def psum(self, name, shape, dtype):
        return Buf(self.stack.enter_context(self.nc.psum_tensor(self._nm(name), shape, dtype)), name)

    @contextmanager
    def scope(self):
        outer = self.stack
        with ExitStack() as inner:
            self.stack = inner
            try:
                yield
            finally:
                self.stack = outer
                self.barrier()

    def barrier(self):
        ticks = []
        for s in self.streams.values():
            if s.seq > 0:
                ticks.append((s, s.ensure_inc(s.seq - 1)))
        for s2 in self.streams.values():
            for s, t in ticks:
                if s is not s2 and s2.waited.get(s.name, -1) < t:
                    s2.eng.wait_ge(s.sem, t)
                    s2.waited[s.name] = t

    def _wait_for(self, st, producer_stream, producer_seq, late=False):
        if producer_stream is st:
            gap = 2 if late else 1
            if producer_seq >= st.seq - gap:
                t = producer_stream.ensure_inc(producer_seq)
                st.eng.wait_ge(st.sem, t)
            return
        t = producer_stream.ensure_inc(producer_seq)
        if st.waited.get(producer_stream.name, -1) < t:
            st.eng.wait_ge(producer_stream.sem, t)
            st.waited[producer_stream.name] = t

    def _pre(self, st, r, w):
        for b in r:
            if b.writer is not None:
                self._wait_for(st, *b.writer)
        for b in w:
            if b.writer is not None:
                self._wait_for(st, *b.writer)
            for (s2, q2) in b.readers:
                self._wait_for(st, s2, q2) if s2 is not st else None

    def _post(self, st, seq, r, w, late):
        st.seq += 1
        for b in r:
            b.readers.append((st, seq))
        for b in w:
            b.writer = (st, seq, late)
            b.readers = []

    def op(self, stream, emit, r=(), w=(), late=False):
        st = self.streams[stream]
        self._pre(st, r, w)
        inst = emit(st.eng)
        if isinstance(inst, (list, tuple)):
            inst = inst[-1]
        seq = st.seq
        st.insts.append(inst)
        self._post(st, seq, r, w, late)
        return seq

    def dma(self, qname, out_ap, in_ap, r=(), w=(), **kw):
        st = self.streams[qname]
        self._pre(st, r, w)
        inst = st.eng.dma_start(out=out_ap, in_=in_ap, **kw)
        seq = st.seq
        st.insts.append(inst)
        st.tick += 16
        inst.then_inc(st.sem, 16)
        st.incs.append((seq, st.tick))
        self._post(st, seq, r, w, False)
        return seq

    def finish(self):
        self.barrier()



def emit_constants(ctx, c):
    c["ident"] = ctx.sbuf("ident", [128, 128], BF16)
    c["ones"] = ctx.sbuf("ones_row", [1, 128], BF16)
    c["epsc"] = ctx.sbuf("epsc", [128, 4], F32)
    with ctx.scope():
        identf = ctx.sbuf("identf", [128, 128], F32)
        ctx.op("pool", lambda e: e.iota(identf[:, :], [[1, 128]], channel_multiplier=-1,
                                        allow_small_or_imprecise_dtypes=True), w=[identf])
        ctx.op("dve", lambda e: e.tensor_scalar(c["ident"][:, :], identf[:, :], 0.0, None,
                                                op0=ALU.is_equal), r=[identf], w=[c["ident"]])
    ctx.op("dve", lambda e: e.memset(c["ones"][:, :], 1.0), w=[c["ones"]])
    for i, v in enumerate([1e-5, 1e-6, 1e-8, 0.0]):
        ctx.op("dve", lambda e, i=i, v=v: e.memset(c["epsc"][:, i:i + 1], v), w=[c["epsc"]])


def emit_tri_mask(ctx, c, dst):
    with ctx.scope():
        tif = ctx.sbuf("trif", [128, 128], F32)
        ctx.op("pool", lambda e: e.iota(tif[:, :], [[1, 128]], channel_multiplier=-1,
                                        allow_small_or_imprecise_dtypes=True), w=[tif])
        ctx.op("dve", lambda e: e.tensor_scalar(dst[:, :], tif[:, :], 0.0, None,
                                                op0=ALU.is_ge), r=[tif], w=[dst])


def emit_band_mask(ctx, c, dst):
    with ctx.scope():
        tif = ctx.sbuf("bandf", [128, 128], F32)
        tif2 = ctx.sbuf("bandf2", [128, 128], F32)
        ctx.op("pool", lambda e: e.iota(tif[:, :], [[1, 128]], channel_multiplier=-1,
                                        allow_small_or_imprecise_dtypes=True), w=[tif])
        ctx.op("dve", lambda e: e.tensor_scalar(tif2[:, :], tif[:, :], 0.0, None,
                                                op0=ALU.is_ge), r=[tif], w=[tif2])
        ctx.op("dve", lambda e: e.tensor_scalar(tif[:, :], tif[:, :], 6.0, None,
                                                op0=ALU.is_le), r=[tif], w=[tif])
        ctx.op("dve", lambda e: e.tensor_mul(dst[:, :], tif[:, :], tif2[:, :]),
               r=[tif, tif2], w=[dst])


def emit_bias_cols(ctx, c, rows_bf, cols_f32, n, ptb):
    def tr(e):
        last = None
        for cc in range(n):
            last = e.transpose(ptb[:, cc:cc + 1], rows_bf[0:1, cc * 128:(cc + 1) * 128],
                               c["ident"][:, :])
        return last
    ctx.op("pe", tr, r=[rows_bf, c["ident"]], w=[ptb])
    ctx.op("act", lambda e: e.copy(cols_f32[:, 0:n], ptb[:, 0:n]), r=[ptb], w=[cols_f32])


def emit_load_x(ctx, x_hbm, xb):
    src = x_hbm.rearrange("(t p) f -> p t f", p=128)
    for i in range(4):
        t0 = i * (NT // 4)
        ctx.dma("sp", xb[:, t0 * D:(t0 + NT // 4) * D], src[:, t0:t0 + NT // 4, :], w=[xb])


def emit_store_x(ctx, xb, xo_hbm):
    dst = xo_hbm.rearrange("(t p) f -> p t f", p=128)
    for i in range(4):
        t0 = i * (NT // 4)
        ctx.dma("sp", dst[:, t0:t0 + NT // 4, :], xb[:, t0 * D:(t0 + NT // 4) * D], r=[xb])


class LNScratch:
    def __init__(self, ctx, tag):
        self.sums = ctx.sbuf(f"lns_{tag}", [128, 1], F32)
        self.mu = ctx.sbuf(f"lnm_{tag}", [128, 1], F32)
        self.ssq = ctx.sbuf(f"lnq_{tag}", [128, 1], F32)
        self.sd = ctx.sbuf(f"lnd_{tag}", [128, 1], F32)
        self.rstd = ctx.sbuf(f"lnr_{tag}", [128, 1], F32)
        self.tctr = ctx.sbuf(f"lnt_{tag}", [128, D], F32)
        self.z = ctx.sbuf(f"lnz_{tag}", [128, D], BF16)


def emit_ln_tile(ctx, c, ln, src_buf, src_col0, zT, zT_tcol, GT, ptb, junk_ps,
                 gamma_b=None, beta_b=None, z_out=None, zt_inplace=None):
    xa = src_buf[:, src_col0:src_col0 + D]
    ctx.op("dve", lambda e: e.tensor_reduce(ln.sums[:, :], xa, axis=AX.X, op=ALU.add),
           r=[src_buf], w=[ln.sums], late=True)
    ctx.op("dve", lambda e: e.tensor_scalar(ln.mu[:, :], ln.sums[:, :], 1.0 / D, None, op0=ALU.mult),
           r=[ln.sums], w=[ln.mu])
    ctx.op("dve", lambda e: e.tensor_scalar(ln.tctr[:, :], xa, ln.mu[:, :], None, op0=ALU.subtract),
           r=[src_buf, ln.mu], w=[ln.tctr])
    ctx.op("act", lambda e: e.activation(junk_ps[:, 0:D], ln.tctr[:, :], AF.Square,
                                         accum_out=ln.ssq[:, :]),
           r=[ln.tctr], w=[junk_ps, ln.ssq], late=True)
    ctx.op("act", lambda e: e.activation(ln.sd[:, :], ln.ssq[:, :], AF.Sqrt,
                                         bias=c["epsc"][:, 0:1], scale=1.0 / D),
           r=[ln.ssq, c["epsc"]], w=[ln.sd])
    ctx.op("dve", lambda e: e.reciprocal(ln.rstd[:, :], ln.sd[:, :]), r=[ln.sd], w=[ln.rstd],
           late=True)
    if gamma_b is None:
        ctx.op("dve", lambda e: e.tensor_scalar(ln.z[:, :], ln.tctr[:, :], ln.rstd[:, :], None,
                                                op0=ALU.mult), r=[ln.tctr, ln.rstd], w=[ln.z])
    else:
        ctx.op("dve", lambda e: e.tensor_scalar(ln.tctr[:, :], ln.tctr[:, :], ln.rstd[:, :], None,
                                                op0=ALU.mult), r=[ln.tctr, ln.rstd], w=[ln.tctr])
        ctx.op("dve", lambda e: e.tensor_mul(ln.tctr[:, :], ln.tctr[:, :], gamma_b[:, :]),
               r=[ln.tctr, gamma_b], w=[ln.tctr])
        ctx.op("dve", lambda e: e.tensor_add(ln.z[:, :], ln.tctr[:, :], beta_b[:, :]),
               r=[ln.tctr, beta_b], w=[ln.z])
    if z_out is not None:
        buf, col = z_out
        ctx.op("dve", lambda e: e.tensor_copy(buf[:, col:col + D], ln.z[:, :]),
               r=[ln.z], w=[buf])

    def tr(e):
        last = None
        for cc in range(KC):
            last = e.transpose(ptb[:, cc * 128:(cc + 1) * 128],
                               ln.z[:, cc * 128:(cc + 1) * 128], c["ident"][:, :])
        return last
    ctx.op("pe", tr, r=[ln.z, c["ident"]], w=[ptb])

    if zt_inplace is not None:
        buf, col = zt_inplace
        ctx.op("act", lambda e: e.copy(buf[:, col:col + D], ptb[:, 0:D]), r=[ptb], w=[buf])
    else:
        def ev(e):
            out_view = zT[:, :].rearrange("p (c g) -> p c g", g=GT)[:, :, zT_tcol:zT_tcol + 128]
            return e.copy(out_view, ptb[:, :].rearrange("p (c w) -> p c w", w=128))
        ctx.op("act", ev, r=[ptb], w=[zT])


def load_w_kc(ctx, dst, hbm_ap, k0, krows, cols0, ncols, dst_col0=0):
    src = hbm_ap[k0:k0 + krows, cols0:cols0 + ncols].rearrange("(kc p) m -> p kc m", p=128)
    ctx.dma("pool", dst[:, dst_col0:dst_col0 + (krows // 128) * ncols], src, w=[dst])



def emit_l0_ffn(ctx, c, W, xb):
    G = 4
    GT = TTOT // G
    NTG = GT // 128
    with ctx.scope():
        zT = ctx.sbuf("f0_zT", [128, KC * GT], BF16)
        coef = ctx.sbuf("f0_coef", [128, NTG * 8], F32)
        sgt = ctx.sbuf("f0_sg", [128, NTG], F32)
        w1b = [ctx.sbuf("f0_w1a", [128, KC * 1024], BF16),
               ctx.sbuf("f0_w1b", [128, KC * 1024], BF16)]
        w2b = ctx.sbuf("f0_w2", [128, KC * 1024], BF16)
        hT = ctx.sbuf("f0_hT", [128, KC * 512], BF16)
        rwb = ctx.sbuf("f0_rw", [128, KC * 9], BF16)
        rs = ctx.sbuf("f0_rs", [128, 8], F32)
        probs = ctx.sbuf("f0_probs", [128, 8], F32)
        pr2 = ctx.sbuf("f0_pr2", [128, 8], F32)
        msk = ctx.sbuf("f0_msk", [128, 8], F32)
        ln = LNScratch(ctx, "f0")

        ctx.op("dve", lambda e: e.memset(rs[:, 4:5], 0.0), w=[rs])
        load_w_kc(ctx, rwb, W["L0_ffn_rwsg"], 0, D, 0, 9)

        for g in range(G):
            with ctx.scope():
                p_small = ctx.psum("f0_psmall", [128, 16], F32)
                ptb = ctx.psum("f0_ptb", [128, 1024], BF16)
                junk = ctx.psum("f0_junk", [128, 1024], F32)
                for j in range(NTG):
                    jg = g * NTG + j
                    emit_ln_tile(ctx, c, ln, xb, jg * D, zT, j * 128, GT, ptb, junk)
                for j in range(NTG):
                    def logits_mm(e, j=j):
                        last = None
                        for k in range(KC):
                            last = e.matmul(p_small[:, 0:9],
                                            zT[:, k * GT + j * 128:k * GT + j * 128 + 128],
                                            rwb[:, k * 9:(k + 1) * 9],
                                            start=(k == 0), stop=(k == KC - 1))
                        return last
                    ctx.op("pe", logits_mm, r=[zT, rwb], w=[p_small])
                    ctx.op("act", lambda e, j=j: e.activation(sgt[:, j:j + 1], p_small[:, 8:9],
                                                              AF.Sigmoid), r=[p_small], w=[sgt])
                    ctx.op("dve", lambda e: e.tensor_reduce(rs[:, 0:1], p_small[:, 0:8],
                                                            axis=AX.X, op=ALU.max, negate=True),
                           r=[p_small], w=[rs], late=True)
                    ctx.op("act", lambda e: e.activation(probs[:, :], p_small[:, 0:8], AF.Exp,
                                                         bias=rs[:, 0:1], accum_out=rs[:, 1:2]),
                           r=[p_small, rs], w=[probs, rs], late=True)
                    ctx.op("dve", lambda e: e.reciprocal(rs[:, 2:3], rs[:, 1:2]), r=[rs], w=[rs],
                           late=True)
                    ctx.op("dve", lambda e: e.tensor_scalar(probs[:, :], probs[:, :], rs[:, 2:3],
                                                            None, op0=ALU.mult),
                           r=[probs, rs], w=[probs])
                    ctx.op("dve", lambda e: e.tensor_reduce(rs[:, 3:4], probs[:, :], axis=AX.X,
                                                            op=ALU.max, negate=True),
                           r=[probs], w=[rs], late=True)
                    ctx.op("dve", lambda e: e.tensor_scalar(msk[:, :], probs[:, :], rs[:, 3:4],
                                                            rs[:, 4:5], op0=ALU.add,
                                                            op1=ALU.is_ge),
                           r=[probs, rs], w=[msk])
                    ctx.op("dve", lambda e: e.scalar_tensor_tensor(pr2[:, :], msk[:, :], -1e9,
                                                                   probs[:, :], op0=ALU.mult,
                                                                   op1=ALU.add),
                           r=[msk, probs], w=[pr2])
                    ctx.op("dve", lambda e: e.tensor_reduce(rs[:, 5:6], pr2[:, :], axis=AX.X,
                                                            op=ALU.max, negate=True),
                           r=[pr2], w=[rs], late=True)
                    ctx.op("dve", lambda e: e.tensor_scalar(msk[:, :], probs[:, :], rs[:, 5:6],
                                                            rs[:, 4:5], op0=ALU.add,
                                                            op1=ALU.is_ge),
                           r=[probs, rs], w=[msk])
                    ctx.op("dve", lambda e, j=j: e.tensor_mul(coef[:, j * 8:(j + 1) * 8],
                                                              probs[:, :], msk[:, :]),
                           r=[probs, msk], w=[coef])
            with ctx.scope():
                p_h = [ctx.psum(f"f0_ph{i}", [128, 512], F32) for i in range(3)]
                p_y = [ctx.psum(f"f0_py{i}", [128, 512], F32) for i in range(4)]
                for e_i in range(9):
                    is_sh = (e_i == 8)
                    if not is_sh:
                        wa = w1b[e_i % 2]
                        if e_i == 0:
                            load_w_kc(ctx, wa, W["L0_ffn_w1"][0], 0, D, 0, 1024)
                        load_w_kc(ctx, w2b, W["L0_ffn_w2"][e_i], 0, D, 0, 1024)
                    else:
                        load_w_kc(ctx, w1b[0], W["L0_ffn_shared_gate_up"], 0, D, 0, 1024)
                        load_w_kc(ctx, w1b[1], W["L0_ffn_shared_gate_up"], 0, D, 1024, 1024)
                        load_w_kc(ctx, w2b, W["L0_ffn_shared_down"], 0, D, 0, 1024)
                    for t in range(GT // 512):
                        for m in range(KC):
                            if not is_sh:
                                ph = p_h[m % 3]
                                def hmm(e, m=m, t=t, wa=wa, ph=ph):
                                    last = None
                                    for k in range(KC):
                                        last = e.matmul(
                                            ph[:, :],
                                            wa[:, k * 1024 + m * 128:k * 1024 + (m + 1) * 128],
                                            zT[:, k * GT + t * 512:k * GT + (t + 1) * 512],
                                            start=(k == 0), stop=(k == KC - 1))
                                    return last
                                ctx.op("pe", hmm, r=[wa, zT], w=[ph])
                                ctx.op("act", lambda e, m=m, ph=ph: e.activation(
                                    hT[:, m * 512:(m + 1) * 512], ph[:, :], AF.Silu),
                                    r=[ph], w=[hT])
                            else:
                                ph = p_h[(2 * m) % 3]
                                ph2 = p_h[(2 * m + 1) % 3]
                                def gumm(e, m=m, t=t, ph=ph, ph2=ph2):
                                    last = None
                                    for k in range(KC):
                                        last = e.matmul(
                                            ph[:, :],
                                            w1b[0][:, k * 1024 + m * 128:k * 1024 + (m + 1) * 128],
                                            zT[:, k * GT + t * 512:k * GT + (t + 1) * 512],
                                            start=(k == 0), stop=(k == KC - 1))
                                    for k in range(KC):
                                        last = e.matmul(
                                            ph2[:, :],
                                            w1b[1][:, k * 1024 + m * 128:k * 1024 + (m + 1) * 128],
                                            zT[:, k * GT + t * 512:k * GT + (t + 1) * 512],
                                            start=(k == 0), stop=(k == KC - 1))
                                    return last
                                ctx.op("pe", gumm, r=[w1b[0], w1b[1], zT], w=[ph, ph2])
                                ctx.op("act", lambda e, ph=ph: e.activation(
                                    ln.z[:, 0:512], ph[:, :], AF.Silu), r=[ph], w=[ln.z])
                                ctx.op("dve", lambda e, m=m, ph2=ph2: e.tensor_mul(
                                    hT[:, m * 512:(m + 1) * 512], ln.z[:, 0:512], ph2[:, :]),
                                    r=[ln.z, ph2], w=[hT])
                        for sub in range(4):
                            j_in_g = t * 4 + sub
                            jg = g * NTG + j_in_g
                            cc_buf = sgt if is_sh else coef
                            cc = (sgt[:, j_in_g:j_in_g + 1] if is_sh
                                  else coef[:, j_in_g * 8 + e_i:j_in_g * 8 + e_i + 1])
                            for half in range(2):
                                py = p_y[(sub * 2 + half) % 4]
                                sc = sub * 128
                                def ymm(e, sc=sc, py=py, half=half):
                                    last = None
                                    for k in range(KC):
                                        last = e.matmul(
                                            py[:, :],
                                            hT[:, k * 512 + sc:k * 512 + sc + 128],
                                            w2b[:, k * 1024 + half * 512:
                                                k * 1024 + (half + 1) * 512],
                                            start=(k == 0), stop=(k == KC - 1))
                                    return last
                                ctx.op("pe", ymm, r=[hT, w2b], w=[py])
                                ctx.op("dve", lambda e, py=py, jg=jg, cc=cc, half=half:
                                       e.scalar_tensor_tensor(
                                    xb[:, jg * D + half * 512:jg * D + (half + 1) * 512],
                                    py[:, :], cc,
                                    xb[:, jg * D + half * 512:jg * D + (half + 1) * 512],
                                    op0=ALU.mult, op1=ALU.add),
                                    r=[py, cc_buf, xb], w=[xb])
                    if not is_sh and e_i + 1 < 8:
                        load_w_kc(ctx, w1b[(e_i + 1) % 2], W["L0_ffn_w1"][e_i + 1], 0, D, 0, 1024)


def emit_l1_ffn(ctx, c, W, xb):
    G = 4
    GT = TTOT // G
    NTG = GT // 128
    NQ = 4
    with ctx.scope():
        zT = ctx.sbuf("f1_zT", [128, KC * GT], BF16)
        coef = ctx.sbuf("f1_coef", [128, NTG * 8], F32)
        w1b = ctx.sbuf("f1_w1", [128, KC * 1024], BF16)
        w2b = ctx.sbuf("f1_w2", [128, KC * 1024], BF16)
        hT = ctx.sbuf("f1_hT", [128, KC * 512], BF16)
        rwb = ctx.sbuf("f1_rw", [128, KC * 8], BF16)
        rs = ctx.sbuf("f1_rs", [128, 8], F32)
        msk = ctx.sbuf("f1_msk", [128, 8], F32)
        ln = LNScratch(ctx, "f1")
        ctx.op("dve", lambda e: e.memset(rs[:, 4:5], 0.0), w=[rs])
        load_w_kc(ctx, rwb, W["L1_ffn_Wr"], 0, D, 0, 8)

        for g in range(G):
            with ctx.scope():
                p_small = ctx.psum("f1_psmall", [128, 16], F32)
                ptb = ctx.psum("f1_ptb", [128, 1024], BF16)
                junk = ctx.psum("f1_junk", [128, 1024], F32)
                for j in range(NTG):
                    jg = g * NTG + j
                    emit_ln_tile(ctx, c, ln, xb, jg * D, zT, j * 128, GT, ptb, junk)
                for j in range(NTG):
                    def logits_mm(e, j=j):
                        last = None
                        for k in range(KC):
                            last = e.matmul(p_small[:, 0:8],
                                            zT[:, k * GT + j * 128:k * GT + j * 128 + 128],
                                            rwb[:, k * 8:(k + 1) * 8],
                                            start=(k == 0), stop=(k == KC - 1))
                        return last
                    ctx.op("pe", logits_mm, r=[zT, rwb], w=[p_small])
                    ctx.op("dve", lambda e: e.tensor_reduce(rs[:, 0:1], p_small[:, 0:8],
                                                            axis=AX.X, op=ALU.max, negate=True),
                           r=[p_small], w=[rs], late=True)
                    ctx.op("dve", lambda e: e.tensor_scalar(msk[:, :], p_small[:, 0:8],
                                                            rs[:, 0:1], rs[:, 4:5],
                                                            op0=ALU.add, op1=ALU.is_ge),
                           r=[p_small, rs], w=[msk])
                    ctx.op("dve", lambda e: e.scalar_tensor_tensor(ln.tctr[:, 0:8], msk[:, :],
                                                                   -1e9, p_small[:, 0:8],
                                                                   op0=ALU.mult, op1=ALU.add),
                           r=[msk, p_small], w=[ln.tctr])
                    ctx.op("dve", lambda e: e.tensor_reduce(rs[:, 1:2], ln.tctr[:, 0:8],
                                                            axis=AX.X, op=ALU.max, negate=True),
                           r=[ln.tctr], w=[rs], late=True)
                    ctx.op("dve", lambda e: e.tensor_sub(rs[:, 2:3], rs[:, 1:2], rs[:, 0:1]),
                           r=[rs], w=[rs])
                    ctx.op("act", lambda e: e.activation(rs[:, 3:4], rs[:, 2:3], AF.Sigmoid),
                           r=[rs], w=[rs])
                    ctx.op("dve", lambda e, j=j: e.tensor_scalar(coef[:, j * 8:(j + 1) * 8],
                                                                 msk[:, :], rs[:, 3:4], None,
                                                                 op0=ALU.mult),
                           r=[msk, rs], w=[coef])
            with ctx.scope():
                p_h = [ctx.psum(f"f1_ph{i}", [128, 512], F32) for i in range(3)]
                p_y = [ctx.psum(f"f1_py{i}", [128, 512], F32) for i in range(4)]
                for e_i in range(8):
                    for q in range(NQ):
                        load_w_kc(ctx, w1b, W["L1_ffn_W1"][e_i], 0, D, q * 1024, 1024)
                        load_w_kc(ctx, w2b, W["L1_ffn_W2"][e_i], q * 1024, 1024, 0, 1024)
                        for t in range(GT // 512):
                            for m in range(KC):
                                ph = p_h[m % 3]
                                def hmm(e, m=m, t=t, ph=ph):
                                    last = None
                                    for k in range(KC):
                                        last = e.matmul(
                                            ph[:, :],
                                            w1b[:, k * 1024 + m * 128:k * 1024 + (m + 1) * 128],
                                            zT[:, k * GT + t * 512:k * GT + (t + 1) * 512],
                                            start=(k == 0), stop=(k == KC - 1))
                                    return last
                                ctx.op("pe", hmm, r=[w1b, zT], w=[ph])
                                ctx.op("act", lambda e, m=m, ph=ph: e.activation(
                                    hT[:, m * 512:(m + 1) * 512], ph[:, :], AF.Gelu),
                                    r=[ph], w=[hT])
                            for sub in range(4):
                                j_in_g = t * 4 + sub
                                jg = g * NTG + j_in_g
                                cc = coef[:, j_in_g * 8 + e_i:j_in_g * 8 + e_i + 1]
                                for half in range(2):
                                    py = p_y[(sub * 2 + half) % 4]
                                    sc = sub * 128
                                    def ymm(e, sc=sc, py=py, half=half):
                                        last = None
                                        for k in range(KC):
                                            last = e.matmul(
                                                py[:, :],
                                                hT[:, k * 512 + sc:k * 512 + sc + 128],
                                                w2b[:, k * 1024 + half * 512:
                                                    k * 1024 + (half + 1) * 512],
                                                start=(k == 0), stop=(k == KC - 1))
                                        return last
                                    ctx.op("pe", ymm, r=[hT, w2b], w=[py])
                                    ctx.op("dve", lambda e, py=py, jg=jg, cc=cc, half=half:
                                           e.scalar_tensor_tensor(
                                        xb[:, jg * D + half * 512:jg * D + (half + 1) * 512],
                                        py[:, :], cc,
                                        xb[:, jg * D + half * 512:jg * D + (half + 1) * 512],
                                        op0=ALU.mult, op1=ALU.add),
                                        r=[py, coef, xb], w=[xb])



def emit_l0_attn_b(ctx, c, W, xb):
    NC = S // 128
    with ctx.scope():
        bqc = ctx.sbuf("ab_bqc", [128, 8], F32)
        bkc = ctx.sbuf("ab_bkc", [128, 8], F32)
        tri = ctx.sbuf("ab_tri", [128, 128], BF16)
        onesf = ctx.sbuf("ab_onesf", [1, 128], F32)
        bxr = ctx.sbuf("ab_brow", [1, 2 * D], BF16)
        ptb = ctx.psum("ab_ptb", [128, 1024], BF16)
        emit_tri_mask(ctx, c, tri)
        ctx.op("dve", lambda e: e.memset(onesf[:, :], 1.0), w=[onesf])
        for i, nm in enumerate(["bv", "bo"]):
            ctx.dma("pool", bxr[0:1, i * D:(i + 1) * D],
                    W["L0_attn_b_" + nm].unsqueeze(0), w=[bxr])
        with ctx.scope():
            bqkr = ctx.sbuf("ab_bqkr", [1, 2 * D], BF16)
            for i, nm in enumerate(["bq", "bk"]):
                ctx.dma("pool", bqkr[0:1, i * D:(i + 1) * D],
                        W["L0_attn_b_" + nm].unsqueeze(0), w=[bqkr])
            for cols, off in [(bqc, 0), (bkc, D)]:
                def tr(e, off=off):
                    last = None
                    for cc in range(8):
                        last = e.transpose(ptb[:, 2 * cc:2 * cc + 1],
                                           bqkr[0:1, off + cc * 128:off + (cc + 1) * 128],
                                           c["ident"][0:1, 0:1])
                    return last
                ctx.op("pe", tr, r=[bqkr, c["ident"]], w=[ptb])
                ctx.op("act", lambda e, cols=cols: e.copy(
                    cols[:, 0:8], ptb[:, :].rearrange("p (a b) -> p a b", b=2)[:, 0:8, 0]),
                       r=[ptb], w=[cols])
        zT = ctx.sbuf("ab_zT", [128, KC * S], BF16)
        wq_sl = ctx.sbuf("ab_wq", [128, KC * 128], BF16)
        wk_sl = ctx.sbuf("ab_wk", [128, KC * 128], BF16)
        wv_sl = ctx.sbuf("ab_wv", [128, KC * 128], BF16)
        wo_sl = ctx.sbuf("ab_wo", [128, 1024], BF16)
        qphi = ctx.sbuf("ab_qphi", [128, S], BF16)
        kphi = ctx.sbuf("ab_kphi", [128, S], BF16)
        ktokA = ctx.sbuf("ab_ktA", [128, NC * 64], BF16)
        ktokB = ctx.sbuf("ab_ktB", [128, NC * 64], BF16)
        vauA = ctx.sbuf("ab_vaA", [128, NC * 65], BF16)
        vauB = ctx.sbuf("ab_vaB", [128, NC * 65], BF16)
        otp = ctx.sbuf("ab_otp", [128, S], BF16)
        stsb = ctx.sbuf("ab_stsb", [128, 65], BF16)
        den = ctx.sbuf("ab_den", [1, 256], F32)
        rden = ctx.sbuf("ab_rden", [1, 256], F32)
        rbA = ctx.sbuf("ab_rbA", [64, 128], F32)
        rbB = ctx.sbuf("ab_rbB", [64, 128], F32)
        elu1 = ctx.sbuf("ab_e1", [128, 512], F32)
        elu2 = ctx.sbuf("ab_e2", [128, 512], BF16)
        ln = LNScratch(ctx, "ab")

        for b in range(2):
            with ctx.scope():
                junk = ctx.psum("ab_junk", [128, 1024], F32)
                for j in range(NC):
                    jg = b * NC + j
                    emit_ln_tile(ctx, c, ln, xb, jg * D, zT, j * 128, S, ptb, junk)
            for pr in range(8):
                hA = 2 * pr
                load_w_kc(ctx, wq_sl, W["L0_attn_b_WqT"], 0, D, hA * 64, 128)
                load_w_kc(ctx, wk_sl, W["L0_attn_b_WkT"], 0, D, hA * 64, 128)
                load_w_kc(ctx, wv_sl, W["L0_attn_b_WvT"], 0, D, hA * 64, 128)
                load_w_kc(ctx, wo_sl, W["L0_attn_b_WoT"], pr * 128, 128, 0, 1024)
                with ctx.scope():
                    p0 = ctx.psum("ab_p0", [128, 512], F32)
                    p1 = ctx.psum("ab_p1", [128, 512], F32)
                    for dst, wsl, bc in [(qphi, wq_sl, bqc), (kphi, wk_sl, bkc)]:
                        for t in range(4):
                            pp = [p0, p1][t % 2]
                            def qmm(e, t=t, pp=pp, wsl=wsl):
                                last = None
                                for k in range(KC):
                                    last = e.matmul(
                                        pp[:, :], wsl[:, k * 128:(k + 1) * 128],
                                        zT[:, k * S + t * 512:k * S + (t + 1) * 512],
                                        start=(k == 0), stop=(k == KC - 1))
                                return last
                            ctx.op("pe", qmm, r=[wsl, zT], w=[pp])
                            bap = bc[:, pr:pr + 1]
                            ctx.op("dve", lambda e, pp=pp, bap=bap, bc=bc: e.tensor_scalar(
                                elu1[:, :], pp[:, :], bap, 0.0, op0=ALU.add, op1=ALU.min),
                                r=[pp, bc], w=[elu1])
                            ctx.op("act", lambda e: e.activation(elu2[:, :], elu1[:, :], AF.Exp),
                                   r=[elu1], w=[elu2])
                            ctx.op("dve", lambda e, pp=pp, bap=bap, bc=bc: e.tensor_scalar(
                                elu1[:, :], pp[:, :], bap, 0.0, op0=ALU.add, op1=ALU.max),
                                r=[pp, bc], w=[elu1])
                            ctx.op("dve", lambda e, t=t, dst=dst: e.tensor_add(
                                dst[:, t * 512:(t + 1) * 512], elu1[:, :], elu2[:, :]),
                                r=[elu1, elu2], w=[dst])
                    for src_lo, ktok in [(0, ktokA), (64, ktokB)]:
                        def ktr(e, src_lo=src_lo):
                            last = None
                            for i in range(NC):
                                last = e.transpose(ptb[:, i * 64:(i + 1) * 64],
                                                   kphi[src_lo:src_lo + 64, i * 128:(i + 1) * 128],
                                                   c["ident"][src_lo:src_lo + 64,
                                                              src_lo:src_lo + 64])
                            return last
                        ctx.op("pe", ktr, r=[kphi, c["ident"]], w=[ptb])
                        ctx.op("act", lambda e, ktok=ktok: e.copy(ktok[:, :], ptb[:, 0:NC * 64]),
                               r=[ptb], w=[ktok])
                    ctx.op("dve", lambda e: e.memset(vauA[:, :], 1.0), w=[vauA])
                    ctx.op("dve", lambda e: e.memset(vauB[:, :], 1.0), w=[vauB])
                    for t in range(NC):
                        pp = [p0, p1][t % 2]
                        def vmm(e, t=t, pp=pp):
                            for k in range(KC):
                                e.matmul(pp[:, 0:128],
                                         zT[:, k * S + t * 128:k * S + (t + 1) * 128],
                                         wv_sl[:, k * 128:(k + 1) * 128],
                                         start=(k == 0), stop=False)
                            return e.matmul(pp[:, 0:128], c["ones"][0:1, :],
                                            bxr[0:1, hA * 64:hA * 64 + 128],
                                            start=False, stop=True)
                        ctx.op("pe", vmm, r=[zT, wv_sl, c["ones"], bxr], w=[pp])
                        ctx.op("act", lambda e, t=t, pp=pp: e.copy(vauA[:, t * 65:t * 65 + 64],
                                                                   pp[:, 0:64]), r=[pp], w=[vauA])
                        ctx.op("act", lambda e, t=t, pp=pp: e.copy(vauB[:, t * 65:t * 65 + 64],
                                                                   pp[:, 64:128]), r=[pp], w=[vauB])
                with ctx.scope():
                    smA = ctx.psum("ab_smA", [128, 128], F32)
                    smB = ctx.psum("ab_smB", [128, 128], F32)
                    nmA = ctx.psum("ab_nmA", [65, 128], F32)
                    nmB = ctx.psum("ab_nmB", [65, 128], F32)
                    pbA = ctx.psum("ab_pbA", [64, 128], F32)
                    pbB = ctx.psum("ab_pbB", [64, 128], F32)
                    st2 = ctx.psum("ab_st2", [128, 65], F32)
                    smAs = ctx.sbuf("ab_smAs", [128, 128], BF16)
                    smBs = ctx.sbuf("ab_smBs", [128, 128], BF16)
                    ctx.op("dve", lambda e: e.memset(stsb[:, :], 0.0), w=[stsb])
                    for i in range(NC):
                        for lo, sm, sms, nm, pb, rb, ktok, vau, dc in [
                                (0, smA, smAs, nmA, pbA, rbA, ktokA, vauA, 0),
                                (64, smB, smBs, nmB, pbB, rbB, ktokB, vauB, 128)]:
                            ctx.op("pe", lambda e, lo=lo, sm=sm, i=i: e.matmul(
                                sm[:, :], kphi[lo:lo + 64, i * 128:(i + 1) * 128],
                                qphi[lo:lo + 64, i * 128:(i + 1) * 128],
                                start=True, stop=True, tile_position=(lo, 0)),
                                r=[kphi, qphi], w=[sm])
                            ctx.op("dve", lambda e, sm=sm, sms=sms: e.tensor_mul(
                                sms[:, :], sm[:, :], tri[:, :]),
                                r=[sm, tri], w=[sms])
                            def numm(e, lo=lo, nm=nm, sms=sms, vau=vau, i=i):
                                e.matmul(nm[:, :], stsb[lo:lo + 64, :],
                                         qphi[lo:lo + 64, i * 128:(i + 1) * 128],
                                         start=True, stop=False)
                                return e.matmul(nm[:, :], vau[:, i * 65:(i + 1) * 65],
                                                sms[:, :], start=False, stop=True)
                            ctx.op("pe", numm, r=[stsb, qphi, vau, sms], w=[nm])
                            ctx.op("pe", lambda e, lo=lo, ktok=ktok, vau=vau, i=i: e.matmul(
                                st2[lo:lo + 64, :], ktok[:, i * 64:(i + 1) * 64],
                                vau[:, i * 65:(i + 1) * 65],
                                start=(i == 0), stop=(i == NC - 1)),
                                r=[ktok, vau], w=[st2])
                            ctx.op("act", lambda e, lo=lo: e.copy(
                                stsb[lo:lo + 64, :], st2[lo:lo + 64, :]), r=[st2], w=[stsb])
                            ctx.op("dve", lambda e, nm=nm, dc=dc: e.tensor_scalar(
                                den[0:1, dc:dc + 128], nm[64:65, :], 1e-6, None, op0=ALU.add),
                                r=[nm], w=[den])
                            ctx.op("dve", lambda e, dc=dc: e.reciprocal(
                                rden[0:1, dc:dc + 128], den[0:1, dc:dc + 128]),
                                r=[den], w=[rden], late=True)
                            ctx.op("pe", lambda e, pb=pb, dc=dc: e.matmul(
                                pb[:, :], onesf[0:1, 0:64], rden[0:1, dc:dc + 128],
                                start=True, stop=True), r=[onesf, rden], w=[pb])
                            ctx.op("act", lambda e, rb=rb, pb=pb: e.copy(rb[:, :], pb[:, :]),
                                   r=[pb], w=[rb])
                            ctx.op("dve", lambda e, lo=lo, nm=nm, rb=rb, i=i: e.tensor_mul(
                                otp[lo:lo + 64, i * 128:(i + 1) * 128], nm[0:64, :], rb[:, :]),
                                r=[nm, rb], w=[otp])
                with ctx.scope():
                    py = [ctx.psum("ab_py0", [128, 512], F32), ctx.psum("ab_py1", [128, 512], F32)]
                    for t in range(NC):
                        for half in range(2):
                            pp = py[half]
                            def ymm(e, t=t, half=half, pp=pp, pr=pr):
                                last = e.matmul(pp[:, :], otp[:, t * 128:(t + 1) * 128],
                                                wo_sl[:, half * 512:(half + 1) * 512],
                                                start=True, stop=(pr != 0))
                                if pr == 0:
                                    last = e.matmul(pp[:, :], c["ones"][0:1, :],
                                                    bxr[0:1, D + half * 512:
                                                        D + (half + 1) * 512],
                                                    start=False, stop=True)
                                return last
                            ctx.op("pe", ymm, r=[otp, wo_sl, c["ones"], bxr], w=[pp])
                            jg = b * NC + t
                            ctx.op("dve", lambda e, pp=pp, jg=jg, half=half: e.tensor_add(
                                xb[:, jg * D + half * 512:jg * D + (half + 1) * 512],
                                xb[:, jg * D + half * 512:jg * D + (half + 1) * 512],
                                pp[:, :]), r=[pp, xb], w=[xb])


def emit_l0_attn_a(ctx, c, W, xb):
    NC = S // 128
    QT = 122
    NTILE = (S + QT - 1) // QT
    PAD = (NTILE - 1) * QT + 128
    with ctx.scope():
        zT = ctx.sbuf("aa_zT", [128, KC * S], BF16)
        band = ctx.sbuf("aa_band", [128, 128], BF16)
        bor = ctx.sbuf("aa_bor", [1, D], BF16)
        projB = ctx.sbuf("aa_proj", [128, 64], BF16)
        wsl = ctx.sbuf("aa_wsl", [128, KC * 128], BF16)
        wosl = ctx.sbuf("aa_wo", [128, 1024], BF16)
        qf = ctx.sbuf("aa_qf", [128, S], BF16)
        kfp = ctx.sbuf("aa_kfp", [128, PAD], BF16)
        vfp = ctx.sbuf("aa_vfp", [128, PAD], BF16)
        qp = ctx.sbuf("aa_qp", [64, S], BF16)
        kpp = ctx.sbuf("aa_kpp", [64, PAD], BF16)
        otb = ctx.sbuf("aa_ot", [128, S], BF16)
        smf = ctx.sbuf("aa_smf", [128, 128], F32)
        smz = ctx.sbuf("aa_smz", [128, 128], BF16)
        szt = ctx.sbuf("aa_szt", [128, 128], BF16)
        vwt = ctx.sbuf("aa_vwt", [128, 128], BF16)
        rs = ctx.sbuf("aa_rs", [128, 2], F32)
        elu1 = ctx.sbuf("aa_e1", [64, 512], F32)
        elu2 = ctx.sbuf("aa_e2", [64, 512], BF16)
        ln = LNScratch(ctx, "aa")
        ptb = ctx.psum("aa_ptb", [128, 1024], BF16)

        emit_band_mask(ctx, c, band)
        ctx.dma("pool", bor[0:1, :], W["L0_attn_a_b_out"].unsqueeze(0), w=[bor])
        ctx.dma("pool", projB[:, :], W["L0_attn_a_proj"], w=[projB])

        for b in range(2):
            with ctx.scope():
                junk = ctx.psum("aa_junk", [128, 1024], F32)
                for j in range(NC):
                    jg = b * NC + j
                    emit_ln_tile(ctx, c, ln, xb, jg * D, zT, j * 128, S, ptb, junk)
            for h in range(8):
                with ctx.scope():
                    p0 = ctx.psum("aa_p0", [128, 512], F32)
                    p1 = ctx.psum("aa_p1", [128, 512], F32)
                    pf = ctx.psum("aa_pf", [64, 512], F32)
                    for dst, coff, pad in [(qf, h * 128, 0), (kfp, D + h * 128, 3),
                                           (vfp, 2 * D + h * 128, 3)]:
                        if pad:
                            ctx.op("dve", lambda e, dst=dst: e.memset(dst[:, :], 0.0), w=[dst])
                        load_w_kc(ctx, wsl, W["L0_attn_a_w_qkv"], 0, D, coff, 128)
                        for t in range(4):
                            pp = [p0, p1][t % 2]
                            def qmm(e, t=t, pp=pp):
                                last = None
                                for k in range(KC):
                                    last = e.matmul(
                                        pp[:, :], wsl[:, k * 128:(k + 1) * 128],
                                        zT[:, k * S + t * 512:k * S + (t + 1) * 512],
                                        start=(k == 0), stop=(k == KC - 1))
                                return last
                            ctx.op("pe", qmm, r=[wsl, zT], w=[pp])
                            ctx.op("act", lambda e, t=t, pp=pp, dst=dst, pad=pad: e.copy(
                                dst[:, pad + t * 512:pad + (t + 1) * 512], pp[:, :]),
                                r=[pp], w=[dst])
                    for src, dst, pad, n in [(qf, qp, 0, 4), (kfp, kpp, 3, 4)]:
                        if pad:
                            ctx.op("dve", lambda e, dst=dst: e.memset(dst[:, :], 0.0), w=[dst])
                        for t in range(n):
                            def fmm(e, t=t, src=src, pad=pad):
                                return e.matmul(pf[:, :], projB[:, :],
                                                src[:, pad + t * 512:pad + (t + 1) * 512],
                                                start=True, stop=True)
                            ctx.op("pe", fmm, r=[projB, src], w=[pf])
                            ctx.op("dve", lambda e: e.tensor_scalar(
                                elu1[:, :], pf[:, :], 0.0, None, op0=ALU.min),
                                r=[pf], w=[elu1])
                            ctx.op("act", lambda e: e.activation(elu2[:, :], elu1[:, :], AF.Exp),
                                   r=[elu1], w=[elu2])
                            ctx.op("dve", lambda e: e.tensor_scalar(
                                elu1[:, :], pf[:, :], 0.0, None, op0=ALU.max),
                                r=[pf], w=[elu1])
                            ctx.op("dve", lambda e, t=t, dst=dst, pad=pad: e.tensor_add(
                                dst[:, pad + t * 512:pad + (t + 1) * 512], elu1[:, :],
                                elu2[:, :]), r=[elu1, elu2], w=[dst])
                with ctx.scope():
                    psm = ctx.psum("aa_psm", [128, 128], F32)
                    pot = ctx.psum("aa_pot", [128, 128], F32)
                    for j in range(NTILE):
                        q0 = j * QT
                        qn = min(QT, S - q0)
                        ctx.op("pe", lambda e, q0=q0, qn=qn: e.matmul(
                            psm[0:qn, :], qp[:, q0:q0 + qn], kpp[:, q0:q0 + 128],
                            start=True, stop=True), r=[qp, kpp], w=[psm])
                        ctx.op("dve", lambda e, qn=qn: e.tensor_mul(
                            smf[0:qn, :], psm[0:qn, :], band[0:qn, :]),
                            r=[psm, band], w=[smf])
                        ctx.op("dve", lambda e, qn=qn: e.tensor_reduce(
                            rs[0:qn, 0:1], smf[0:qn, :], axis=AX.X, op=ALU.add),
                            r=[smf], w=[rs], late=True)
                        ctx.op("dve", lambda e, qn=qn: e.tensor_scalar(
                            rs[0:qn, 1:2], rs[0:qn, 0:1], 1e-8, None, op0=ALU.add),
                            r=[rs], w=[rs])
                        ctx.op("dve", lambda e, qn=qn: e.reciprocal(
                            rs[0:qn, 0:1], rs[0:qn, 1:2]), r=[rs], w=[rs], late=True)
                        ctx.op("dve", lambda e, qn=qn: e.tensor_scalar(
                            smz[0:qn, :], smf[0:qn, :], rs[0:qn, 0:1], None, op0=ALU.mult),
                            r=[smf, rs], w=[smz])
                        def trz(e, qn=qn):
                            return e.transpose(ptb[:, 0:qn], smz[0:qn, :],
                                               c["ident"][0:qn, 0:qn])
                        ctx.op("pe", trz, r=[smz, c["ident"]], w=[ptb])
                        ctx.op("act", lambda e, qn=qn: e.copy(szt[:, 0:qn], ptb[:, 0:qn]),
                               r=[ptb], w=[szt])
                        def trv(e, q0=q0):
                            return e.transpose(ptb[:, 128:256], vfp[:, q0:q0 + 128],
                                               c["ident"][:, :])
                        ctx.op("pe", trv, r=[vfp, c["ident"]], w=[ptb])
                        ctx.op("act", lambda e: e.copy(vwt[:, :], ptb[:, 128:256]),
                               r=[ptb], w=[vwt])
                        ctx.op("pe", lambda e, qn=qn: e.matmul(
                            pot[:, 0:qn], vwt[:, :], szt[:, 0:qn], start=True, stop=True),
                            r=[vwt, szt], w=[pot])
                        ctx.op("act", lambda e, q0=q0, qn=qn: e.copy(
                            otb[:, q0:q0 + qn], pot[:, 0:qn]), r=[pot], w=[otb])
                load_w_kc(ctx, wosl, W["L0_attn_a_w_out"], h * 128, 128, 0, 1024)
                with ctx.scope():
                    py = [ctx.psum("aa_py0", [128, 512], F32), ctx.psum("aa_py1", [128, 512], F32)]
                    for t in range(NC):
                        for half in range(2):
                            pp = py[half]
                            def ymm(e, t=t, half=half, pp=pp, h=h):
                                last = e.matmul(pp[:, :], otb[:, t * 128:(t + 1) * 128],
                                                wosl[:, half * 512:(half + 1) * 512],
                                                start=True, stop=(h != 0))
                                if h == 0:
                                    last = e.matmul(pp[:, :], c["ones"][0:1, :],
                                                    bor[0:1, half * 512:(half + 1) * 512],
                                                    start=False, stop=True)
                                return last
                            ctx.op("pe", ymm, r=[otb, wosl, c["ones"], bor], w=[pp])
                            jg = b * NC + t
                            ctx.op("dve", lambda e, pp=pp, jg=jg, half=half: e.tensor_add(
                                xb[:, jg * D + half * 512:jg * D + (half + 1) * 512],
                                xb[:, jg * D + half * 512:jg * D + (half + 1) * 512],
                                pp[:, :]), r=[pp, xb], w=[xb])



def emit_rope_tables(ctx, c, ct, st, mode):
    TWO_PI = 6.283185307179586
    with ctx.scope():
        pos = ctx.sbuf("rp_pos", [128, 1024], F32)
        th = ctx.sbuf("rp_th", [128, 64], F32)
        m = ctx.sbuf("rp_m", [128, 1024], F32)
        fr = ctx.sbuf("rp_fr", [128, 1024], F32)
        i32 = ctx.sbuf("rp_i32", [128, 1024], mybir.dt.int32)
        ctx.op("pool", lambda e: e.iota(pos[:, :].rearrange("p (j d) -> p j d", d=64),
                                        [[128, 16], [0, 64]], channel_multiplier=1,
                                        allow_small_or_imprecise_dtypes=True), w=[pos])
        if mode == "neox":
            ctx.op("pool", lambda e: e.iota(th[:, :], [[0, 2], [1, 32]], channel_multiplier=0,
                                            allow_small_or_imprecise_dtypes=True), w=[th])
        else:
            ctx.op("pool", lambda e: e.iota(th[:, :], [[1, 32], [0, 2]], channel_multiplier=0,
                                            allow_small_or_imprecise_dtypes=True), w=[th])
        ctx.op("act", lambda e: e.activation(th[:, :], th[:, :], AF.Exp,
                                             scale=-0.2878231366242557), r=[th], w=[th])
        ctx.op("dve", lambda e: e.tensor_mul(m[:, :].rearrange("p (j d) -> p j d", d=64),
                                             pos[:, :].rearrange("p (j d) -> p j d", d=64),
                                             th[:, :].unsqueeze(1).broadcast_to((128, 16, 64))),
               r=[pos, th], w=[m])
        ctx.op("dve", lambda e: e.tensor_scalar(m[:, :], m[:, :], 1.0 / TWO_PI, None,
                                                op0=ALU.mult), r=[m], w=[m])
        ctx.op("dve", lambda e: e.tensor_copy(i32[:, :], m[:, :]), r=[m], w=[i32])
        ctx.op("dve", lambda e: e.tensor_copy(pos[:, :], i32[:, :]), r=[i32], w=[pos])
        ctx.op("dve", lambda e: e.tensor_sub(fr[:, :], m[:, :], pos[:, :]), r=[m, pos], w=[fr])
        ctx.op("act", lambda e: e.activation(fr[:, :], fr[:, :], AF.Sin, scale=TWO_PI),
               r=[fr], w=[fr])
        if mode == "neox":
            v = fr[:, :].rearrange("p (j h d) -> p j h d", h=2, d=32)
            sv = st[:, :].rearrange("p (j h d) -> p j h d", h=2, d=32)
            ctx.op("dve", lambda e: e.tensor_scalar(sv[:, :, 0, :], v[:, :, 0, :], -1.0, None,
                                                    op0=ALU.mult), r=[fr], w=[st])
            ctx.op("dve", lambda e: e.tensor_copy(sv[:, :, 1, :], v[:, :, 1, :]),
                   r=[fr], w=[st])
        else:
            v2 = fr[:, :].rearrange("p (j d h) -> p j d h", h=2, d=32)
            sv = st[:, :].rearrange("p (j d h) -> p j d h", h=2, d=32)
            ctx.op("dve", lambda e: e.tensor_scalar(sv[:, :, :, 0], v2[:, :, :, 0], -1.0, None,
                                                    op0=ALU.mult), r=[fr], w=[st])
            ctx.op("dve", lambda e: e.tensor_copy(sv[:, :, :, 1], v2[:, :, :, 1]),
                   r=[fr], w=[st])
        ctx.op("dve", lambda e: e.tensor_scalar(m[:, :], m[:, :], 0.25, None, op0=ALU.add),
               r=[m], w=[m])
        ctx.op("dve", lambda e: e.tensor_copy(i32[:, :], m[:, :]), r=[m], w=[i32])
        ctx.op("dve", lambda e: e.tensor_copy(pos[:, :], i32[:, :]), r=[i32], w=[pos])
        ctx.op("dve", lambda e: e.tensor_sub(fr[:, :], m[:, :], pos[:, :]), r=[m, pos], w=[fr])
        ctx.op("act", lambda e: e.activation(ct[:, :], fr[:, :], AF.Sin, scale=TWO_PI),
               r=[fr], w=[ct])



def emit_l1_attn_b(ctx, c, W, xb):
    NC = S // 128
    H, KV, HD = 16, 4, 64
    with ctx.scope():
        ct = ctx.sbuf("sb_ct", [128, NC * HD], BF16)
        st = ctx.sbuf("sb_st", [128, NC * HD], BF16)
        zT = ctx.sbuf("sb_zT", [128, KC * S], BF16)
        masks = ctx.sbuf("sb_msk", [128, 8 * 512], BF16)
        emit_rope_tables(ctx, c, ct, st, "neox")
        with ctx.scope():
            mi = ctx.sbuf("sb_mi", [128, 512], F32)
            mj = ctx.sbuf("sb_mj", [128, 512], F32)
            mk = ctx.sbuf("sb_mk", [128, 512], F32)
            ctx.op("pool", lambda e: e.iota(
                mi[:, :], [[1, 512]], channel_multiplier=-1,
                allow_small_or_imprecise_dtypes=True), w=[mi])
            for mix, delta in enumerate(range(-384, 513, 128)):
                ctx.op("dve", lambda e, delta=delta: e.tensor_scalar(
                    mj[:, :], mi[:, :], float(-delta), None, op0=ALU.is_ge),
                    r=[mi], w=[mj])
                ctx.op("dve", lambda e, delta=delta: e.tensor_scalar(
                    mk[:, :], mi[:, :], float(511 - delta), None,
                    op0=ALU.is_le), r=[mi], w=[mk])
                ctx.op("dve", lambda e, mix=mix: e.tensor_mul(
                    masks[:, mix * 512:(mix + 1) * 512], mj[:, :], mk[:, :]),
                    r=[mj, mk], w=[masks])
        for b in range(2):
            with ctx.scope():
                junk = ctx.psum("sb_junk", [128, 1024], F32)
                ptb0 = ctx.psum("sb_ptb0", [128, 1024], BF16)
                ln = LNScratch(ctx, "sb")
                for j in range(NC):
                    jg = b * NC + j
                    emit_ln_tile(ctx, c, ln, xb, jg * D, zT, j * 128, S, ptb0, junk)
            for g in range(KV):
                with ctx.scope():
                    qf = ctx.sbuf("sb_qf", [64, 4 * S], BF16)
                    kf = ctx.sbuf("sb_kf", [64, S], BF16)
                    vau = ctx.sbuf("sb_vau", [128, NC * 65], BF16)
                    otb = ctx.sbuf("sb_ot", [128, S], BF16)
                    with ctx.scope():
                        wq_sl = ctx.sbuf("sb_wq", [128, KC * 256], BF16)
                        wkv_sl = ctx.sbuf("sb_wkv", [128, KC * 128], BF16)
                        qn = ctx.sbuf("sb_qn", [128, 256], F32)
                        t1 = ctx.sbuf("sb_t1", [128, 256], F32)
                        t2 = ctx.sbuf("sb_t2", [128, 256], BF16)
                        qro = ctx.sbuf("sb_qro", [128, 256], BF16)
                        nsq = ctx.sbuf("sb_nsq", [128, 12], F32)
                        pq = ctx.psum("sb_pq", [128, 384], F32)
                        ptb = ctx.psum("sb_ptb", [128, 1024], BF16)
                        load_w_kc(ctx, wq_sl, W["L1_attn_b_WqT"], 0, D, g * 256, 256)
                        load_w_kc(ctx, wkv_sl, W["L1_attn_b_WkT"], 0, D, g * 64, 64)
                        load_w_kc(ctx, wkv_sl, W["L1_attn_b_WvT"], 0, D, g * 64, 64,
                                  dst_col0=KC * 64)
                        ctx.op("dve", lambda e: e.memset(vau[:, :], 1.0), w=[vau])
                        for j in range(NC):
                            def pmm(e, j=j):
                                for k in range(KC):
                                    e.matmul(pq[:, 0:256],
                                             zT[:, k * S + j * 128:k * S + (j + 1) * 128],
                                             wq_sl[:, k * 256:(k + 1) * 256],
                                             start=(k == 0), stop=(k == KC - 1))
                                for k in range(KC):
                                    e.matmul(pq[:, 256:320],
                                             zT[:, k * S + j * 128:k * S + (j + 1) * 128],
                                             wkv_sl[:, k * 64:(k + 1) * 64],
                                             start=(k == 0), stop=(k == KC - 1))
                                last = None
                                for k in range(KC):
                                    last = e.matmul(
                                        pq[:, 320:384],
                                        zT[:, k * S + j * 128:k * S + (j + 1) * 128],
                                        wkv_sl[:, KC * 64 + k * 64:KC * 64 + (k + 1) * 64],
                                        start=(k == 0), stop=(k == KC - 1))
                                return last
                            ctx.op("pe", pmm, r=[zT, wq_sl, wkv_sl], w=[pq])
                            ctx.op("act", lambda e, j=j: e.copy(vau[:, j * 65:j * 65 + 64],
                                                                pq[:, 320:384]),
                                   r=[pq], w=[vau])
                            ctx.op("act", lambda e: e.activation(
                                qn[:, 0:256], pq[:, 0:256], AF.Square), r=[pq], w=[qn])
                            ctx.op("act", lambda e: e.activation(
                                t1[:, 0:64], pq[:, 256:320], AF.Square), r=[pq], w=[t1])
                            ctx.op("dve", lambda e: e.tensor_reduce(
                                nsq[:, 0:4], qn[:, 0:256].rearrange("p (h d) -> p h d", d=64),
                                axis=AX.X, op=ALU.add), r=[qn], w=[nsq], late=True)
                            ctx.op("dve", lambda e: e.tensor_reduce(
                                nsq[:, 4:5], t1[:, 0:64].rearrange("p (h d) -> p h d", d=64),
                                axis=AX.X, op=ALU.add), r=[t1], w=[nsq], late=True)
                            ctx.op("act", lambda e: e.activation(nsq[:, 6:11], nsq[:, 0:5],
                                                                 AF.Sqrt), r=[nsq], w=[nsq])
                            ctx.op("dve", lambda e: e.tensor_scalar(
                                nsq[:, 0:5], nsq[:, 6:11], 1e-6, None, op0=ALU.add),
                                r=[nsq], w=[nsq])
                            ctx.op("dve", lambda e: e.reciprocal(nsq[:, 6:11], nsq[:, 0:5]),
                                   r=[nsq], w=[nsq], late=True)
                            ctx.op("dve", lambda e: e.tensor_mul(
                                qn[:, 0:256].rearrange("p (h d) -> p h d", d=64),
                                pq[:, 0:256].rearrange("p (h d) -> p h d", d=64),
                                nsq[:, 6:10].unsqueeze(2).broadcast_to((128, 4, 64))),
                                r=[pq, nsq], w=[qn])
                            ctx.op("dve", lambda e, j=j: e.tensor_mul(
                                t1[:, 0:256].rearrange("p (h d) -> p h d", d=64),
                                qn[:, 0:256].rearrange("p (h d) -> p h d", d=64),
                                ct[:, j * 64:(j + 1) * 64].unsqueeze(1).broadcast_to((128, 4, 64))),
                                r=[qn, ct], w=[t1])
                            ctx.op("dve", lambda e, j=j: e.tensor_mul(
                                t2[:, 0:256].rearrange("p (h c d) -> p h c d", c=2, d=32),
                                qn[:, 0:256].rearrange("p (h c d) -> p h c d", c=2, d=32)
                                [:, :, ::-1, :],
                                st[:, j * 64:(j + 1) * 64].rearrange("p (c d) -> p c d", d=32)
                                .unsqueeze(1).broadcast_to((128, 4, 2, 32))),
                                r=[qn, st], w=[t2])
                            ctx.op("dve", lambda e: e.tensor_add(qro[:, 0:256], t1[:, 0:256],
                                                                 t2[:, 0:256]),
                                   r=[t1, t2], w=[qro])
                            def qtr(e, j=j):
                                last = None
                                for i in range(4):
                                    last = e.transpose(ptb[:, i * 128:(i + 1) * 128][0:64, :],
                                                       qro[:, i * 64:(i + 1) * 64],
                                                       c["ident"][:, :])
                                return last
                            ctx.op("pe", qtr, r=[qro, c["ident"]], w=[ptb])
                            ctx.op("act", lambda e, j=j: e.copy(
                                qf[:, :].rearrange("p (h t) -> p h t", t=S)[:, :, j * 128:(j + 1) * 128],
                                ptb[0:64, 0:512].rearrange("p (h t) -> p h t", t=128)),
                                r=[ptb], w=[qf])
                            ctx.op("dve", lambda e: e.tensor_scalar(
                                qn[:, 0:64], pq[:, 256:320], nsq[:, 10:11], None, op0=ALU.mult),
                                r=[pq, nsq], w=[qn])
                            ctx.op("dve", lambda e, j=j: e.tensor_mul(
                                t1[:, 0:64], qn[:, 0:64], ct[:, j * 64:(j + 1) * 64]),
                                r=[qn, ct], w=[t1])
                            ctx.op("dve", lambda e, j=j: e.tensor_mul(
                                t2[:, 0:64].rearrange("p (c d) -> p c d", d=32),
                                qn[:, 0:64].rearrange("p (c d) -> p c d", d=32)[:, ::-1, :],
                                st[:, j * 64:(j + 1) * 64].rearrange("p (c d) -> p c d", d=32)),
                                r=[qn, st], w=[t2])
                            ctx.op("dve", lambda e: e.tensor_add(qro[:, 0:64], t1[:, 0:64],
                                                                 t2[:, 0:64]),
                                   r=[t1, t2], w=[qro])
                            ctx.op("pe", lambda e, j=j: e.transpose(
                                ptb[:, 512:640][0:64, :], qro[:, 0:64], c["ident"][:, :]),
                                r=[qro, c["ident"]], w=[ptb])
                            ctx.op("act", lambda e, j=j: e.copy(
                                kf[:, j * 128:(j + 1) * 128], ptb[0:64, 512:640]),
                                r=[ptb], w=[kf])
                    with ctx.scope():
                        atl = [ctx.sbuf("sb_at0", [128, 512], BF16),
                               ctx.sbuf("sb_at1", [128, 512], BF16)]
                        exl = [ctx.sbuf("sb_exp0", [128, 512], BF16),
                               ctx.sbuf("sb_exp1", [128, 512], BF16)]
                        rden = ctx.sbuf("sb_rden", [1, 512], F32)
                        rb = ctx.sbuf("sb_rb", [64, 512], BF16)
                        onesf = ctx.sbuf("sb_onesf", [1, 64], F32)
                        wo_sl = ctx.sbuf("sb_wo", [128, 1024], BF16)
                        pst = [ctx.psum(f"sb_pst{i}", [128, 512], F32) for i in range(3)]
                        ppv = [ctx.psum("sb_ppv0", [65, 512], F32),
                               ctx.psum("sb_ppv1", [65, 512], F32)]
                        ppb = ctx.psum("sb_ppb", [64, 512], F32)
                        py = ctx.psum("sb_py", [128, 512], F32)
                        ctx.op("dve", lambda e: e.memset(onesf[:, :], 1.0), w=[onesf])
                        for hi in range(4):
                            lo = (hi % 2) * 64
                            for qt in range(4):
                                ppvb = ppv[qt % 2]
                                kts = list(range(max(0, 4 * qt - 4), 4 * qt + 4))
                                def score(ki):
                                    kt = kts[ki]
                                    delta = qt * 512 - kt * 128
                                    mix = (delta + 384) // 128
                                    pss = pst[ki % 3]
                                    ctx.op("pe", lambda e, kt=kt, hi=hi, qt=qt, pss=pss:
                                           e.matmul(
                                        pss[:, :], kf[:, kt * 128:(kt + 1) * 128],
                                        qf[:, hi * S + qt * 512:hi * S + (qt + 1) * 512],
                                        start=True, stop=True), r=[kf, qf], w=[pss])
                                    ctx.op("act", lambda e, pss=pss, ki=ki: e.activation(
                                        exl[ki % 2][:, :], pss[:, :], AF.Exp, scale=0.125),
                                        r=[pss], w=[exl[ki % 2]])
                                    ctx.op("dve", lambda e, mix=mix, ki=ki: e.tensor_mul(
                                        atl[ki % 2][:, :], exl[ki % 2][:, :],
                                        masks[:, mix * 512:(mix + 1) * 512]),
                                        r=[exl[ki % 2], masks], w=[atl[ki % 2]])
                                def pv(ki):
                                    kt = kts[ki]
                                    ctx.op("pe", lambda e, kt=kt, ki=ki, ppvb=ppvb: e.matmul(
                                        ppvb[:, :], vau[:, kt * 65:(kt + 1) * 65],
                                        atl[ki % 2][:, :],
                                        start=(ki == 0), stop=(ki == len(kts) - 1)),
                                        r=[vau, atl[ki % 2]], w=[ppvb])
                                score(0)
                                for ki in range(len(kts)):
                                    if ki + 1 < len(kts):
                                        score(ki + 1)
                                    pv(ki)
                                ctx.op("dve", lambda e, ppvb=ppvb: e.reciprocal(
                                    rden[0:1, :], ppvb[64:65, :]), r=[ppvb], w=[rden], late=True)
                                ctx.op("pe", lambda e: e.matmul(
                                    ppb[:, :], onesf[0:1, :], rden[0:1, :],
                                    start=True, stop=True), r=[onesf, rden], w=[ppb])
                                ctx.op("act", lambda e: e.copy(rb[:, :], ppb[:, :]),
                                       r=[ppb], w=[rb])
                                ctx.op("dve", lambda e, lo=lo, qt=qt, ppvb=ppvb:
                                       e.tensor_mul(otb[lo:lo + 64, qt * 512:(qt + 1) * 512],
                                                    ppvb[0:64, :], rb[:, :]),
                                       r=[ppvb, rb], w=[otb])
                            if hi % 2 == 1:
                                pglob = g * 2 + hi // 2
                                load_w_kc(ctx, wo_sl, W["L1_attn_b_WprojT"], pglob * 128, 128,
                                          0, 1024)
                                for t in range(NC):
                                    for half in range(2):
                                        ctx.op("pe", lambda e, t=t, half=half: e.matmul(
                                            py[:, :], otb[:, t * 128:(t + 1) * 128],
                                            wo_sl[:, half * 512:(half + 1) * 512],
                                            start=True, stop=True), r=[otb, wo_sl], w=[py])
                                        jg = b * NC + t
                                        ctx.op("dve", lambda e, jg=jg, half=half: e.tensor_add(
                                            xb[:, jg * D + half * 512:jg * D + (half + 1) * 512],
                                            xb[:, jg * D + half * 512:jg * D + (half + 1) * 512],
                                            py[:, :]), r=[py, xb], w=[xb])




def emit_l1_attn_a(ctx, c, W, xb):
    NC = S // 128
    with ctx.scope():
        g1c = ctx.sbuf("ta_g1c", [128, 8], F32)
        b1c = ctx.sbuf("ta_b1c", [128, 8], BF16)
        stats = ctx.sbuf("ta_stats", [128, NT * 4], F32)
        with ctx.scope():
            ptb = ctx.psum("ta_ptb", [128, 1024], BF16)
            pbc = ctx.psum("ta_pbc", [128, 1024], F32)
            grow = ctx.sbuf("ta_grow", [1, 4 * D], BF16)
            for i, nm in enumerate(["g1", "beta1", "g2", "beta2"]):
                ctx.dma("pool", grow[0:1, i * D:(i + 1) * D],
                        W["L1_attn_a_" + nm].unsqueeze(0), w=[grow])
            for cols, off in [(g1c, 0), (b1c, D)]:
                def tr(e, off=off):
                    last = None
                    for cc in range(8):
                        last = e.transpose(ptb[:, 2 * cc:2 * cc + 1],
                                           grow[0:1, off + cc * 128:off + (cc + 1) * 128],
                                           c["ident"][0:1, 0:1])
                    return last
                ctx.op("pe", tr, r=[grow, c["ident"]], w=[ptb])
                ctx.op("act", lambda e, cols=cols: e.copy(
                    cols[:, 0:8], ptb[:, :].rearrange("p (a b2) -> p a b2", b2=2)[:, 0:8, 0]),
                    r=[ptb], w=[cols])
        for b in range(2):
            with ctx.scope():
                zt2 = ctx.sbuf("ta_zt2", [128, NC * D], BF16)
                with ctx.scope():
                    ln = LNScratch(ctx, "ta")
                    t1s = ctx.sbuf("ta_t1s", [128, D], F32)
                    junk = ctx.psum("ta_junk", [128, 1024], F32)
                    for j in range(NC):
                        jg = b * NC + j
                        xa = xb[:, jg * D:(jg + 1) * D]
                        sc = stats[:, jg * 4:jg * 4 + 4]
                        ctx.op("dve", lambda e, xa=xa: e.tensor_reduce(
                            ln.sums[:, :], xa, axis=AX.X, op=ALU.add),
                            r=[xb], w=[ln.sums], late=True)
                        ctx.op("dve", lambda e, sc=sc: e.tensor_scalar(
                            sc[:, 0:1], ln.sums[:, :], 1.0 / D, None, op0=ALU.mult),
                            r=[ln.sums], w=[stats])
                        ctx.op("dve", lambda e, xa=xa, sc=sc: e.tensor_scalar(
                            ln.tctr[:, :], xa, sc[:, 0:1], None, op0=ALU.subtract),
                            r=[xb, stats], w=[ln.tctr])
                        ctx.op("act", lambda e: e.activation(junk[:, 0:D], ln.tctr[:, :],
                                                             AF.Square, accum_out=ln.ssq[:, :]),
                               r=[ln.tctr], w=[junk, ln.ssq], late=True)
                        ctx.op("act", lambda e: e.activation(ln.sd[:, :], ln.ssq[:, :], AF.Sqrt,
                                                             bias=c["epsc"][:, 0:1],
                                                             scale=1.0 / D),
                               r=[ln.ssq, c["epsc"]], w=[ln.sd])
                        ctx.op("dve", lambda e, sc=sc: e.reciprocal(sc[:, 1:2], ln.sd[:, :]),
                               r=[ln.sd], w=[stats], late=True)
                        ctx.op("dve", lambda e, sc=sc: e.tensor_scalar(
                            t1s[:, :], ln.tctr[:, :], sc[:, 1:2], None, op0=ALU.mult),
                            r=[ln.tctr, stats], w=[t1s])
                        ctx.op("dve", lambda e: e.tensor_reduce(
                            ln.sums[:, :], t1s[:, :], axis=AX.X, op=ALU.add),
                            r=[t1s], w=[ln.sums], late=True)
                        ctx.op("dve", lambda e, sc=sc: e.tensor_scalar(
                            sc[:, 2:3], ln.sums[:, :], 1.0 / D, None, op0=ALU.mult),
                            r=[ln.sums], w=[stats])
                        ctx.op("dve", lambda e, sc=sc: e.tensor_scalar(
                            ln.tctr[:, :], t1s[:, :], sc[:, 2:3], None, op0=ALU.subtract),
                            r=[t1s, stats], w=[ln.tctr])
                        ctx.op("act", lambda e: e.activation(junk[:, 0:D], ln.tctr[:, :],
                                                             AF.Square, accum_out=ln.ssq[:, :]),
                               r=[ln.tctr], w=[junk, ln.ssq], late=True)
                        ctx.op("act", lambda e: e.activation(ln.sd[:, :], ln.ssq[:, :], AF.Sqrt,
                                                             bias=c["epsc"][:, 0:1],
                                                             scale=1.0 / D),
                               r=[ln.ssq, c["epsc"]], w=[ln.sd])
                        ctx.op("dve", lambda e, sc=sc: e.reciprocal(sc[:, 3:4], ln.sd[:, :]),
                               r=[ln.sd], w=[stats], late=True)
                with ctx.scope():
                  ct = ctx.sbuf("ta_ct", [128, NC * 64], BF16)
                  st = ctx.sbuf("ta_st", [128, NC * 64], BF16)
                  bor = ctx.sbuf("ta_bor", [1, D], BF16)
                  emit_rope_tables(ctx, c, ct, st, "ilv")
                  ctx.dma("pool", bor[0:1, :], W["L1_attn_a_bo"].unsqueeze(0), w=[bor])
                  for pr in range(8):
                    with ctx.scope():
                        wsl = [ctx.sbuf(f"ta_w{i}", [128, KC * 128], BF16) for i in range(3)]
                        wtmp = ctx.sbuf("ta_wtmp", [128, KC * 128], BF16)
                        befr = ctx.sbuf("ta_bef", [1, 3 * 128], BF16)
                        bqr = ctx.sbuf("ta_bqr", [1, 3 * 128], BF16)
                        for i, nm in enumerate(["bq", "bk", "bv"]):
                            ctx.dma("pool", bqr[0:1, i * 128:(i + 1) * 128],
                                    W["L1_attn_a_" + nm][pr * 128:(pr + 1) * 128].unsqueeze(0),
                                    w=[bqr])
                        wo_sl = ctx.sbuf("ta_wo", [128, 1024], BF16)
                        qf = ctx.sbuf("ta_qf", [128, S], BF16)
                        kf = ctx.sbuf("ta_kf", [128, S], BF16)
                        vau = ctx.sbuf("ta_vau", [128, NC * 130], BF16)
                        otp = ctx.sbuf("ta_otp", [128, S], BF16)
                        ztl = ctx.sbuf("ta_ztl", [128, D], BF16)
                        zsc = ctx.sbuf("ta_zsc", [128, D], F32)
                        tt1 = ctx.sbuf("ta_tt1", [128, 128], BF16)
                        tt2 = ctx.sbuf("ta_tt2", [128, 128], BF16)
                        qro = ctx.sbuf("ta_qro", [128, 128], BF16)
                        nmx = ctx.sbuf("ta_nmx", [128, 4], F32)
                        nmxb = ctx.sbuf("ta_nmxb", [128, 4], BF16)
                        nsq = ctx.sbuf("ta_nsq", [128, 4], F32)
                        mrow = ctx.sbuf("ta_mrow", [4, 128], BF16)
                        mcol = ctx.sbuf("ta_mcol", [4, 1], BF16)
                        crow = ctx.sbuf("ta_crow", [1, 8], F32)
                        negc = ctx.sbuf("ta_negc", [128, 2], F32)
                        exps = ctx.sbuf("ta_exps", [128, 512], BF16)
                        exps2 = ctx.sbuf("ta_exps2", [128, 512], BF16)
                        rden = ctx.sbuf("ta_rden", [1, 512], BF16)
                        rb = ctx.sbuf("ta_rb", [64, 512], BF16)
                        crowb = ctx.sbuf("ta_crowb", [1, 2], BF16)
                        ctx.op("dve", lambda e: e.memset(vau[:, :], 1.0), w=[vau])
                        ctx.op("dve", lambda e: e.memset(nmx[:, :], 0.0), w=[nmx])
                        with ctx.scope():
                            pq = ctx.psum("ta_pq", [128, 384], F32)
                            ptb = ctx.psum("ta_ptb1", [128, 1024], BF16)
                            for wi, nm in enumerate(["Wq", "Wk", "Wv"]):
                                load_w_kc(ctx, wtmp, W["L1_attn_a_" + nm], 0, D, pr * 128, 128)
                                def bmm2(e, wi=wi):
                                    last = None
                                    for k in range(KC):
                                        last = e.matmul(pq[0:1, 0:128],
                                                        b1c[:, k:k + 1],
                                                        wtmp[:, k * 128:(k + 1) * 128],
                                                        start=(k == 0), stop=(k == KC - 1))
                                    return last
                                ctx.op("pe", bmm2, r=[b1c, wtmp], w=[pq])
                                ctx.op("dve", lambda e, wi=wi: e.tensor_add(
                                    befr[0:1, wi * 128:(wi + 1) * 128], pq[0:1, 0:128],
                                    bqr[0:1, wi * 128:(wi + 1) * 128]),
                                    r=[pq, bqr], w=[befr])
                                for k in range(KC):
                                    ctx.op("dve", lambda e, wi=wi, k=k: e.tensor_scalar(
                                        wsl[wi][:, k * 128:(k + 1) * 128],
                                        wtmp[:, k * 128:(k + 1) * 128], g1c[:, k:k + 1],
                                        None, op0=ALU.mult), r=[wtmp, g1c], w=[wsl[wi]])
                            load_w_kc(ctx, wo_sl, W["L1_attn_a_Wo"], pr * 128, 128, 0, 1024)
                            for j in range(NC):
                                jg = b * NC + j
                                sc = stats[:, jg * 4:jg * 4 + 4]
                                ctx.op("dve", lambda e, jg=jg, sc=sc: e.tensor_scalar(
                                    zsc[:, :], xb[:, jg * D:(jg + 1) * D], sc[:, 0:1],
                                    sc[:, 1:2], op0=ALU.subtract, op1=ALU.mult),
                                    r=[xb, stats], w=[zsc])
                                ctx.op("dve", lambda e, sc=sc: e.tensor_scalar(
                                    ztl[:, :], zsc[:, :], sc[:, 2:3], sc[:, 3:4],
                                    op0=ALU.subtract, op1=ALU.mult),
                                    r=[zsc, stats], w=[ztl])
                                def ztr(e):
                                    last = None
                                    for cc in range(KC):
                                        last = e.transpose(ptb[:, cc * 128:(cc + 1) * 128],
                                                           ztl[:, cc * 128:(cc + 1) * 128],
                                                           c["ident"][:, :])
                                    return last
                                ctx.op("pe", ztr, r=[ztl, c["ident"]], w=[ptb])
                                ctx.op("act", lambda e: e.copy(ztl[:, :], ptb[:, 0:1024]),
                                       r=[ptb], w=[ztl])
                                def pmm(e, j=j):
                                    last = None
                                    for wi in range(3):
                                        for k in range(KC):
                                            e.matmul(pq[:, wi * 128:(wi + 1) * 128],
                                                     ztl[:, k * 128:(k + 1) * 128],
                                                     wsl[wi][:, k * 128:(k + 1) * 128],
                                                     start=(k == 0), stop=False)
                                        last = e.matmul(pq[:, wi * 128:(wi + 1) * 128],
                                                        c["ones"][0:1, :],
                                                        befr[0:1, wi * 128:(wi + 1) * 128],
                                                        start=False, stop=True)
                                    return last
                                ctx.op("pe", pmm, r=[ztl, wsl[0], wsl[1], wsl[2], c["ones"],
                                                     befr], w=[pq])
                                ctx.op("act", lambda e, j=j: e.copy(
                                    vau[:, :].rearrange("p (t h dd) -> p t h dd", h=2, dd=65)
                                    [:, j, :, 0:64],
                                    pq[:, 256:384].rearrange("p (h dd) -> p h dd", dd=64)),
                                    r=[pq], w=[vau])
                                ctx.op("act", lambda e: e.activation(zsc[:, 0:256],
                                                                     pq[:, 0:256], AF.Square),
                                       r=[pq], w=[zsc])
                                ctx.op("dve", lambda e: e.tensor_reduce(
                                    nsq[:, 0:4],
                                    zsc[:, 0:256].rearrange("p (h dd) -> p h dd", dd=64),
                                    axis=AX.X, op=ALU.add), r=[zsc], w=[nsq], late=True)
                                ctx.op("dve", lambda e: e.tensor_max(nmx[:, :], nmx[:, :],
                                                                     nsq[:, :]),
                                       r=[nsq, nmx], w=[nmx])
                                for src0, dstf in [(0, qf), (128, kf)]:
                                    ctx.op("dve", lambda e, src0=src0, j=j: e.tensor_mul(
                                        tt1[:, :].rearrange("p (h dd) -> p h dd", dd=64),
                                        pq[:, src0:src0 + 128].rearrange(
                                            "p (h dd) -> p h dd", dd=64),
                                        ct[:, j * 64:(j + 1) * 64].unsqueeze(1)
                                        .broadcast_to((128, 2, 64))), r=[pq, ct], w=[tt1])
                                    ctx.op("dve", lambda e, src0=src0, j=j: e.tensor_mul(
                                        tt2[:, :].rearrange("p (h dd c2) -> p h dd c2",
                                                            dd=32, c2=2),
                                        pq[:, src0:src0 + 128].rearrange(
                                            "p (h dd c2) -> p h dd c2", dd=32, c2=2)
                                        [:, :, :, ::-1],
                                        st[:, j * 64:(j + 1) * 64]
                                        .rearrange("p (dd c2) -> p dd c2", c2=2)
                                        .unsqueeze(1).broadcast_to((128, 2, 32, 2))),
                                        r=[pq, st], w=[tt2])
                                    ctx.op("dve", lambda e: e.tensor_add(qro[:, :], tt1[:, :],
                                                                         tt2[:, :]),
                                           r=[tt1, tt2], w=[qro])
                                    ctx.op("pe", lambda e: e.transpose(
                                        ptb[:, 0:128], qro[:, :], c["ident"][:, :]),
                                        r=[qro, c["ident"]], w=[ptb])
                                    ctx.op("act", lambda e, j=j, dstf=dstf: e.copy(
                                        dstf[:, j * 128:(j + 1) * 128], ptb[:, 0:128]),
                                        r=[ptb], w=[dstf])
                            ctx.op("dve", lambda e: e.tensor_copy(nmxb[:, :], nmx[:, :]),
                                   r=[nmx], w=[nmxb])
                            ctx.op("pe", lambda e: e.transpose(ptb[0:4, 0:128], nmxb[:, :],
                                                               c["ident"][:, :]),
                                   r=[nmxb, c["ident"]], w=[ptb])
                            ctx.op("act", lambda e: e.copy(mrow[0:4, 0:128], ptb[0:4, 0:128]),
                                   r=[ptb], w=[mrow])
                            ctx.op("dve", lambda e: e.tensor_reduce(
                                nsq[0:4, 0:1], mrow[0:4, 0:128], axis=AX.X, op=ALU.max),
                                r=[mrow], w=[nsq], late=True)
                            ctx.op("dve", lambda e: e.tensor_copy(mcol[0:4, 0:1],
                                                                  nsq[0:4, 0:1]),
                                   r=[nsq], w=[mcol])
                            ctx.op("pe", lambda e: e.transpose(ptb[0:1, 0:4], mcol[0:4, 0:1],
                                                               c["ident"][0:4, 0:4]),
                                   r=[mcol, c["ident"]], w=[ptb])
                            ctx.op("act", lambda e: e.copy(crow[0:1, 0:4], ptb[0:1, 0:4]),
                                   r=[ptb], w=[crow])
                            ctx.op("dve", lambda e: e.tensor_mul(crow[0:1, 4:6],
                                                                 crow[0:1, 0:2],
                                                                 crow[0:1, 2:4]),
                                   r=[crow], w=[crow])
                            ctx.op("act", lambda e: e.activation(crow[0:1, 6:8],
                                                                 crow[0:1, 4:6], AF.Sqrt),
                                   r=[crow], w=[crow], late=True)
                            ctx.op("dve", lambda e: e.tensor_scalar(
                                crow[0:1, 0:2], crow[0:1, 6:8], -0.125, None, op0=ALU.mult),
                                r=[crow], w=[crow])
                            ctx.op("dve", lambda e: e.tensor_copy(crowb[0:1, :],
                                                                  crow[0:1, 0:2]),
                                   r=[crow], w=[crowb])
                            def bcc(e):
                                e.matmul(pq[:, 0:1], c["ones"][0:1, :], crowb[0:1, 0:1],
                                         start=True, stop=True)
                                return e.matmul(pq[:, 1:2], c["ones"][0:1, :],
                                                crowb[0:1, 1:2], start=True, stop=True)
                            ctx.op("pe", bcc, r=[c["ones"], crowb], w=[pq])
                            ctx.op("act", lambda e: e.copy(negc[:, 0:2], pq[:, 0:2]),
                                   r=[pq], w=[negc])
                        with ctx.scope():
                            pst = [ctx.psum(f"ta_pst{i}", [128, 512], F32) for i in range(4)]
                            ppv = [ctx.psum("ta_ppv0", [65, 512], F32),
                                   ctx.psum("ta_ppv1", [65, 512], F32)]
                            ppb = ctx.psum("ta_ppb", [64, 512], F32)
                            exl = [exps, exps2]
                            for hi in range(2):
                                lo = hi * 64
                                for qt in range(4):
                                    ppvb = ppv[qt % 2]
                                    def score(kt):
                                        pss = pst[kt % 4]
                                        ctx.op("pe", lambda e, kt=kt, qt=qt, pss=pss,
                                               lo=lo: e.matmul(
                                            pss[:, :], kf[lo:lo + 64, kt * 128:(kt + 1) * 128],
                                            qf[lo:lo + 64, qt * 512:(qt + 1) * 512],
                                            start=True, stop=True, tile_position=(lo, 0)),
                                            r=[kf, qf], w=[pss])
                                        ctx.op("act", lambda e, pss=pss, hi=hi, kt=kt:
                                               e.activation(
                                            exl[kt % 2][:, :], pss[:, :], AF.Exp, scale=0.125,
                                            bias=negc[:, hi:hi + 1]),
                                            r=[pss, negc], w=[exl[kt % 2]])
                                    def pv(kt):
                                        ctx.op("pe", lambda e, kt=kt, hi=hi, ppvb=ppvb:
                                               e.matmul(
                                            ppvb[:, :],
                                            vau[:, kt * 130 + hi * 65:kt * 130 + hi * 65 + 65],
                                            exl[kt % 2][:, :],
                                            start=(kt == 0), stop=(kt == NC - 1)),
                                            r=[vau, exl[kt % 2]], w=[ppvb])
                                    score(0)
                                    for kt in range(NC):
                                        if kt + 1 < NC:
                                            score(kt + 1)
                                        pv(kt)
                                    ctx.op("dve", lambda e, ppvb=ppvb: e.reciprocal(
                                        zsc[0:1, 0:512], ppvb[64:65, :]), r=[ppvb], w=[zsc],
                                        late=True)
                                    ctx.op("dve", lambda e: e.tensor_copy(
                                        rden[0:1, :], zsc[0:1, 0:512]), r=[zsc], w=[rden])
                                    ctx.op("pe", lambda e: e.matmul(
                                        ppb[:, :], c["ones"][0:1, 0:64], rden[0:1, :],
                                        start=True, stop=True), r=[c["ones"], rden], w=[ppb])
                                    ctx.op("act", lambda e: e.copy(rb[:, :], ppb[:, :]),
                                           r=[ppb], w=[rb])
                                    ctx.op("dve", lambda e, lo=lo, qt=qt, ppvb=ppvb:
                                           e.tensor_mul(otp[lo:lo + 64,
                                                            qt * 512:(qt + 1) * 512],
                                                        ppvb[0:64, :], rb[:, :]),
                                           r=[ppvb, rb], w=[otp])
                        with ctx.scope():
                            py = [ctx.psum("ta_py0", [128, 512], F32),
                                  ctx.psum("ta_py1", [128, 512], F32)]
                            for t in range(NC):
                                jg = b * NC + t
                                sc = stats[:, jg * 4:jg * 4 + 4]
                                for half in range(2):
                                    pp = py[half]
                                    def ymm(e, t=t, half=half, pp=pp, pr=pr):
                                        last = e.matmul(pp[:, :], otp[:, t * 128:(t + 1) * 128],
                                                        wo_sl[:, half * 512:(half + 1) * 512],
                                                        start=True, stop=(pr != 0))
                                        if pr == 0:
                                            last = e.matmul(pp[:, :], c["ones"][0:1, :],
                                                            bor[0:1, half * 512:
                                                                (half + 1) * 512],
                                                            start=False, stop=True)
                                        return last
                                    ctx.op("pe", ymm, r=[otp, wo_sl, c["ones"], bor], w=[pp])
                                    ca = zt2[:, t * D + half * 512:t * D + (half + 1) * 512]
                                    if pr == 0:
                                        ctx.op("dve", lambda e, jg=jg, half=half, sc=sc:
                                               e.tensor_scalar(
                                            zsc[:, half * 512:(half + 1) * 512],
                                            xb[:, jg * D + half * 512:
                                               jg * D + (half + 1) * 512],
                                            sc[:, 0:1], None, op0=ALU.subtract),
                                            r=[xb, stats], w=[zsc])
                                        ctx.op("dve", lambda e, ca=ca, pp=pp, sc=sc, half=half:
                                               e.scalar_tensor_tensor(
                                            ca, zsc[:, half * 512:(half + 1) * 512],
                                            sc[:, 1:2], pp[:, :],
                                            op0=ALU.mult, op1=ALU.add),
                                            r=[zsc, stats, pp], w=[zt2])
                                    else:
                                        ctx.op("dve", lambda e, ca=ca, pp=pp: e.tensor_add(
                                            ca, ca, pp[:, :]), r=[pp, zt2], w=[zt2])
                with ctx.scope():
                    ln = LNScratch(ctx, "ta2")
                    g2b = ctx.sbuf("ta_g2b", [128, D], BF16)
                    b2b = ctx.sbuf("ta_b2b", [128, D], BF16)
                    grow2 = ctx.sbuf("ta_grow2", [1, 2 * D], BF16)
                    junk = ctx.psum("ta_junk2", [128, 1024], F32)
                    ptb = ctx.psum("ta_ptb2", [128, 1024], BF16)
                    for i, nm in enumerate(["g2", "beta2"]):
                        ctx.dma("pool", grow2[0:1, i * D:(i + 1) * D],
                                W["L1_attn_a_" + nm].unsqueeze(0), w=[grow2])
                    for dst, off in [(g2b, 0), (b2b, D)]:
                        def bc(e, off=off, dst=dst):
                            last = None
                            for hh in range(2):
                                last = e.matmul(junk[:, hh * 512:(hh + 1) * 512],
                                                c["ones"][0:1, :],
                                                grow2[0:1, off + hh * 512:off + (hh + 1) * 512],
                                                start=True, stop=True)
                            return last
                        ctx.op("pe", bc, r=[grow2, c["ones"]], w=[junk])
                        ctx.op("act", lambda e, dst=dst: e.copy(dst[:, :], junk[:, 0:1024]),
                               r=[junk], w=[dst])
                    for j in range(NC):
                        jg = b * NC + j
                        ctx.op("dve", lambda e, j=j, jg=jg: e.tensor_add(
                            xb[:, jg * D:(jg + 1) * D], xb[:, jg * D:(jg + 1) * D],
                            zt2[:, j * D:(j + 1) * D]), r=[zt2, xb], w=[xb])
                        emit_ln_tile(ctx, c, ln, zt2, j * D, None, None, None, ptb, junk,
                                     gamma_b=g2b, beta_b=b2b, zt_inplace=(zt2, j * D))
                with ctx.scope():
                    w1b = ctx.sbuf("ta_fw1", [128, KC * 1024], BF16)
                    w2b = ctx.sbuf("ta_fw2", [128, KC * 1024], BF16)
                    hT = ctx.sbuf("ta_fh", [128, KC * 512], BF16)
                    b1r = ctx.sbuf("ta_fb1r", [1, 1024], BF16)
                    b1cq = ctx.sbuf("ta_fb1c", [128, 8], F32)
                    b2r = ctx.sbuf("ta_fb2r", [1, 1024], BF16)
                    p_h = [ctx.psum(f"ta_fph{i}", [128, 512], F32) for i in range(3)]
                    p_y = [ctx.psum(f"ta_fpy{i}", [128, 512], F32) for i in range(4)]
                    ptb = ctx.psum("ta_ptb3", [128, 128], BF16)
                    ctx.dma("pool", b2r[0:1, :], W["L1_attn_a_b2"].unsqueeze(0), w=[b2r])
                    for q in range(4):
                        load_w_kc(ctx, w1b, W["L1_attn_a_W1"], 0, D, q * 1024, 1024)
                        load_w_kc(ctx, w2b, W["L1_attn_a_W2"], q * 1024, 1024, 0, 1024)
                        ctx.dma("pool", b1r[0:1, :],
                                W["L1_attn_a_b1"][q * 1024:(q + 1) * 1024].unsqueeze(0),
                                w=[b1r])
                        def trb(e):
                            last = None
                            for cc in range(8):
                                last = e.transpose(ptb[:, 2 * cc:2 * cc + 1],
                                                   b1r[0:1, cc * 128:(cc + 1) * 128],
                                                   c["ident"][0:1, 0:1])
                            return last
                        ctx.op("pe", trb, r=[b1r, c["ident"]], w=[ptb])
                        ctx.op("act", lambda e: e.copy(
                            b1cq[:, 0:8],
                            ptb[:, 0:16].rearrange("p (a c2) -> p a c2", c2=2)[:, 0:8, 0]),
                            r=[ptb], w=[b1cq])
                        for t in range(4):
                            for m in range(KC):
                                ph = p_h[m % 3]
                                def hmm(e, m=m, t=t, ph=ph):
                                    last = None
                                    for k in range(KC):
                                        last = e.matmul(
                                            ph[:, :],
                                            w1b[:, k * 1024 + m * 128:k * 1024 + (m + 1) * 128],
                                            zt2[:, :].rearrange(
                                                "p (tt kc w) -> p tt kc w", kc=KC, w=128)
                                            [:, 4 * t:4 * t + 4, k, :],
                                            start=(k == 0), stop=(k == KC - 1))
                                    return last
                                ctx.op("pe", hmm, r=[w1b, zt2], w=[ph])
                                ctx.op("act", lambda e, m=m, ph=ph: e.activation(
                                    hT[:, m * 512:(m + 1) * 512], ph[:, :], AF.Relu,
                                    bias=b1cq[:, m:m + 1]), r=[ph, b1cq], w=[hT])
                            for sub in range(4):
                                jg = b * NC + t * 4 + sub
                                sc2 = sub * 128
                                for half in range(2):
                                    pyy = p_y[(sub * 2 + half) % 4]
                                    def ymm2(e, sc2=sc2, pyy=pyy, q=q, half=half):
                                        last = None
                                        for k in range(KC):
                                            last = e.matmul(
                                                pyy[:, :],
                                                hT[:, k * 512 + sc2:k * 512 + sc2 + 128],
                                                w2b[:, k * 1024 + half * 512:
                                                    k * 1024 + (half + 1) * 512],
                                                start=(k == 0),
                                                stop=(k == KC - 1 and q != 0))
                                        if q == 0:
                                            last = e.matmul(
                                                pyy[:, :], c["ones"][0:1, :],
                                                b2r[0:1, half * 512:(half + 1) * 512],
                                                start=False, stop=True)
                                        return last
                                    ctx.op("pe", ymm2, r=[hT, w2b, c["ones"], b2r], w=[pyy])
                                    ctx.op("dve", lambda e, pyy=pyy, jg=jg, half=half:
                                           e.tensor_add(
                                        xb[:, jg * D + half * 512:jg * D + (half + 1) * 512],
                                        xb[:, jg * D + half * 512:jg * D + (half + 1) * 512],
                                        pyy[:, :]), r=[pyy, xb], w=[xb])



WSPECS = {
    "l0ffn": ["L0_ffn_rwsg", "L0_ffn_w1", "L0_ffn_w2", "L0_ffn_shared_gate_up",
              "L0_ffn_shared_down"],
    "l1ffn": ["L1_ffn_Wr", "L1_ffn_W1", "L1_ffn_W2"],
    "l0ab": ["L0_attn_b_WqT", "L0_attn_b_WkT", "L0_attn_b_WvT", "L0_attn_b_WoT",
             "L0_attn_b_bq", "L0_attn_b_bk", "L0_attn_b_bv", "L0_attn_b_bo"],
    "l0aa": ["L0_attn_a_w_qkv", "L0_attn_a_proj", "L0_attn_a_w_out", "L0_attn_a_b_out"],
    "l1ab": ["L1_attn_b_WqT", "L1_attn_b_WkT", "L1_attn_b_WvT", "L1_attn_b_WprojT"],
    "l1aa": ["L1_attn_a_Wq", "L1_attn_a_Wk", "L1_attn_a_Wv", "L1_attn_a_Wo",
             "L1_attn_a_W1", "L1_attn_a_W2", "L1_attn_a_bq", "L1_attn_a_bk",
             "L1_attn_a_bv", "L1_attn_a_bo", "L1_attn_a_b1", "L1_attn_a_b2",
             "L1_attn_a_g1", "L1_attn_a_beta1", "L1_attn_a_g2", "L1_attn_a_beta2"],
}

EMITTERS = {
    "l0ffn": emit_l0_ffn,
    "l1ffn": emit_l1_ffn,
    "l0ab": emit_l0_attn_b,
    "l0aa": emit_l0_attn_a,
    "l1ab": emit_l1_attn_b,
    "l1aa": emit_l1_attn_a,
}

WSHAPES = {
    "L0_attn_a_w_qkv": (D, 3072), "L0_attn_a_proj": (128, 64),
    "L0_attn_a_w_out": (D, D), "L0_attn_a_b_out": (D,),
    "L0_ffn_rwsg": (D, 9), "L0_ffn_w1": (8, D, D), "L0_ffn_w2": (8, D, D),
    "L0_ffn_shared_gate_up": (D, 2048), "L0_ffn_shared_down": (D, D),
    "L0_ffn_shared_gate_w": (D, 1),
    "L0_attn_b_WqT": (D, D), "L0_attn_b_WkT": (D, D), "L0_attn_b_WvT": (D, D),
    "L0_attn_b_WoT": (D, D), "L0_attn_b_bq": (D,), "L0_attn_b_bk": (D,),
    "L0_attn_b_bv": (D,), "L0_attn_b_bo": (D,),
    "L1_attn_a_Wq": (D, D), "L1_attn_a_Wk": (D, D), "L1_attn_a_Wv": (D, D),
    "L1_attn_a_Wo": (D, D), "L1_attn_a_W1": (D, 4096), "L1_attn_a_W2": (4096, D),
    "L1_attn_a_bq": (D,), "L1_attn_a_bk": (D,), "L1_attn_a_bv": (D,),
    "L1_attn_a_bo": (D,), "L1_attn_a_b1": (4096,), "L1_attn_a_b2": (D,),
    "L1_attn_a_g1": (D,), "L1_attn_a_beta1": (D,), "L1_attn_a_g2": (D,),
    "L1_attn_a_beta2": (D,),
    "L1_ffn_Wr": (D, 8), "L1_ffn_W1": (8, D, 4096), "L1_ffn_W2": (8, 4096, D),
    "L1_attn_b_WqT": (D, D), "L1_attn_b_WkT": (D, 256), "L1_attn_b_WvT": (D, 256),
    "L1_attn_b_WprojT": (D, D),
}


def host_weights(inputs):
    w = {}
    for k, v in inputs.items():
        if k == "L0_attn_a_x":
            continue
        w[k] = np.ascontiguousarray(np.asarray(v, dtype=np.float32))
    w["L0_ffn_rwsg"] = np.ascontiguousarray(
        np.concatenate([w["L0_ffn_router_w"], w["L0_ffn_shared_gate_w"]], axis=1))
    w["L0_attn_b_WqT"] = np.ascontiguousarray(w.pop("L0_attn_b_Wq").T)
    w["L0_attn_b_WkT"] = np.ascontiguousarray(w.pop("L0_attn_b_Wk").T)
    w["L0_attn_b_WvT"] = np.ascontiguousarray(w.pop("L0_attn_b_Wv").T)
    w["L0_attn_b_WoT"] = np.ascontiguousarray(w.pop("L0_attn_b_Wo").T)
    w["L1_attn_b_WqT"] = np.ascontiguousarray(w.pop("L1_attn_b_Wq").T)
    kvT = np.ascontiguousarray(w.pop("L1_attn_b_Wkv").T)
    w["L1_attn_b_WkT"] = np.ascontiguousarray(kvT[:, :256])
    w["L1_attn_b_WvT"] = np.ascontiguousarray(kvT[:, 256:])
    w["L1_attn_b_WprojT"] = np.ascontiguousarray(w.pop("L1_attn_b_Wproj").T)
    return w


def build_program(sublayers):
    nc = bass.Bass("TRN2", target_bir_lowering=False)
    x_in = nc.dram_tensor("x", [TTOT, D], F32, kind="ExternalInput").ap()
    x_out = nc.dram_tensor("xo", [TTOT, D], F32, kind="ExternalOutput").ap()
    wnames = sorted({n for s_ in sublayers for n in WSPECS[s_]})
    W = {n: nc.dram_tensor(n, list(WSHAPES[n]), F32, kind="ExternalInput").ap() for n in wnames}
    with ExitStack() as stack:
        ctx = Ctx(nc, stack)
        c = {}
        xb = ctx.sbuf("x_res", [128, NT * D], F32)
        emit_constants(ctx, c)
        emit_load_x(ctx, x_in, xb)
        for s_ in sublayers:
            EMITTERS[s_](ctx, c, W, xb)
        emit_store_x(ctx, xb, x_out)
        ctx.finish()
    return nc, wnames


ALL_SUBLAYERS = ["l0aa", "l0ffn", "l0ab", "l0ffn", "l1aa", "l1ffn", "l1ab", "l1ffn"]


def run_chain(inputs, chains):
    w = host_weights(inputs)
    x = np.ascontiguousarray(np.asarray(inputs["L0_attn_a_x"], np.float32).reshape(16 * S, D))
    shards = [np.ascontiguousarray(x[i * TTOT:(i + 1) * TTOT]) for i in range(NCORES)]
    progs = {}
    for ch in chains:
        key = tuple(ch)
        if key not in progs:
            progs[key] = build_program(ch)
    for ch in chains:
        nc, wnames = progs[tuple(ch)]
        in_maps = [dict({"x": shards[i]}, **{n: w[n] for n in wnames}) for i in range(NCORES)]
        res = run_bass_kernel_spmd(nc, in_maps, core_ids=list(range(NCORES)))
        shards = [np.asarray(res.results[i]["xo"]) for i in range(NCORES)]
    return np.concatenate(shards, axis=0).reshape(16, S, D)


def kernel(**inputs):
    return run_chain(inputs, [ALL_SUBLAYERS])
